# version 02
# baseline (speedup 1.0000x reference)
from contextlib import ExitStack

import dataclasses

import numpy as np
import ml_dtypes
import concourse.bass as bass
import concourse.mybir as mybir
from concourse.bass_utils import run_bass_kernel_spmd

F32 = mybir.dt.float32
BF16 = mybir.dt.bfloat16
AF = mybir.ActivationFunctionType
ALU = mybir.AluOpType
AX = mybir.AxisListType
BF = ml_dtypes.bfloat16

NCORES = 8
B, T, D = 2, 8192, 1024
TC = T // 4
MAGIC = 12582912.0

ENGS = ("sync", "gpsimd", "vector", "scalar", "tensor")


class Res:
    __slots__ = ("name", "writer", "readers")

    def __init__(self, name=""):
        self.name = name
        self.writer = None
        self.readers = []


class Prog:

    def __init__(self, nc, blk):
        self.nc, self.blk = nc, blk
        self.sem = {}
        self.cnt = {e: 0 for e in ENGS}
        self.clock = {e: {f: 0 for f in ENGS} for e in ENGS}
        self._cms = []
        for e in ENGS:
            cm = nc.semaphore("sem_" + e)
            self.sem[e] = cm.__enter__()
            self._cms.append(cm)
        cm = nc.semaphore("sem_aux")
        self.aux = cm.__enter__()
        self._cms.append(cm)
        self.auxc = [0]

    def chain(self, e):
        aux, auxc = self.aux, self.auxc
        def ch(instr):
            auxc[0] += 1
            instr.then_inc(aux, 1)
            e.wait_ge(aux, auxc[0])
            return instr
        return ch

    def step(self, eng, reads=(), writes=(), dma=False, fixed_inc=None):
        def deco(fn):
            needs = {}

            def need(tok, raw=False):
                if tok is None:
                    return
                e2, c2, _clk = tok
                if e2 == eng and not (raw or eng in ("sync", "gpsimd")):
                    return
                if self.clock[eng][e2] >= c2:
                    return
                needs[e2] = max(needs.get(e2, 0), c2)

            for r in reads:
                need(r.writer, raw=True)
            for w in writes:
                need(w.writer)
                for t in w.readers:
                    need(t)

            emitted = {"n": 0}
            prog = self

            section = getattr(self.blk, eng)

            def body(e):
                for e2, c2 in sorted(needs.items()):
                    e.wait_ge(prog.sem[e2], c2)
                if dma:
                    def dd(out, in_, **kw):
                        ins = e.dma_start(out=out, in_=in_, **kw)
                        ins.then_inc(prog.sem[eng], 16)
                        if fixed_inc is None:
                            emitted["n"] += 16
                        return ins
                    fn(e, dd)
                    if fixed_inc is not None:
                        emitted["n"] += fixed_inc
                else:
                    last = fn(e)
                    assert last is not None, "compute step must return last instr"
                    last.then_inc(prog.sem[eng], 1)
                    emitted["n"] += 1
            section(body)

            for e2, c2 in needs.items():
                if c2 > self.clock[eng][e2]:
                    self.clock[eng][e2] = c2
            self.cnt[eng] += emitted["n"]
            tok = (eng, self.cnt[eng], dict(self.clock[eng]))
            for w in writes:
                w.writer = tok
                w.readers = []
            for r in reads:
                r.readers.append(tok)
            return tok
        return deco

    def fence(self):
        snap = dict(self.cnt)
        for eng in ENGS:
            section = getattr(self.blk, eng)
            def body(e, eng=eng):
                for e2 in ENGS:
                    if e2 == eng and eng not in ("sync", "gpsimd"):
                        continue
                    if snap[e2] > self.clock[eng].get(e2, 0) or e2 == eng:
                        e.wait_ge(self.sem[e2], snap[e2])
            section(body)
            for e2 in ENGS:
                if e2 != eng:
                    self.clock[eng][e2] = max(self.clock[eng][e2], snap[e2])
        return snap

    def drain(self):
        cnts = dict(self.cnt)
        @self.blk.gpsimd
        def _(e):
            for e2 in ENGS:
                if cnts[e2] > 0:
                    e.wait_ge(self.sem[e2], cnts[e2])



def emit_quant(v, x_ap, out_ap, scr, nb, ch):
    P = 128
    F = nb * 16
    x3 = x_ap.rearrange("p (n b) -> p n b", b=16)
    z3 = scr['z'][:, :F].rearrange("p (n b) -> p n b", b=16)
    v.tensor_reduce(scr['amax'][:, :nb], x3, AX.X, ALU.max, apply_absolute_value=True)
    ch(v.tensor_scalar(scr['s6'][:, :nb], scr['amax'][:, :nb], 1.0 / 6.0, 1e-12, ALU.mult, ALU.max))
    ch(v.reciprocal(scr['r2'][:, :nb], scr['s6'][:, :nb]))
    ch(v.tensor_scalar_mul(scr['r2'][:, :nb], scr['r2'][:, :nb], 2.0))
    r2b = scr['r2'][:, :nb].unsqueeze(2).broadcast_to([P, nb, 16])
    s6b = scr['s6'][:, :nb].unsqueeze(2).broadcast_to([P, nb, 16])
    z = scr['z'][:, :F]
    a = scr['a'][:, :F]
    b = scr['b'][:, :F]
    c = scr['c'][:, :F]
    a3v = a.rearrange("p (n b) -> p n b", b=16)
    v.tensor_tensor(z3, a3v, r2b, ALU.mult)
    v.tensor_scalar(a, z, MAGIC, -MAGIC, ALU.add, ALU.add)
    v.tensor_scalar(b, z, 0.5, MAGIC, ALU.mult, ALU.add)
    v.tensor_scalar(c, b, -(MAGIC + 2.0), 0.0, ALU.add, ALU.max)
    v.tensor_scalar(b, a, 4.0, 0.5, ALU.min, ALU.mult)
    v.scalar_tensor_tensor(a, c, 2.0, b, ALU.min, ALU.add)
    v.tensor_single_scalar(c, z, 10.0, ALU.is_gt)
    v.scalar_tensor_tensor(b, c, 2.0, a, ALU.mult, ALU.add)
    v.tensor_tensor(a, b, scr['sgn'][:, :F], ALU.mult)
    a3 = a.rearrange("p (n b) -> p n b", b=16)
    o3 = out_ap.rearrange("p (n b) -> p n b", b=16)
    return v.scalar_tensor_tensor(o3, a3, 0.0, s6b, ALU.add, ALU.mult)


def emit_ln_stats(v, x_ap, st, F, ch):
    v.reduce_sum(st['s1'], x_ap, AX.X)
    v.tensor_tensor(st['sq'][:, :F], x_ap, x_ap, ALU.mult)
    ch(v.reduce_sum(st['s2'], st['sq'][:, :F], AX.X))
    v.tensor_scalar_mul(st['mu'], st['s1'], 1.0 / F)
    ch(v.tensor_scalar_mul(st['m2'], st['s2'], 1.0 / F))
    ch(v.tensor_tensor(st['t'], st['mu'], st['mu'], ALU.mult))
    ch(v.tensor_tensor(st['var'], st['m2'], st['t'], ALU.subtract))
    return v.tensor_scalar_add(st['vpe'], st['var'], 1e-5)



def build_l0ffn(debug=False):
    nc = bass.Bass("TRN2", target_bir_lowering=False)
    xin = nc.dram_tensor("xin", [TC, D], F32, kind="ExternalInput").ap()
    wg_d = nc.dram_tensor("wg", [2816, D], F32, kind="ExternalInput").ap()
    wu_d = nc.dram_tensor("wu", [2816, D], F32, kind="ExternalInput").ap()
    wd_d = nc.dram_tensor("wd", [D, 2816], F32, kind="ExternalInput").ap()
    idb_d = nc.dram_tensor("idb", [128, 128], BF16, kind="ExternalInput").ap()
    xout = nc.dram_tensor("xout", [TC, D], F32, kind="ExternalOutput").ap()
    hq_hbm = nc.dram_tensor("hqscratch", [TC // 128, 128, 2816], BF16, kind="Internal").ap()
    if debug:
        dbg_n = nc.dram_tensor("dbg_n", [128, 1024], F32, kind="ExternalOutput").ap()
        dbg_xq = nc.dram_tensor("dbg_xq", [128, 1024], F32, kind="ExternalOutput").ap()
        dbg_wgq = nc.dram_tensor("dbg_wgq", [128, 2816], F32, kind="ExternalOutput").ap()
        dbg_h = nc.dram_tensor("dbg_h", [128, 2816], F32, kind="ExternalOutput").ap()
        dbg_hq = nc.dram_tensor("dbg_hq", [128, 2816], F32, kind="ExternalOutput").ap()

    NT = TC // 128
    NI = 2816 // 128

    es = ExitStack()
    sb = lambda *a: es.enter_context(nc.sbuf_tensor(*a))
    ps = lambda *a: es.enter_context(nc.psum_tensor(*a))
    idb = sb("idb_sb", [128, 128], BF16)
    xt = sb("xt", [128, 2, 1408], F32)
    z_ = sb("z", [128, 1408], F32)
    a_ = sb("a", [128, 1408], F32)
    b_ = sb("bq", [128, 1408], F32)
    c_ = sb("cq", [128, 1408], F32)
    sgn_ = sb("sgn", [128, 1408], BF16)
    qout = sb("qout", [128, 2, 1408], BF16)
    nsm = sb("nsm", [128, 16], F32)
    amax_ = sb("amax", [128, 88], F32)
    s6_ = sb("s6", [128, 88], F32)
    r2_ = sb("r2", [128, 88], F32)
    nbuf = sb("nbuf", [128, 2, 1024], F32)
    xo = sb("xo", [128, 2, 1024], F32)
    tp0 = ps("tp0", [128, 128], BF16)
    tp1 = ps("tp1", [128, 128], BF16)
    gps = ps("gps", [128, 512], F32)
    ups = ps("ups", [128, 512], F32)
    dps = ps("dps", [128, 2, 512], F32)
    with nc.Block() as blk:
        p = Prog(nc, blk)
        scr = dict(z=z_, a=a_, b=b_, c=c_, sgn=sgn_, amax=amax_, s6=s6_, r2=r2_)
        st = dict(s1=nsm[:, 0:1], s2=nsm[:, 1:2], mu=nsm[:, 2:3], m2=nsm[:, 3:4],
                  t=nsm[:, 4:5], var=nsm[:, 5:6], vpe=nsm[:, 6:7], std=nsm[:, 7:8],
                  rstd=nsm[:, 8:9], sq=z_)
        r_idb = Res("idb")
        r_xt = [Res("xt0"), Res("xt1")]
        r_scr = Res("scr")
        r_sgn = Res("sgn")
        r_qout = [Res("q0"), Res("q1")]
        r_nbuf = [Res("n0"), Res("n1")]
        r_tp = [Res("tp0"), Res("tp1")]
        r_gps, r_ups = Res("gps"), Res("ups")
        r_dps = [Res("dps0"), Res("dps1")]
        r_xo = [Res("xo0"), Res("xo1")]
        r_hqt = Res("hqTt")
        r_hqhbm = [Res() for _ in range(16)]
        r_hbuf = Res("hbuf")
        r_gs = Res("gs")

        @p.step("sync", writes=[r_idb], dma=True)
        def _(e, dd):
            dd(out=idb[:], in_=idb_d[:, :])

        tp = [tp0, tp1]
        tpi = [0]

        def quant_chain(x_res, x_ap, q_res, q_ap, nb):
            F = nb * 16
            @p.step("scalar", reads=[x_res], writes=[r_sgn, r_scr])
            def _(e):
                e.activation(out=sgn_[:, :F], in_=x_ap, func=AF.Sign)
                return e.activation(out=a_[:, :F], in_=x_ap, func=AF.Abs)
            @p.step("vector", reads=[x_res, r_sgn], writes=[q_res, r_scr])
            def _(e):
                return emit_quant(e, x_ap, q_ap, scr, nb, p.chain(e))

        def transpose_to(q_res, q_ap_fn, nblk, dst_res, dst_ap_fn):
            for d in range(nblk):
                i = tpi[0] % 2
                tpi[0] += 1
                @p.step("tensor", reads=[q_res, r_idb], writes=[r_tp[i]])
                def _(e, d=d, i=i):
                    return e.transpose(tp[i][:], q_ap_fn(d), idb[:])
                @p.step("vector", reads=[r_tp[i]], writes=[dst_res])
                def _(e, d=d, i=i):
                    return e.tensor_copy(dst_ap_fn(d), tp[i][:])

        es1 = ExitStack()
        sb1 = lambda *a: es1.enter_context(nc.sbuf_tensor(*a))
        wgT = sb1("wgT", [128, 8, 2816], BF16)
        wuT = sb1("wuT", [128, 8, 2816], BF16)
        xqTt = sb1("xqTt", [128, 2, 1024], BF16)
        hqTt = sb1("hqTt", [128, 2816], BF16)
        hbuf = sb1("hbuf", [128, 2816], F32)
        gs = sb1("gs", [128, 512], BF16)
        if True:
            r_wgT, r_wuT = Res("wgT"), Res("wuT")
            r_xqTt = [Res("xqTt0"), Res("xqTt1")]

            for wdram, wT, wres in ((wg_d, wgT, r_wgT), (wu_d, wuT, r_wuT)):
                for i in range(NI):
                    bi = i % 2
                    @p.step("sync", writes=[r_xt[bi]], dma=True)
                    def _(e, dd, i=i, bi=bi, wdram=wdram):
                        dd(out=xt[:, bi, :1024], in_=wdram[i * 128:(i + 1) * 128, :])
                    quant_chain(r_xt[bi], xt[:, bi, :1024], r_qout[bi],
                                qout[:, bi, :1024], 64)
                    transpose_to(r_qout[bi], (lambda d, bi=bi: qout[:, bi, d * 128:(d + 1) * 128]),
                                 8, wres,
                                 (lambda d, i=i, wT=wT: wT[:, d, i * 128:(i + 1) * 128]))

            if debug:
                @p.step("gpsimd", reads=[r_wgT], dma=True)
                def _(e, dd):
                    dd(out=dbg_wgq[:, :], in_=wgT[:, 0, :])
            otiles = [(0, 512), (512, 512), (1024, 512), (1536, 512), (2048, 512), (2560, 256)]
            for t in range(NT):
                bi = t % 2
                @p.step("sync", writes=[r_xt[bi]], dma=True)
                def _(e, dd, t=t, bi=bi):
                    dd(out=xt[:, bi, :1024], in_=xin[t * 128:(t + 1) * 128, :])
                @p.step("vector", reads=[r_xt[bi]], writes=[r_scr])
                def _(e, bi=bi):
                    return emit_ln_stats(e, xt[:, bi, :1024], st, 1024, p.chain(e))
                @p.step("scalar", reads=[r_scr], writes=[r_scr])
                def _(e):
                    return e.activation(out=st['std'], in_=st['vpe'], func=AF.Sqrt)
                @p.step("vector", reads=[r_xt[bi], r_scr], writes=[r_nbuf[bi], r_scr])
                def _(e, bi=bi):
                    p.chain(e)(e.reciprocal(st['rstd'], st['std']))
                    return e.tensor_scalar(nbuf[:, bi, :], xt[:, bi, :1024],
                                           st['mu'], st['rstd'], ALU.subtract, ALU.mult)
                if debug and t == 0:
                    @p.step("gpsimd", reads=[r_nbuf[bi]], dma=True)
                    def _(e, dd, bi=bi):
                        dd(out=dbg_n[:, :], in_=nbuf[:, bi, :])
                quant_chain(r_nbuf[bi], nbuf[:, bi, :], r_qout[bi], qout[:, bi, :1024], 64)
                if debug and t == 0:
                    @p.step("gpsimd", reads=[r_qout[bi]], dma=True)
                    def _(e, dd, bi=bi):
                        dd(out=dbg_xq[:, :], in_=qout[:, bi, :1024])
                transpose_to(r_qout[bi], (lambda d, bi=bi: qout[:, bi, d * 128:(d + 1) * 128]),
                             8, r_xqTt[bi],
                             (lambda d, bi=bi: xqTt[:, bi, d * 128:(d + 1) * 128]))
                for (o0, ow) in otiles:
                    @p.step("tensor", reads=[r_xqTt[bi], r_wgT, r_wuT], writes=[r_gps, r_ups])
                    def _(e, bi=bi, o0=o0, ow=ow):
                        for d in range(8):
                            e.matmul(gps[:, :ow], xqTt[:, bi, d * 128:(d + 1) * 128],
                                     wgT[:, d, o0:o0 + ow], start=(d == 0), stop=(d == 7))
                        for d in range(8):
                            last = e.matmul(ups[:, :ow], xqTt[:, bi, d * 128:(d + 1) * 128],
                                            wuT[:, d, o0:o0 + ow], start=(d == 0), stop=(d == 7))
                        return last
                    @p.step("scalar", reads=[r_gps], writes=[r_gs])
                    def _(e, ow=ow):
                        return e.activation(out=gs[:, :ow], in_=gps[:, :ow], func=AF.Silu)
                    @p.step("vector", reads=[r_gs, r_ups], writes=[r_hbuf])
                    def _(e, o0=o0, ow=ow):
                        return e.tensor_tensor(hbuf[:, o0:o0 + ow], ups[:, :ow],
                                               gs[:, :ow], ALU.mult)
                for hf in range(2):
                    h0 = hf * 1408
                    quant_chain(r_hbuf, hbuf[:, h0:h0 + 1408], r_qout[hf],
                                qout[:, hf, :1408], 88)
                    transpose_to(r_qout[hf], (lambda d, hf=hf: qout[:, hf, d * 128:(d + 1) * 128]),
                                 11, r_hqt,
                                 (lambda d, h0=h0: hqTt[:, h0 + d * 128: h0 + (d + 1) * 128]))
                if debug and t == 0:
                    @p.step("gpsimd", reads=[r_hbuf], dma=True)
                    def _(e, dd):
                        dd(out=dbg_h[:, :], in_=hbuf[:, :])
                    @p.step("gpsimd", reads=[r_hqt], dma=True)
                    def _(e, dd):
                        dd(out=dbg_hq[:, :], in_=hqTt[:, :])
                @p.step("gpsimd", reads=[r_hqt], writes=[r_hqhbm[t]], dma=True)
                def _(e, dd, t=t):
                    dd(out=hq_hbm[t], in_=hqTt[:, :])

        es1.close()
        es2 = ExitStack()
        wdT = es2.enter_context(nc.sbuf_tensor("wdT", [128, NI, 1024], BF16))
        hqin = es2.enter_context(nc.sbuf_tensor("hqin", [128, 2, 2816], BF16))
        if True:
            r_wdT = Res("wdT")
            r_hqin = [Res("hq0"), Res("hq1")]
            for i in range(8):
                for hf in range(2):
                    h0 = hf * 1408
                    bi = (2 * i + hf) % 2
                    @p.step("sync", writes=[r_xt[bi]], dma=True)
                    def _(e, dd, i=i, h0=h0, bi=bi):
                        dd(out=xt[:, bi, :1408], in_=wd_d[i * 128:(i + 1) * 128, h0:h0 + 1408])
                    quant_chain(r_xt[bi], xt[:, bi, :1408], r_qout[bi],
                                qout[:, bi, :1408], 88)
                    transpose_to(r_qout[bi], (lambda d, bi=bi: qout[:, bi, d * 128:(d + 1) * 128]),
                                 11, r_wdT,
                                 (lambda d, i=i, h0=h0: wdT[:, h0 // 128 + d, i * 128:(i + 1) * 128]))
            for t in range(NT):
                bi = t % 2
                @p.step("sync", reads=[r_hqhbm[t]], writes=[r_hqin[bi]], dma=True)
                def _(e, dd, t=t, bi=bi):
                    dd(out=hqin[:, bi, :], in_=hq_hbm[t])
                @p.step("sync", writes=[r_xt[bi]], dma=True)
                def _(e, dd, t=t, bi=bi):
                    dd(out=xt[:, bi, :1024], in_=xin[t * 128:(t + 1) * 128, :])
                for oi, (o0, ow) in enumerate([(0, 512), (512, 512)]):
                    @p.step("tensor", reads=[r_hqin[bi], r_wdT], writes=[r_dps[oi]])
                    def _(e, bi=bi, o0=o0, ow=ow, oi=oi):
                        for i in range(NI):
                            last = e.matmul(dps[:, oi, :ow], hqin[:, bi, i * 128:(i + 1) * 128],
                                            wdT[:, i, o0:o0 + ow], start=(i == 0), stop=(i == NI - 1))
                        return last
                    @p.step("vector", reads=[r_dps[oi], r_xt[bi]], writes=[r_xo[bi]])
                    def _(e, bi=bi, o0=o0, ow=ow, oi=oi):
                        return e.tensor_tensor(xo[:, bi, o0:o0 + ow], dps[:, oi, :ow],
                                               xt[:, bi, o0:o0 + ow], ALU.add)
                @p.step("gpsimd", reads=[r_xo[bi]], writes=[], dma=True)
                def _(e, dd, t=t, bi=bi):
                    dd(out=xout[t * 128:(t + 1) * 128, :], in_=xo[:, bi, :])
        es2.close()
        p.drain()
    es.close()
    return nc





def build_l0attn(debug=False):
    nc = bass.Bass("TRN2", target_bir_lowering=False)
    NTE = 18
    xin = nc.dram_tensor("xin", [NTE * 128, D], F32, kind="ExternalInput").ap()
    wqkv_d = nc.dram_tensor("wqkv", [D, 3 * D], F32, kind="ExternalInput").ap()
    wout_d = nc.dram_tensor("wout", [D, D], F32, kind="ExternalInput").ap()
    bout_d = nc.dram_tensor("bout", [1, D], F32, kind="ExternalInput").ap()
    cos_d = nc.dram_tensor("cosd", [NTE * 128, 32], F32, kind="ExternalInput").ap()
    sin_d = nc.dram_tensor("sind", [NTE * 128, 32], F32, kind="ExternalInput").ap()
    msk_d = nc.dram_tensor("mskd", [16 * 128, 384], BF16, kind="ExternalInput").ap()
    idb_d = nc.dram_tensor("idb", [128, 128], BF16, kind="ExternalInput").ap()
    xout = nc.dram_tensor("xout", [TC, D], F32, kind="ExternalOutput").ap()
    if debug:
        dbg_q = nc.dram_tensor("dbg_q", [128, D], F32, kind="ExternalOutput").ap()
        dbg_k = nc.dram_tensor("dbg_k", [128, D], F32, kind="ExternalOutput").ap()
        dbg_v = nc.dram_tensor("dbg_v", [128, D], F32, kind="ExternalOutput").ap()
        dbg_p = nc.dram_tensor("dbg_p", [128, 384], F32, kind="ExternalOutput").ap()
        dbg_at = nc.dram_tensor("dbg_at", [128, 8 * 128], F32, kind="ExternalOutput").ap()

    aT_hbm = nc.dram_tensor("aTscratch", [8, 128, TC], BF16, kind="Internal").ap()
    es = ExitStack()
    sb = lambda *a: es.enter_context(nc.sbuf_tensor(*a))
    ps = lambda *a: es.enter_context(nc.psum_tensor(*a))
    idb = sb("idb_sb", [128, 128], BF16)
    bout_sb = sb("bout_sb", [1, D], BF16)
    ones_sb = sb("ones_sb", [1, 128], BF16)
    cos_sb = sb("cos_sb", [128, NTE, 32], BF16)
    sin_sb = sb("sin_sb", [128, NTE, 32], BF16)
    xt = sb("xt", [128, 2, D], F32)
    nbf = sb("nbf", [128, 2, D], BF16)
    ntS = sb("ntS", [128, 2, D], BF16)
    qt = sb("qt", [128, 2, D], BF16)
    kt = sb("kt", [128, 2, D], BF16)
    rt = sb("rt", [128, 2048], F32)
    vbuf = sb("vbuf", [128, NTE, D], BF16)
    qT = sb("qT", [128, 8, NTE * 128], BF16)
    kT = sb("kT", [128, 8, NTE * 128], BF16)
    pb = sb("pb", [128, 384], BF16)
    pT = sb("pT", [128, 384], BF16)
    nsm = sb("nsm2", [128, 16], F32)
    xo = sb("xo", [128, 2, D], F32)
    es1 = ExitStack()
    wq_sb = es1.enter_context(nc.sbuf_tensor("wq_sb", [128, 8, 3 * D], BF16))
    tpA = ps("tpA", [128, 128], BF16)
    tpB = ps("tpB", [128, 128], BF16)
    mmA = ps("mmA", [128, 512], F32)
    mmB = ps("mmB", [128, 512], F32)
    scp = ps("scp", [128, 384], F32)
    ops_ = ps("ops", [64, 128], F32)
    with nc.Block() as blk:
        p = Prog(nc, blk)
        st = dict(s1=nsm[:, 0:1], s2=nsm[:, 1:2], mu=nsm[:, 2:3], m2=nsm[:, 3:4],
                  t=nsm[:, 4:5], var=nsm[:, 5:6], vpe=nsm[:, 6:7], std=nsm[:, 7:8],
                  rstd=nsm[:, 8:9], mx=nsm[:, 9:10], bias=nsm[:, 10:11],
                  rs=nsm[:, 11:12], rr=nsm[:, 12:13], sq=rt)
        r_const = Res("const")
        r_xt = [Res(), Res()]
        r_nbf = [Res(), Res()]
        r_ntS = [Res(), Res()]
        r_qt = [Res(), Res()]
        r_kt = [Res(), Res()]
        r_rt = Res("rt")
        r_vbuf = Res("vbuf")
        r_qT, r_kT = Res("qT"), Res("kT")
        r_pb, r_pT = Res("pb"), Res("pT")
        r_nsm = Res("nsm")
        r_tp = [Res(), Res()]
        r_mm = [Res(), Res()]
        r_scp, r_ops = Res("scp"), Res("ops")
        r_xo = [Res(), Res()]
        tp = [tpA, tpB]
        mm = [mmA, mmB]

        @p.step("sync", writes=[r_const], dma=True)
        def _(e, dd):
            dd(out=idb[:], in_=idb_d[:, :])

        @p.step("gpsimd", writes=[r_const], dma=True)
        def _(e, dd):
            dd(out=cos_sb[:], in_=cos_d.rearrange("(t p) c -> p t c", p=128))
            dd(out=sin_sb[:], in_=sin_d.rearrange("(t p) c -> p t c", p=128))
            dd(out=wq_sb[:], in_=wqkv_d.rearrange("(c p) o -> p c o", p=128))
            dd(out=bout_sb[:], in_=bout_d[:, :])

        @p.step("vector", writes=[r_const])
        def _(e):
            return e.memset(ones_sb[:], 1.0)

        for t in range(NTE):
            bi = t % 2
            @p.step("sync", writes=[r_xt[bi]], dma=True)
            def _(e, dd, t=t):
                dd(out=xt[:, t % 2, :], in_=xin[t * 128:(t + 1) * 128, :])
            @p.step("vector", reads=[r_xt[bi]], writes=[r_nsm, r_rt])
            def _(e, bi=bi):
                return emit_ln_stats(e, xt[:, bi, :], st, 1024, p.chain(e))
            @p.step("scalar", reads=[r_nsm], writes=[r_nsm])
            def _(e):
                return e.activation(out=st['std'], in_=st['vpe'], func=AF.Sqrt)
            @p.step("vector", reads=[r_xt[bi], r_nsm], writes=[r_nbf[bi], r_nsm])
            def _(e, bi=bi):
                p.chain(e)(e.reciprocal(st['rstd'], st['std']))
                return e.tensor_scalar(nbf[:, bi, :], xt[:, bi, :],
                                       st['mu'], st['rstd'], ALU.subtract, ALU.mult)
            for d in range(8):
                ti = d % 2
                @p.step("tensor", reads=[r_nbf[bi], r_const], writes=[r_tp[ti]])
                def _(e, bi=bi, d=d, ti=ti):
                    return e.transpose(tp[ti][:], nbf[:, bi, d * 128:(d + 1) * 128], idb[:])
                @p.step("vector", reads=[r_tp[ti]], writes=[r_ntS[bi]])
                def _(e, bi=bi, d=d, ti=ti):
                    return e.tensor_copy(ntS[:, bi, d * 128:(d + 1) * 128], tp[ti][:])
            for s6 in range(6):
                mi = s6 % 2
                @p.step("tensor", reads=[r_ntS[bi], r_const], writes=[r_mm[mi]])
                def _(e, bi=bi, s6=s6, mi=mi):
                    for d in range(8):
                        last = e.matmul(mm[mi][:], ntS[:, bi, d * 128:(d + 1) * 128],
                                        wq_sb[:, d, s6 * 512:(s6 + 1) * 512],
                                        start=(d == 0), stop=(d == 7))
                    return last
                dst = [(qt, r_qt[bi], 0), (qt, r_qt[bi], 512), (kt, r_kt[bi], 0),
                       (kt, r_kt[bi], 512), (None, r_vbuf, 0), (None, r_vbuf, 512)][s6]
                @p.step("vector", reads=[r_mm[mi]], writes=[dst[1]])
                def _(e, bi=bi, s6=s6, mi=mi, dst=dst, t=t):
                    if dst[0] is None:
                        return e.tensor_copy(vbuf[:, t, dst[2]:dst[2] + 512], mm[mi][:])
                    return e.tensor_copy(dst[0][:, bi, dst[2]:dst[2] + 512], mm[mi][:])
            for (buf, rres) in ((qt, r_qt[bi]), (kt, r_kt[bi])):
                @p.step("vector", reads=[rres, r_const], writes=[rres, r_rt])
                def _(e, bi=bi, buf=buf, t=t):
                    v3 = buf[:, bi, :].rearrange("p (h x) -> p h x", x=64)
                    x1, x2 = v3[:, :, 0:32], v3[:, :, 32:64]
                    cb = cos_sb[:, t, :].unsqueeze(1).broadcast_to([128, 16, 32])
                    sbq = sin_sb[:, t, :].unsqueeze(1).broadcast_to([128, 16, 32])
                    t1 = rt[:, 0:512].rearrange("p (h x) -> p h x", x=32)
                    t2 = rt[:, 512:1024].rearrange("p (h x) -> p h x", x=32)
                    t3 = rt[:, 1024:1536].rearrange("p (h x) -> p h x", x=32)
                    t4 = rt[:, 1536:2048].rearrange("p (h x) -> p h x", x=32)
                    e.tensor_tensor(t1, x1, cb, ALU.mult)
                    e.tensor_tensor(t2, x2, sbq, ALU.mult)
                    e.tensor_tensor(t3, x2, cb, ALU.mult)
                    e.tensor_tensor(t4, x1, sbq, ALU.mult)
                    e.tensor_tensor(x1, t1, t2, ALU.subtract)
                    return e.tensor_tensor(x2, t3, t4, ALU.add)
            if debug and t == 2:
                @p.step("gpsimd", reads=[r_qt[bi], r_kt[bi], r_vbuf], dma=True)
                def _(e, dd, bi=bi, t=t):
                    dd(out=dbg_q[:, :], in_=qt[:, bi, :])
                    dd(out=dbg_k[:, :], in_=kt[:, bi, :])
                    dd(out=dbg_v[:, :], in_=vbuf[:, t, :])
            for (buf, rres, dstT, dres) in ((qt, r_qt[bi], qT, r_qT), (kt, r_kt[bi], kT, r_kT)):
                for d in range(8):
                    ti = d % 2
                    @p.step("tensor", reads=[rres, r_const], writes=[r_tp[ti]])
                    def _(e, bi=bi, d=d, ti=ti, buf=buf):
                        return e.transpose(tp[ti][:], buf[:, bi, d * 128:(d + 1) * 128], idb[:])
                    @p.step("vector", reads=[r_tp[ti]], writes=[dres])
                    def _(e, d=d, ti=ti, t=t, dstT=dstT):
                        return e.tensor_copy(dstT[:, d, t * 128:(t + 1) * 128], tp[ti][:])

        es1.close()
        es2 = ExitStack()
        msk_sb = es2.enter_context(nc.sbuf_tensor("msk_sb", [128, 16, 384], BF16))
        wo_sb = es2.enter_context(nc.sbuf_tensor("wo_sb", [128, 8, D], BF16))
        aTs = es2.enter_context(nc.sbuf_tensor("aTs", [128, 8, 128], BF16))
        aTin = es2.enter_context(nc.sbuf_tensor("aTin", [128, 2, 8, 128], BF16))
        r_aTs = Res("aTs")
        r_aTin = [Res(), Res()]
        r_aThbm = [Res() for _ in range(16)]

        @p.step("sync", writes=[r_const], dma=True)
        def _(e, dd):
            dd(out=msk_sb[:], in_=msk_d.rearrange("(t p) c -> p t c", p=128))

        @p.step("gpsimd", writes=[r_const], dma=True)
        def _(e, dd):
            dd(out=wo_sb[:], in_=wout_d.rearrange("(c p) o -> p c o", p=128))

        for qb in range(16):
            for h in range(16):
                hp0 = (h % 2) * 64
                hc = h // 2
                @p.step("tensor", reads=[r_qT, r_kT], writes=[r_scp])
                def _(e, h=h, qb=qb, hp0=hp0, hc=hc):
                    return e.matmul(scp[:],
                                    qT[hp0:hp0 + 64, hc, (qb + 1) * 128:(qb + 2) * 128],
                                    kT[hp0:hp0 + 64, hc, qb * 128:qb * 128 + 384],
                                    start=True, stop=True)
                @p.step("vector", reads=[r_scp], writes=[r_nsm])
                def _(e):
                    ch = p.chain(e)
                    ch(e.reduce_max(st['mx'], scp[:], AX.X))
                    return e.tensor_scalar_mul(st['bias'], st['mx'], -0.125)
                @p.step("scalar", reads=[r_scp, r_nsm], writes=[r_pb])
                def _(e):
                    return e.activation(out=pb[:], in_=scp[:], func=AF.Exp,
                                        bias=st['bias'], scale=0.125)
                @p.step("vector", reads=[r_pb, r_const], writes=[r_pb, r_nsm])
                def _(e, qb=qb):
                    ch = p.chain(e)
                    e.tensor_tensor(pb[:], pb[:], msk_sb[:, qb, :], ALU.mult)
                    ch(e.reduce_sum(st['rs'], pb[:], AX.X))
                    ch(e.reciprocal(st['rr'], st['rs']))
                    return e.tensor_scalar_mul(pb[:], pb[:], st['rr'])
                if debug and qb == 4 and h == 3:
                    @p.step("gpsimd", reads=[r_pb], dma=True)
                    def _(e, dd):
                        dd(out=dbg_p[:, :], in_=pb[:])
                for j in range(3):
                    ti = j % 2
                    @p.step("tensor", reads=[r_pb, r_const], writes=[r_tp[ti]])
                    def _(e, j=j, ti=ti):
                        return e.transpose(tp[ti][:], pb[:, j * 128:(j + 1) * 128], idb[:])
                    @p.step("vector", reads=[r_tp[ti]], writes=[r_pT])
                    def _(e, j=j, ti=ti):
                        return e.tensor_copy(pT[:, j * 128:(j + 1) * 128], tp[ti][:])
                @p.step("tensor", reads=[r_pT, r_vbuf], writes=[r_ops])
                def _(e, h=h, qb=qb):
                    for j in range(3):
                        last = e.matmul(ops_[:], vbuf[:, qb + j, h * 64:(h + 1) * 64],
                                        pT[:, j * 128:(j + 1) * 128],
                                        start=(j == 0), stop=(j == 2))
                    return last
                @p.step("vector", reads=[r_ops], writes=[r_aTs])
                def _(e, h=h, hp0=hp0, hc=hc):
                    return e.tensor_copy(aTs[hp0:hp0 + 64, hc, :], ops_[:])
            @p.step("gpsimd", reads=[r_aTs], writes=[r_aThbm[qb]], dma=True)
            def _(e, dd, qb=qb):
                dd(out=aT_hbm[:, :, qb * 128:(qb + 1) * 128].rearrange("c p q -> p c q"),
                   in_=aTs[:])
            if debug and qb == 4:
                @p.step("gpsimd", reads=[r_aTs], dma=True)
                def _(e, dd):
                    dd(out=dbg_at.rearrange("p (c q) -> p c q", q=128), in_=aTs[:])

        for t in range(16):
            bi = t % 2
            @p.step("sync", reads=[r_aThbm[t]], writes=[r_xt[bi], r_aTin[bi]], dma=True)
            def _(e, dd, t=t, bi=bi):
                dd(out=xt[:, bi, :], in_=xin[(t + 1) * 128:(t + 2) * 128, :])
                dd(out=aTin[:, bi, :, :],
                   in_=aT_hbm[:, :, t * 128:(t + 1) * 128].rearrange("c p q -> p c q"))
            for oi in range(2):
                mi = oi % 2
                @p.step("tensor", reads=[r_aTin[bi], r_const], writes=[r_mm[mi]])
                def _(e, t=t, oi=oi, mi=mi, bi=bi):
                    for d in range(8):
                        e.matmul(mm[mi][:], aTin[:, bi, d, :],
                                 wo_sb[:, d, oi * 512:(oi + 1) * 512],
                                 start=(d == 0), stop=False)
                    return e.matmul(mm[mi][:], ones_sb[:, 0:128],
                                    bout_sb[:, oi * 512:(oi + 1) * 512],
                                    start=False, stop=True)
                @p.step("vector", reads=[r_mm[mi], r_xt[bi]], writes=[r_xo[bi]])
                def _(e, t=t, oi=oi, mi=mi, bi=bi):
                    return e.tensor_tensor(xo[:, bi, oi * 512:(oi + 1) * 512], mm[mi][:],
                                           xt[:, bi, oi * 512:(oi + 1) * 512], ALU.add)
            @p.step("gpsimd", reads=[r_xo[bi]], dma=True)
            def _(e, dd, t=t, bi=bi):
                dd(out=xout[t * 128:(t + 1) * 128, :], in_=xo[:, bi, :])
        es2.close()
        p.drain()
    es.close()
    return nc


def _halo_slice(x_full, c):
    b, sl = c // 4, c % 4
    start, end = sl * TC, (sl + 1) * TC
    out = np.zeros((TC + 256, D), np.float32)
    lo, hi = max(0, start - 128), min(T, end + 128)
    out[lo - (start - 128):hi - (start - 128)] = x_full[b, lo:hi]
    return out


def _l0attn_consts(c):
    b, sl = c // 4, c % 4
    start = sl * TC
    pos = np.arange(start - 128, start + TC + 128)
    inv = (1.0 / 10000.0 ** (np.arange(0, 64, 2, dtype=np.float32) / 64)).astype(np.float32)
    pv = np.where((pos >= 0) & (pos < T), pos, 0).astype(np.float32)
    fr = np.outer(pv, inv).astype(np.float32)
    cos, sin = np.cos(fr).astype(np.float32), np.sin(fr).astype(np.float32)
    qb = np.arange(16)[:, None, None]
    qi = np.arange(128)[None, :, None]
    ki = np.arange(384)[None, None, :]
    gq = start + qb * 128 + qi
    gk = start - 128 + qb * 128 + ki
    keep = (np.abs(gk - gq) <= 128) & (gk >= 0) & (gk < T)
    msk = keep.astype(BF).reshape(16 * 128, 384)
    return cos, sin, msk


def run_l0attn(x_full, wqkv, wout, bout):
    nc = _get("l0attn", build_l0attn)
    in_maps = []
    for c in range(NCORES):
        cos, sin, msk = _l0attn_consts(c)
        in_maps.append({"xin": _halo_slice(x_full, c), "wqkv": wqkv, "wout": wout,
                        "bout": bout.reshape(1, D), "cosd": cos, "sind": sin,
                        "mskd": msk, "idb": _EYE_BF})
    res = run_bass_kernel_spmd(nc, in_maps, core_ids=list(range(NCORES)))
    return np.stack([r["xout"] for r in res.results]).reshape(2, 8192, 1024)





def build_l1attn_pre():
    nc = bass.Bass("TRN2", target_bir_lowering=False)
    xin = nc.dram_tensor("xin", [TC, D], F32, kind="ExternalInput").ap()
    wq_d = nc.dram_tensor("wq", [D, D], F32, kind="ExternalInput").ap()
    wk_d = nc.dram_tensor("wk", [D, D], F32, kind="ExternalInput").ap()
    wv_d = nc.dram_tensor("wv", [D, D], F32, kind="ExternalInput").ap()
    gb_d = nc.dram_tensor("gb", [128, 4 * 64], F32, kind="ExternalInput").ap()
    idb_d = nc.dram_tensor("idb", [128, 128], BF16, kind="ExternalInput").ap()
    qT_o = nc.dram_tensor("qT_o", [8, 128, TC], BF16, kind="ExternalOutput").ap()
    kT_o = nc.dram_tensor("kT_o", [8, 128, TC], BF16, kind="ExternalOutput").ap()
    ktm_o = nc.dram_tensor("ktm_o", [TC, D], BF16, kind="ExternalOutput").ap()
    vtm_o = nc.dram_tensor("vtm_o", [TC, D], BF16, kind="ExternalOutput").ap()

    es = ExitStack()
    sb = lambda *a: es.enter_context(nc.sbuf_tensor(*a))
    ps = lambda *a: es.enter_context(nc.psum_tensor(*a))
    idb = sb("idb_sb", [128, 128], BF16)
    gb = sb("gb_sb", [128, 4 * 64], F32)
    wln = sb("wln", [128, 3, 8, 1024], BF16)
    wT = sb("wT", [128, 3, 8, 1024], BF16)
    xt = sb("xt", [128, 2, D], F32)
    nbf = sb("nbf", [128, 2, D], BF16)
    ntS = sb("ntS", [128, 2, D], BF16)
    qtm = sb("qtm", [128, 2, D], BF16)
    ktm = sb("ktm", [128, 2, D], BF16)
    vtm = sb("vtm", [128, 2, D], BF16)
    sca = sb("sca", [128, 2048], F32)
    hst = sb("hst", [128, 8, 16], F32)
    sT = sb("sT", [128, 8, 128], BF16)
    nsm = sb("nsm3", [128, 16], F32)
    tpA = ps("tpA", [128, 128], BF16)
    tpB = ps("tpB", [128, 128], BF16)
    mmA = ps("mmA", [128, 512], F32)
    mmB = ps("mmB", [128, 512], F32)
    with nc.Block() as blk:
        p = Prog(nc, blk)
        st = dict(s1=nsm[:, 0:1], s2=nsm[:, 1:2], mu=nsm[:, 2:3], m2=nsm[:, 3:4],
                  t=nsm[:, 4:5], var=nsm[:, 5:6], vpe=nsm[:, 6:7], std=nsm[:, 7:8],
                  rstd=nsm[:, 8:9], sq=sca)
        hs = dict(s1=hst[:, 0, :], s2=hst[:, 1, :], mu=hst[:, 2, :], m2=hst[:, 3, :],
                  t=hst[:, 4, :], var=hst[:, 5, :], vpe=hst[:, 6, :], std=hst[:, 7, :])
        r_const = Res()
        r_wln, r_wT = Res(), Res()
        r_xt = [Res(), Res()]
        r_nbf = [Res(), Res()]
        r_ntS = [Res(), Res()]
        r_qtm = [Res(), Res()]
        r_ktm = [Res(), Res()]
        r_vtm = [Res(), Res()]
        r_sca, r_hst, r_nsm, r_sT = Res(), Res(), Res(), Res()
        r_tp = [Res(), Res()]
        r_mm = [Res(), Res()]
        tp, mm = [tpA, tpB], [mmA, mmB]

        @p.step("sync", writes=[r_const], dma=True)
        def _(e, dd):
            dd(out=idb[:], in_=idb_d[:, :])
            dd(out=gb[:], in_=gb_d[:, :])

        @p.step("gpsimd", writes=[r_wln], dma=True)
        def _(e, dd):
            dd(out=wln[:, 0], in_=wq_d.rearrange("(c p) o -> p c o", p=128))
            dd(out=wln[:, 1], in_=wk_d.rearrange("(c p) o -> p c o", p=128))
            dd(out=wln[:, 2], in_=wv_d.rearrange("(c p) o -> p c o", p=128))

        for w in range(3):
            for oc in range(8):
                for ic in range(8):
                    ti = ic % 2
                    @p.step("tensor", reads=[r_wln, r_const], writes=[r_tp[ti]])
                    def _(e, w=w, oc=oc, ic=ic, ti=ti):
                        return e.transpose(tp[ti][:], wln[:, w, oc, ic * 128:(ic + 1) * 128], idb[:])
                    @p.step("vector", reads=[r_tp[ti]], writes=[r_wT])
                    def _(e, w=w, oc=oc, ic=ic, ti=ti):
                        return e.tensor_copy(wT[:, w, ic, oc * 128:(oc + 1) * 128], tp[ti][:])

        def headln(e, src_ap, dst_ap, g0):
            ch = p.chain(e)
            x3 = src_ap.rearrange("p (h x) -> p h x", x=64)
            e.tensor_reduce(hs['s1'], x3, AX.X, ALU.add)
            e.tensor_tensor(sca[:, 0:1024], src_ap, src_ap, ALU.mult)
            ch(e.tensor_reduce(hs['s2'], sca[:, 0:1024].rearrange("p (h x) -> p h x", x=64),
                               AX.X, ALU.add))
            e.tensor_scalar_mul(hs['mu'], hs['s1'], 1.0 / 64)
            ch(e.tensor_scalar_mul(hs['m2'], hs['s2'], 1.0 / 64))
            ch(e.tensor_tensor(hs['t'], hs['mu'], hs['mu'], ALU.mult))
            ch(e.tensor_tensor(hs['var'], hs['m2'], hs['t'], ALU.subtract))
            return e.tensor_scalar_add(hs['vpe'], hs['var'], 1e-5)

        def headln_apply(e, src_ap, dst_ap, g0):
            ch = p.chain(e)
            ch(e.reciprocal(hs['s1'], hs['std']))
            x3 = src_ap.rearrange("p (h x) -> p h x", x=64)
            a3 = sca[:, 0:1024].rearrange("p (h x) -> p h x", x=64)
            mu_b = hs['mu'].unsqueeze(2).broadcast_to([128, 16, 64])
            rs_b = hs['s1'].unsqueeze(2).broadcast_to([128, 16, 64])
            gam = gb[:, g0 * 64:(g0 + 1) * 64].unsqueeze(1).broadcast_to([128, 16, 64])
            bet = gb[:, (g0 + 1) * 64:(g0 + 2) * 64].unsqueeze(1).broadcast_to([128, 16, 64])
            e.tensor_tensor(a3, x3, mu_b, ALU.subtract)
            e.tensor_tensor(a3, a3, rs_b, ALU.mult)
            e.tensor_tensor(a3, a3, gam, ALU.mult)
            d3 = dst_ap.rearrange("p (h x) -> p h x", x=64)
            return e.tensor_tensor(d3, a3, bet, ALU.add)

        for t in range(NT_L1 := TC // 128):
            bi = t % 2
            @p.step("sync", writes=[r_xt[bi]], dma=True)
            def _(e, dd, t=t, bi=bi):
                dd(out=xt[:, bi, :], in_=xin[t * 128:(t + 1) * 128, :])
            @p.step("vector", reads=[r_xt[bi]], writes=[r_nsm, r_sca])
            def _(e, bi=bi):
                return emit_ln_stats(e, xt[:, bi, :], st, 1024, p.chain(e))
            @p.step("scalar", reads=[r_nsm], writes=[r_nsm])
            def _(e):
                return e.activation(out=st['std'], in_=st['vpe'], func=AF.Sqrt)
            @p.step("vector", reads=[r_xt[bi], r_nsm], writes=[r_nbf[bi], r_nsm])
            def _(e, bi=bi):
                p.chain(e)(e.reciprocal(st['rstd'], st['std']))
                return e.tensor_scalar(nbf[:, bi, :], xt[:, bi, :],
                                       st['mu'], st['rstd'], ALU.subtract, ALU.mult)
            for d in range(8):
                ti = d % 2
                @p.step("tensor", reads=[r_nbf[bi], r_const], writes=[r_tp[ti]])
                def _(e, bi=bi, d=d, ti=ti):
                    return e.transpose(tp[ti][:], nbf[:, bi, d * 128:(d + 1) * 128], idb[:])
                @p.step("vector", reads=[r_tp[ti]], writes=[r_ntS[bi]])
                def _(e, bi=bi, d=d, ti=ti):
                    return e.tensor_copy(ntS[:, bi, d * 128:(d + 1) * 128], tp[ti][:])
            for (w, dstb, dres) in ((0, qtm, r_qtm[bi]), (1, ktm, r_ktm[bi]), (2, vtm, r_vtm[bi])):
                for oi in range(2):
                    mi = oi % 2
                    @p.step("tensor", reads=[r_ntS[bi], r_wT], writes=[r_mm[mi]])
                    def _(e, bi=bi, w=w, oi=oi, mi=mi):
                        for d in range(8):
                            last = e.matmul(mm[mi][:], ntS[:, bi, d * 128:(d + 1) * 128],
                                            wT[:, w, d, oi * 512:(oi + 1) * 512],
                                            start=(d == 0), stop=(d == 7))
                        return last
                    @p.step("vector", reads=[r_mm[mi]], writes=[dres])
                    def _(e, bi=bi, oi=oi, mi=mi, dstb=dstb):
                        return e.tensor_copy(dstb[:, bi, oi * 512:(oi + 1) * 512], mm[mi][:])
            for (buf, rres, g0) in ((qtm, r_qtm[bi], 0), (ktm, r_ktm[bi], 2)):
                @p.step("vector", reads=[rres], writes=[r_hst, r_sca])
                def _(e, bi=bi, buf=buf, g0=g0):
                    return headln(e, buf[:, bi, :], None, g0)
                @p.step("scalar", reads=[r_hst], writes=[r_hst])
                def _(e):
                    return e.activation(out=hs['std'], in_=hs['vpe'], func=AF.Sqrt)
                @p.step("vector", reads=[rres, r_hst, r_const], writes=[rres, r_hst, r_sca])
                def _(e, bi=bi, buf=buf, g0=g0):
                    return headln_apply(e, buf[:, bi, :], buf[:, bi, :], g0)
            @p.step("gpsimd", reads=[r_ktm[bi], r_vtm[bi]], dma=True)
            def _(e, dd, t=t, bi=bi):
                dd(out=ktm_o[t * 128:(t + 1) * 128, :], in_=ktm[:, bi, :])
                dd(out=vtm_o[t * 128:(t + 1) * 128, :], in_=vtm[:, bi, :])
            for (buf, rres, dram) in ((qtm, r_qtm[bi], qT_o), (ktm, r_ktm[bi], kT_o)):
                for d in range(8):
                    ti = d % 2
                    @p.step("tensor", reads=[rres, r_const], writes=[r_tp[ti]])
                    def _(e, bi=bi, d=d, ti=ti, buf=buf):
                        return e.transpose(tp[ti][:], buf[:, bi, d * 128:(d + 1) * 128], idb[:])
                    @p.step("vector", reads=[r_tp[ti]], writes=[r_sT])
                    def _(e, d=d, ti=ti):
                        return e.tensor_copy(sT[:, d, :], tp[ti][:])
                @p.step("gpsimd", reads=[r_sT], dma=True)
                def _(e, dd, t=t, dram=dram):
                    dd(out=dram[:, :, t * 128:(t + 1) * 128].rearrange("c p q -> p c q"),
                       in_=sT[:])
        p.drain()
    es.close()
    return nc


def build_l1attn_scan():
    nc = bass.Bass("TRN2", target_bir_lowering=False)
    NP = 4
    qT_d = nc.dram_tensor("qT", [NP, 64, T], BF16, kind="ExternalInput").ap()
    kT_d = nc.dram_tensor("kT", [NP, 64, T], BF16, kind="ExternalInput").ap()
    ktm_d = nc.dram_tensor("ktm", [NP, T, 64], BF16, kind="ExternalInput").ap()
    vtm_d = nc.dram_tensor("vtm", [NP, T, 64], BF16, kind="ExternalInput").ap()
    tri_d = nc.dram_tensor("tri", [128, 128], BF16, kind="ExternalInput").ap()
    oT_o = nc.dram_tensor("oT_o", [NP, 64, T], BF16, kind="ExternalOutput").ap()

    NCH = T // 128
    es = ExitStack()
    sb = lambda *a: es.enter_context(nc.sbuf_tensor(*a))
    ps = lambda *a: es.enter_context(nc.psum_tensor(*a))
    tri = sb("tri_sb", [128, 128], BF16)
    qTs = sb("qTs", [64, 2, T], BF16)
    kTs = sb("kTs", [64, 2, T], BF16)
    ktms = sb("ktms", [128, 2, NCH, 64], BF16)
    vtms = sb("vtms", [128, 2, NCH, 64], BF16)
    Sf = sb("Sf", [64, 64], F32)
    Sb = sb("Sb", [64, 64], BF16)
    asb = sb("asb", [128, 2, 128], BF16)
    ost = sb("ost", [64, 2, 128], BF16)
    aps_ = ps("aps", [128, 2, 128], F32)
    ops2 = ps("ops2", [64, 2, 128], F32)
    tps = ps("tps", [64, 2, 64], F32)
    with nc.Block() as blk:
        p = Prog(nc, blk)
        r_const = Res()
        r_in = [Res(), Res()]
        r_S = Res()
        r_asb = [Res(), Res()]
        r_ost = [Res(), Res()]
        r_aps = [Res(), Res()]
        r_ops = [Res(), Res()]
        r_tps = [Res(), Res()]

        @p.step("sync", writes=[r_const], dma=True)
        def _(e, dd):
            dd(out=tri[:], in_=tri_d[:, :])

        for j in range(NP):
            bj = j % 2
            @p.step("sync", writes=[r_in[bj]], dma=True)
            def _(e, dd, j=j, bj=bj):
                dd(out=qTs[:, bj, :], in_=qT_d[j])
                dd(out=kTs[:, bj, :], in_=kT_d[j])
                dd(out=ktms[:, bj], in_=ktm_d[j].rearrange("(c p) x -> p c x", p=128))
                dd(out=vtms[:, bj], in_=vtm_d[j].rearrange("(c p) x -> p c x", p=128))
            @p.step("vector", writes=[r_S])
            def _(e):
                e.memset(Sf[:], 0.0)
                return e.memset(Sb[:], 0.0)
            for c in range(NCH):
                ci = c % 2
                @p.step("tensor", reads=[r_in[bj]], writes=[r_aps[ci]])
                def _(e, bj=bj, c=c, ci=ci):
                    return e.matmul(aps_[:, ci, :], kTs[:, bj, c * 128:(c + 1) * 128],
                                    qTs[:, bj, c * 128:(c + 1) * 128], start=True, stop=True)
                @p.step("vector", reads=[r_aps[ci], r_const], writes=[r_asb[ci]])
                def _(e, ci=ci):
                    return e.tensor_tensor(asb[:, ci, :], aps_[:, ci, :], tri[:], ALU.mult)
                @p.step("tensor", reads=[r_asb[ci], r_in[bj], r_S], writes=[r_ops[ci]])
                def _(e, bj=bj, c=c, ci=ci):
                    e.matmul(ops2[:, ci, :], vtms[:, bj, c, :], asb[:, ci, :],
                             start=True, stop=False)
                    return e.matmul(ops2[:, ci, :], Sb[:, :], qTs[:, bj, c * 128:(c + 1) * 128],
                                    start=False, stop=True)
                @p.step("tensor", reads=[r_in[bj]], writes=[r_tps[ci]])
                def _(e, bj=bj, c=c, ci=ci):
                    return e.matmul(tps[:, ci, :], ktms[:, bj, c, :], vtms[:, bj, c, :],
                                    start=True, stop=True)
                @p.step("vector", reads=[r_ops[ci]], writes=[r_ost[ci]])
                def _(e, ci=ci):
                    return e.tensor_copy(ost[:, ci, :], ops2[:, ci, :])
                @p.step("gpsimd", reads=[r_ost[ci]], dma=True)
                def _(e, dd, j=j, c=c, ci=ci):
                    dd(out=oT_o[j][:, c * 128:(c + 1) * 128], in_=ost[:, ci, :])
                @p.step("vector", reads=[r_tps[ci], r_S], writes=[r_S])
                def _(e, ci=ci):
                    ch = p.chain(e)
                    ch(e.tensor_tensor(Sf[:], Sf[:], tps[:, ci, :], ALU.add))
                    return e.tensor_copy(Sb[:], Sf[:])
        p.drain()
    es.close()
    return nc


def build_l1attn_post():
    nc = bass.Bass("TRN2", target_bir_lowering=False)
    xin = nc.dram_tensor("xin", [TC, D], F32, kind="ExternalInput").ap()
    oT_d = nc.dram_tensor("oT", [8, 128, TC], BF16, kind="ExternalInput").ap()
    wo_d = nc.dram_tensor("wo", [D, D], F32, kind="ExternalInput").ap()
    idb_d = nc.dram_tensor("idb", [128, 128], BF16, kind="ExternalInput").ap()
    xout = nc.dram_tensor("xout", [TC, D], F32, kind="ExternalOutput").ap()

    es = ExitStack()
    sb = lambda *a: es.enter_context(nc.sbuf_tensor(*a))
    ps = lambda *a: es.enter_context(nc.psum_tensor(*a))
    idb = sb("idb_sb", [128, 128], BF16)
    wln = sb("wln", [128, 8, 1024], BF16)
    wT = sb("wT", [128, 8, 1024], BF16)
    oTs = sb("oTs", [128, 8, TC], BF16)
    xt = sb("xt", [128, 2, D], F32)
    xo = sb("xo", [128, 2, D], F32)
    tpA = ps("tpA", [128, 128], BF16)
    tpB = ps("tpB", [128, 128], BF16)
    mmA = ps("mmA", [128, 512], F32)
    mmB = ps("mmB", [128, 512], F32)
    with nc.Block() as blk:
        p = Prog(nc, blk)
        r_const, r_wln, r_wT, r_oTs = Res(), Res(), Res(), Res()
        r_xt = [Res(), Res()]
        r_xo = [Res(), Res()]
        r_tp = [Res(), Res()]
        r_mm = [Res(), Res()]
        tp, mm = [tpA, tpB], [mmA, mmB]

        @p.step("sync", writes=[r_const, r_oTs], dma=True)
        def _(e, dd):
            dd(out=idb[:], in_=idb_d[:, :])
            dd(out=oTs[:], in_=oT_d.rearrange("c p q -> p c q"))

        @p.step("gpsimd", writes=[r_wln], dma=True)
        def _(e, dd):
            dd(out=wln[:], in_=wo_d.rearrange("(c p) o -> p c o", p=128))

        for oc in range(8):
            for ic in range(8):
                ti = ic % 2
                @p.step("tensor", reads=[r_wln, r_const], writes=[r_tp[ti]])
                def _(e, oc=oc, ic=ic, ti=ti):
                    return e.transpose(tp[ti][:], wln[:, oc, ic * 128:(ic + 1) * 128], idb[:])
                @p.step("vector", reads=[r_tp[ti]], writes=[r_wT])
                def _(e, oc=oc, ic=ic, ti=ti):
                    return e.tensor_copy(wT[:, ic, oc * 128:(oc + 1) * 128], tp[ti][:])

        for t in range(TC // 128):
            bi = t % 2
            @p.step("sync", writes=[r_xt[bi]], dma=True)
            def _(e, dd, t=t, bi=bi):
                dd(out=xt[:, bi, :], in_=xin[t * 128:(t + 1) * 128, :])
            for oi in range(2):
                mi = oi % 2
                @p.step("tensor", reads=[r_oTs, r_wT], writes=[r_mm[mi]])
                def _(e, t=t, oi=oi, mi=mi):
                    for d in range(8):
                        last = e.matmul(mm[mi][:], oTs[:, d, t * 128:(t + 1) * 128],
                                        wT[:, d, oi * 512:(oi + 1) * 512],
                                        start=(d == 0), stop=(d == 7))
                    return last
                @p.step("vector", reads=[r_mm[mi], r_xt[bi]], writes=[r_xo[bi]])
                def _(e, oi=oi, mi=mi, bi=bi):
                    return e.tensor_tensor(xo[:, bi, oi * 512:(oi + 1) * 512], mm[mi][:],
                                           xt[:, bi, oi * 512:(oi + 1) * 512], ALU.add)
            @p.step("gpsimd", reads=[r_xo[bi]], dma=True)
            def _(e, dd, t=t, bi=bi):
                dd(out=xout[t * 128:(t + 1) * 128, :], in_=xo[:, bi, :])
        p.drain()
    es.close()
    return nc


def run_l1attn(x_full, Wq, Wk, Wv, Wo, qg, qb_, kg, kb_):
    nc = _get("l1attn_pre", build_l1attn_pre)
    gbrep = np.concatenate([np.broadcast_to(v.reshape(1, 64), (128, 64)) for v in
                            (qg, qb_, kg, kb_)], axis=1).astype(np.float32)
    xs = x_full.reshape(8, TC, D)
    im = [{"xin": np.ascontiguousarray(xs[c]), "wq": Wq, "wk": Wk, "wv": Wv,
           "gb": gbrep, "idb": _EYE_BF} for c in range(NCORES)]
    r1 = run_bass_kernel_spmd(nc, im, core_ids=list(range(NCORES))).results
    qT = np.zeros((2, 16, 64, T), BF)
    kT = np.zeros((2, 16, 64, T), BF)
    ktm = np.zeros((2, T, 16, 64), BF)
    vtm = np.zeros((2, T, 16, 64), BF)
    for c in range(NCORES):
        b, sl = c // 4, c % 4
        cols = slice(sl * TC, (sl + 1) * TC)
        for h in range(16):
            qT[b, h, :, cols] = r1[c]["qT_o"][h // 2, (h % 2) * 64:(h % 2) * 64 + 64, :]
            kT[b, h, :, cols] = r1[c]["kT_o"][h // 2, (h % 2) * 64:(h % 2) * 64 + 64, :]
        ktm[b, cols.start:cols.stop] = r1[c]["ktm_o"].reshape(TC, 16, 64)
        vtm[b, cols.start:cols.stop] = r1[c]["vtm_o"].reshape(TC, 16, 64)
    nc2 = _get("l1attn_scan", build_l1attn_scan)
    tri = np.triu(np.ones((128, 128), np.float32)).astype(BF)
    im2 = []
    for c in range(NCORES):
        prs = [4 * c + j for j in range(4)]
        im2.append({
            "qT": np.stack([qT[pp // 16, pp % 16] for pp in prs]),
            "kT": np.stack([kT[pp // 16, pp % 16] for pp in prs]),
            "ktm": np.stack([np.ascontiguousarray(ktm[pp // 16, :, pp % 16]) for pp in prs]),
            "vtm": np.stack([np.ascontiguousarray(vtm[pp // 16, :, pp % 16]) for pp in prs]),
            "tri": tri,
        })
    r2 = run_bass_kernel_spmd(nc2, im2, core_ids=list(range(NCORES))).results
    oT = np.zeros((2, 16, 64, T), BF)
    for c in range(NCORES):
        for j in range(4):
            pp = 4 * c + j
            oT[pp // 16, pp % 16] = r2[c]["oT_o"][j]
    nc3 = _get("l1attn_post", build_l1attn_post)
    im3 = []
    for c in range(NCORES):
        b, sl = c // 4, c % 4
        cols = slice(sl * TC, (sl + 1) * TC)
        oTi = np.zeros((8, 128, TC), BF)
        for h in range(16):
            oTi[h // 2, (h % 2) * 64:(h % 2) * 64 + 64, :] = oT[b, h][:, cols]
        im3.append({"xin": np.ascontiguousarray(xs[c]), "oT": oTi, "wo": Wo, "idb": _EYE_BF})
    r3 = run_bass_kernel_spmd(nc3, im3, core_ids=list(range(NCORES))).results
    return np.stack([r["xout"] for r in r3]).reshape(2, 8192, 1024)





def build_l1ffn_pre(debug=False):
    nc = bass.Bass("TRN2", target_bir_lowering=False)
    xin = nc.dram_tensor("xin", [TC, D], F32, kind="ExternalInput").ap()
    qw_d = nc.dram_tensor("qw", [D, D], F32, kind="ExternalInput").ap()
    kw_d = nc.dram_tensor("kw", [D, D], F32, kind="ExternalInput").ap()
    selw_d = nc.dram_tensor("selw", [128, D], F32, kind="ExternalInput").ap()
    vw_d = nc.dram_tensor("vw", [64, D, 128], F32, kind="ExternalInput").ap()
    rsc_d = nc.dram_tensor("rsc", [128, 1], F32, kind="ExternalInput").ap()
    idb_d = nc.dram_tensor("idb", [128, 128], BF16, kind="ExternalInput").ap()
    qT_o = nc.dram_tensor("qT_o", [8, 128, TC], BF16, kind="ExternalOutput").ap()
    kT_o = nc.dram_tensor("kT_o", [8, 128, TC], BF16, kind="ExternalOutput").ap()
    V_o = nc.dram_tensor("V_o", [TC, 8, 128], BF16, kind="ExternalOutput").ap()
    go_o = nc.dram_tensor("go_o", [TC, 64], F32, kind="ExternalOutput").ap()
    if debug:
        dbg_probs = nc.dram_tensor("dbg_probs", [128, 16 * 128], F32, kind="ExternalOutput").ap()
        dbg_rank = nc.dram_tensor("dbg_rank", [128, 16 * 128], F32, kind="ExternalOutput").ap()
        dbg_gates = nc.dram_tensor("dbg_gates", [128, 16 * 128], F32, kind="ExternalOutput").ap()

    NT16 = TC // 128
    es = ExitStack()
    sb = lambda *a: es.enter_context(nc.sbuf_tensor(*a))
    ps = lambda *a: es.enter_context(nc.psum_tensor(*a))
    idb = sb("idb_sb", [128, 128], BF16)
    rsc = sb("rsc_sb", [128, 1], F32)
    selT = sb("selT", [128, 8, 128], BF16)
    selw = sb("selw_sb", [128, 1024], BF16)
    nT = sb("nT", [128, 8, TC], BF16)
    qst = sb("qst", [128, 2, 512], BF16)
    probs = sb("probs", [128, NT16, 128], F32)
    rank = sb("rank", [128, NT16, 128], F32)
    gates = sb("gates", [128, NT16, 16, 8], F32)
    Vb = sb("Vb", [128, NT16, 8, 128], BF16)
    xt = sb("xt", [128, 2, D], F32)
    nbf = sb("nbf", [128, 2, D], BF16)
    wvq = sb("wvq", [128, 2, 8, 512], BF16)
    nsm = sb("nsm4", [128, 16], F32)
    sca = sb("sca", [128, 2304], F32)
    tpA = ps("tpA", [128, 128], BF16)
    tpB = ps("tpB", [128, 128], BF16)
    mmA = ps("mmA", [128, 512], F32)
    mmB = ps("mmB", [128, 512], F32)
    lgp = ps("lgp", [128, 128], F32)
    with nc.Block() as blk:
        p = Prog(nc, blk)
        st = dict(s1=nsm[:, 0:1], s2=nsm[:, 1:2], mu=nsm[:, 2:3], m2=nsm[:, 3:4],
                  t=nsm[:, 4:5], var=nsm[:, 5:6], vpe=nsm[:, 6:7], std=nsm[:, 7:8],
                  rstd=nsm[:, 8:9], sq=sca)
        r_const, r_selw, r_selT = Res(), Res(), Res()
        r_nT = Res()
        r_probs, r_rank, r_gates, r_Vb = Res(), Res(), Res(), Res()
        r_xt = [Res(), Res()]
        r_nbf = [Res(), Res()]
        r_wvq = [Res(), Res()]
        r_nsm, r_sca = Res(), Res()
        r_tp = [Res(), Res()]
        r_mm = [Res(), Res()]
        r_lgp = Res()
        tp, mm = [tpA, tpB], [mmA, mmB]

        @p.step("sync", writes=[r_const], dma=True)
        def _(e, dd):
            dd(out=idb[:], in_=idb_d[:, :])
            dd(out=rsc[:], in_=rsc_d[:, :])

        @p.step("gpsimd", writes=[r_selw], dma=True)
        def _(e, dd):
            dd(out=selw[:], in_=selw_d[:, :])

        for ic in range(8):
            ti = ic % 2
            @p.step("tensor", reads=[r_selw, r_const], writes=[r_tp[ti]])
            def _(e, ic=ic, ti=ti):
                return e.transpose(tp[ti][:], selw[:, ic * 128:(ic + 1) * 128], idb[:])
            @p.step("vector", reads=[r_tp[ti]], writes=[r_selT])
            def _(e, ic=ic, ti=ti):
                return e.tensor_copy(selT[:, ic, :], tp[ti][:])

        es1 = ExitStack()
        wqkT = es1.enter_context(nc.sbuf_tensor("wqkT", [128, 2, 8, 1024], BF16))
        wln = es1.enter_context(nc.sbuf_tensor("wln1", [128, 8, 1024], BF16))
        r_wqkT, r_wln = Res(), Res()
        for w, wd in ((0, qw_d), (1, kw_d)):
            @p.step("gpsimd", writes=[r_wln], dma=True)
            def _(e, dd, wd=wd):
                dd(out=wln[:], in_=wd.rearrange("(c p) o -> p c o", p=128))
            for oc in range(8):
                for ic in range(8):
                    ti = ic % 2
                    @p.step("tensor", reads=[r_wln, r_const], writes=[r_tp[ti]])
                    def _(e, oc=oc, ic=ic, ti=ti):
                        return e.transpose(tp[ti][:], wln[:, oc, ic * 128:(ic + 1) * 128], idb[:])
                    @p.step("vector", reads=[r_tp[ti]], writes=[r_wqkT])
                    def _(e, w=w, oc=oc, ic=ic, ti=ti):
                        return e.tensor_copy(wqkT[:, w, ic, oc * 128:(oc + 1) * 128], tp[ti][:])

        for t in range(NT16):
            bi = t % 2
            @p.step("sync", writes=[r_xt[bi]], dma=True)
            def _(e, dd, t=t, bi=bi):
                dd(out=xt[:, bi, :], in_=xin[t * 128:(t + 1) * 128, :])
            @p.step("vector", reads=[r_xt[bi]], writes=[r_nsm, r_sca])
            def _(e, bi=bi):
                return emit_ln_stats(e, xt[:, bi, :], st, 1024, p.chain(e))
            @p.step("scalar", reads=[r_nsm], writes=[r_nsm])
            def _(e):
                return e.activation(out=st['std'], in_=st['vpe'], func=AF.Sqrt)
            @p.step("vector", reads=[r_xt[bi], r_nsm], writes=[r_nbf[bi], r_nsm])
            def _(e, bi=bi):
                p.chain(e)(e.reciprocal(st['rstd'], st['std']))
                return e.tensor_scalar(nbf[:, bi, :], xt[:, bi, :],
                                       st['mu'], st['rstd'], ALU.subtract, ALU.mult)
            for d in range(8):
                ti = d % 2
                @p.step("tensor", reads=[r_nbf[bi], r_const], writes=[r_tp[ti]])
                def _(e, bi=bi, d=d, ti=ti):
                    return e.transpose(tp[ti][:], nbf[:, bi, d * 128:(d + 1) * 128], idb[:])
                @p.step("vector", reads=[r_tp[ti]], writes=[r_nT])
                def _(e, d=d, ti=ti, t=t):
                    return e.tensor_copy(nT[:, d, t * 128:(t + 1) * 128], tp[ti][:])
            @p.step("tensor", reads=[r_nT, r_selT], writes=[r_lgp])
            def _(e, t=t):
                for d in range(8):
                    last = e.matmul(lgp[:], nT[:, d, t * 128:(t + 1) * 128],
                                    selT[:, d, :], start=(d == 0), stop=(d == 7))
                return last
            @p.step("scalar", reads=[r_lgp], writes=[r_probs])
            def _(e, t=t):
                return e.activation(out=probs[:, t, :], in_=lgp[:], func=AF.Sigmoid)

        r_qst = [Res(), Res()]
        for w, dram in ((0, qT_o), (1, kT_o)):
            for hp in range(8):
                for tq in range(4):
                    mi = (hp * 4 + tq) % 2
                    @p.step("tensor", reads=[r_nT, r_wqkT], writes=[r_mm[mi]])
                    def _(e, w=w, hp=hp, tq=tq, mi=mi):
                        for d in range(8):
                            last = e.matmul(mm[mi][:], wqkT[:, w, d, hp * 128:(hp + 1) * 128],
                                            nT[:, d, tq * 512:(tq + 1) * 512],
                                            start=(d == 0), stop=(d == 7))
                        return last
                    @p.step("vector", reads=[r_mm[mi]], writes=[r_qst[mi]])
                    def _(e, mi=mi):
                        return e.tensor_copy(qst[:, mi, :], mm[mi][:])
                    @p.step("gpsimd", reads=[r_qst[mi]], dma=True)
                    def _(e, dd, hp=hp, tq=tq, mi=mi, dram=dram):
                        dd(out=dram[hp][:, tq * 512:(tq + 1) * 512], in_=qst[:, mi, :])
        es1.close()

        @p.step("vector", reads=[r_probs, r_const], writes=[r_rank, r_gates, r_sca])
        def _(e):
            ch = p.chain(e)
            p3 = probs[:].rearrange("p t (g e) -> p (t g) e", e=8)
            r3 = rank[:].rearrange("p t (g e) -> p (t g) e", e=8)
            e.memset(rank[:], 0.0)
            NG = NT16 * 16
            t1 = sca[:, 0:NG * 8].rearrange("p (g e) -> p g e", e=8)
            for dlt in range(1, 8):
                wdt = 8 - dlt
                a = p3[:, :, 0:wdt]
                b = p3[:, :, dlt:8]
                ch(e.tensor_tensor(t1[:, :, 0:wdt], b, a, ALU.is_gt))
                ch(e.tensor_tensor(r3[:, :, 0:wdt], r3[:, :, 0:wdt], t1[:, :, 0:wdt], ALU.add))
                ch(e.tensor_tensor(t1[:, :, 0:wdt], a, b, ALU.is_ge))
                ch(e.tensor_tensor(r3[:, :, dlt:8], r3[:, :, dlt:8], t1[:, :, 0:wdt], ALU.add))
            ch(e.tensor_single_scalar(r3, r3, 2.0, ALU.is_lt))
            ch(e.tensor_tensor(t1, p3, r3, ALU.mult))
            s_ = sca[:, NG * 8:NG * 9]
            ch(e.tensor_reduce(s_.rearrange("p (g x) -> p g x", x=1), t1, AX.X, ALU.add))
            ch(e.tensor_scalar_max(s_, s_, 1e-9))
            ch(e.reciprocal(s_, s_))
            ch(e.tensor_scalar_mul(s_, s_, rsc[:, 0:1]))
            sb_ = s_.rearrange("p (g x) -> p g x", x=1).broadcast_to([128, NG, 8])
            g4 = gates[:].rearrange("p t g e -> p (t g) e")
            return e.tensor_tensor(g4, t1, sb_, ALU.mult)

        @p.step("gpsimd", reads=[r_gates], dma=True)
        def _(e, dd):
            dd(out=go_o.rearrange("(t p) (h e) -> p t h e", p=128, e=8),
               in_=gates[:, :, 8:16, :])
        if debug:
            @p.step("gpsimd", reads=[r_probs, r_rank, r_gates], dma=True)
            def _(e, dd):
                dd(out=dbg_probs.rearrange("p (t x) -> p t x", x=128), in_=probs[:])
                dd(out=dbg_rank.rearrange("p (t x) -> p t x", x=128), in_=rank[:])
                dd(out=dbg_gates.rearrange("p (t h e) -> p t h e", h=16, e=8), in_=gates[:])

        for Q in range(16):
            bq = Q % 2
            @p.step("gpsimd", writes=[r_wvq[bq]], dma=True)
            def _(e, dd, Q=Q, bq=bq):
                for l in range(4):
                    dd(out=wvq[:, bq, :, l * 128:(l + 1) * 128],
                       in_=vw_d[4 * Q + l].rearrange("(c p) x -> p c x", p=128))
            for t in range(NT16):
                mi = t % 2
                @p.step("tensor", reads=[r_nT, r_wvq[bq]], writes=[r_mm[mi]])
                def _(e, Q=Q, t=t, mi=mi, bq=bq):
                    for d in range(8):
                        last = e.matmul(mm[mi][:], nT[:, d, t * 128:(t + 1) * 128],
                                        wvq[:, bq, d, :], start=(d == 0), stop=(d == 7))
                    return last
                @p.step("vector", reads=[r_mm[mi], r_gates], writes=[r_Vb])
                def _(e, Q=Q, t=t, mi=mi):
                    last = None
                    for l in range(4):
                        he = 4 * Q + l
                        h, ei = he // 8, he % 8
                        gcol = gates[:, t, h, ei:ei + 1]
                        dst = Vb[:, t, h, :]
                        if ei == 0:
                            last = e.tensor_scalar_mul(dst, mm[mi][:, l * 128:(l + 1) * 128], gcol)
                        else:
                            last = e.scalar_tensor_tensor(dst, mm[mi][:, l * 128:(l + 1) * 128],
                                                          gcol, dst, ALU.mult, ALU.add)
                    return last

        @p.step("gpsimd", reads=[r_Vb], dma=True)
        def _(e, dd):
            dd(out=V_o.rearrange("(t p) h x -> p t h x", p=128), in_=Vb[:])
        p.drain()
    es.close()
    return nc


def build_l1ffn_flash(debug=False):
    nc = bass.Bass("TRN2", target_bir_lowering=False)
    NP = 2
    SCALE = float(1.0 / np.sqrt(128.0))
    qT_d = nc.dram_tensor("qT", [NP, 128, T], BF16, kind="ExternalInput").ap()
    kT_d = nc.dram_tensor("kT", [NP, 128, T], BF16, kind="ExternalInput").ap()
    V_d = nc.dram_tensor("V", [NP, T, 129], BF16, kind="ExternalInput").ap()
    ctxT_o = nc.dram_tensor("ctxT_o", [NP, 128, T], BF16, kind="ExternalOutput").ap()
    rs_o = nc.dram_tensor("rs_o", [NP, T], F32, kind="ExternalOutput").ap()
    if debug:
        dbg_nrm = nc.dram_tensor("dbg_nrm", [1, 2 * T], F32, kind="ExternalOutput").ap()
        dbg_smax = nc.dram_tensor("dbg_smax", [1, 8], F32, kind="ExternalOutput").ap()
        dbg_bcol = nc.dram_tensor("dbg_bcol", [128, 1], F32, kind="ExternalOutput").ap()

    NKB = T // 128
    es = ExitStack()
    sb = lambda *a: es.enter_context(nc.sbuf_tensor(*a))
    ps = lambda *a: es.enter_context(nc.psum_tensor(*a))
    qs = sb("qs", [128, 2, T], BF16)
    ks = sb("ks", [128, 2, T], BF16)
    vs = sb("vs", [128, 2, NKB, 129], BF16)
    onec = sb("onec", [128, 1], BF16)
    oner = sb("oner", [1, 128], BF16)
    sqb = sb("sqb", [128, 512], BF16)
    nrm = sb("nrm", [1, 2 * T], F32)
    smax = sb("smax", [1, 8], F32)
    bcol = sb("bcol", [128, 1], F32)
    pbt = sb("pbt", [128, 2, 512], BF16)
    cst = sb("cst", [128, 512], BF16)
    rst = sb("rst", [1, 512], F32)
    scp = ps("scp", [128, 2, 512], F32)
    ctxp = ps("ctxp", [128, 512], F32)
    rsp = ps("rsp", [1, 512], F32)
    nrp = ps("nrp", [1, 512], F32)
    mbp = ps("mbp", [128, 1], F32)
    with nc.Block() as blk:
        p = Prog(nc, blk)
        r_const = Res()
        r_in = [Res(), Res()]
        r_sqb, r_nrm, r_smax, r_bcol = Res(), Res(), Res(), Res()
        r_pbt = [Res(), Res()]
        r_cst, r_rst = Res(), Res()
        r_scp = [Res(), Res()]
        r_ctxp, r_rsp, r_nrp, r_mbp = Res(), Res(), Res(), Res()

        @p.step("vector", writes=[r_const])
        def _(e):
            e.memset(onec[:], 1.0)
            return e.memset(oner[:], 1.0)

        for j in range(NP):
            bj = j % 2
            @p.step("sync", writes=[r_in[bj]], dma=True)
            def _(e, dd, j=j, bj=bj):
                dd(out=qs[:, bj, :], in_=qT_d[j])
                dd(out=ks[:, bj, :], in_=kT_d[j])
                dd(out=vs[:, bj], in_=V_d[j].rearrange("(c p) x -> p c x", p=128))
            for w, buf in ((0, qs), (1, ks)):
                for sl in range(16):
                    @p.step("scalar", reads=[r_in[bj]], writes=[r_sqb])
                    def _(e, bj=bj, buf=buf, sl=sl):
                        return e.activation(out=sqb[:], in_=buf[:, bj, sl * 512:(sl + 1) * 512],
                                            func=AF.Square)
                    @p.step("tensor", reads=[r_sqb, r_const], writes=[r_nrp])
                    def _(e):
                        return e.matmul(nrp[:], onec[:], sqb[:], start=True, stop=True)
                    @p.step("vector", reads=[r_nrp], writes=[r_nrm])
                    def _(e, w=w, sl=sl):
                        return e.tensor_copy(nrm[0:1, w * T + sl * 512: w * T + (sl + 1) * 512],
                                             nrp[:])
                @p.step("vector", reads=[r_nrm], writes=[r_smax])
                def _(e, w=w):
                    return e.reduce_max(smax[0:1, w:w + 1], nrm[0:1, w * T:(w + 1) * T], AX.X)
            @p.step("vector", reads=[r_smax], writes=[r_smax])
            def _(e):
                ch = p.chain(e)
                ch(e.tensor_tensor(smax[0:1, 2:3], smax[0:1, 0:1], smax[0:1, 1:2], ALU.mult))
                return e.tensor_scalar_mul(smax[0:1, 3:4], smax[0:1, 2:3], SCALE * SCALE)
            @p.step("scalar", reads=[r_smax], writes=[r_smax])
            def _(e):
                return e.activation(out=smax[0:1, 4:5], in_=smax[0:1, 3:4], func=AF.Sqrt)
            @p.step("vector", reads=[r_smax], writes=[r_smax])
            def _(e):
                return e.tensor_scalar(smax[0:1, 5:6], smax[0:1, 4:5], -1.0, None,
                                       ALU.mult, out_dtype_hint := ALU.bypass)
            @p.step("vector", reads=[r_smax], writes=[r_smax])
            def _(e):
                return e.tensor_copy(smax[0:1, 6:7].bitcast(F32), smax[0:1, 5:6])
            @p.step("vector", reads=[r_smax], writes=[r_sqb])
            def _(e):
                return e.tensor_copy(sqb[0:1, 0:1], smax[0:1, 5:6])
            @p.step("tensor", reads=[r_sqb, r_const], writes=[r_mbp])
            def _(e):
                return e.matmul(mbp[:], oner[:, :], sqb[0:1, 0:1], start=True, stop=True)
            @p.step("vector", reads=[r_mbp], writes=[r_bcol])
            def _(e):
                return e.tensor_copy(bcol[:], mbp[:])
            if debug and j == 0:
                @p.step("gpsimd", reads=[r_nrm, r_smax, r_bcol], dma=True)
                def _(e, dd):
                    dd(out=dbg_nrm[:, :], in_=nrm[:])
                    dd(out=dbg_smax[:, :], in_=smax[:])
                    dd(out=dbg_bcol[:, :], in_=bcol[:])
            for qt in range(16):
                for kb in range(NKB):
                    ci = kb % 2
                    @p.step("tensor", reads=[r_in[bj]], writes=[r_scp[ci]])
                    def _(e, bj=bj, qt=qt, kb=kb, ci=ci):
                        return e.matmul(scp[:, ci, :], ks[:, bj, kb * 128:(kb + 1) * 128],
                                        qs[:, bj, qt * 512:(qt + 1) * 512], start=True, stop=True)
                    @p.step("scalar", reads=[r_scp[ci], r_bcol], writes=[r_pbt[ci]])
                    def _(e, ci=ci):
                        return e.activation(out=pbt[:, ci, :], in_=scp[:, ci, :], func=AF.Exp,
                                            bias=bcol[:, 0:1], scale=SCALE)
                    @p.step("tensor", reads=[r_pbt[ci], r_in[bj]],
                            writes=[r_ctxp] if kb < NKB - 1 else [r_ctxp],
                            )
                    def _(e, bj=bj, kb=kb, ci=ci):
                        return e.matmul(ctxp[:], vs[:, bj, kb, 0:128], pbt[:, ci, :],
                                        start=(kb == 0), stop=(kb == NKB - 1))
                    @p.step("tensor", reads=[r_pbt[ci], r_in[bj]], writes=[r_rsp])
                    def _(e, bj=bj, kb=kb, ci=ci):
                        return e.matmul(rsp[:], vs[:, bj, kb, 128:129], pbt[:, ci, :],
                                        start=(kb == 0), stop=(kb == NKB - 1))
                @p.step("vector", reads=[r_ctxp], writes=[r_cst])
                def _(e):
                    return e.tensor_copy(cst[:], ctxp[:])
                @p.step("vector", reads=[r_rsp], writes=[r_rst])
                def _(e):
                    return e.tensor_copy(rst[:], rsp[:])
                @p.step("gpsimd", reads=[r_cst, r_rst], dma=True)
                def _(e, dd, j=j, qt=qt):
                    dd(out=ctxT_o[j][:, qt * 512:(qt + 1) * 512], in_=cst[:])
                    dd(out=rs_o[j:j + 1, qt * 512:(qt + 1) * 512], in_=rst[:])
        p.drain()
    es.close()
    return nc


def build_l1ffn_post():
    nc = bass.Bass("TRN2", target_bir_lowering=False)
    xin = nc.dram_tensor("xin", [TC, D], F32, kind="ExternalInput").ap()
    ct_d = nc.dram_tensor("ct", [8, 128, TC], BF16, kind="ExternalInput").ap()
    go_d = nc.dram_tensor("go", [TC, 64], F32, kind="ExternalInput").ap()
    rs_d = nc.dram_tensor("rs", [TC, 8], F32, kind="ExternalInput").ap()
    ow_d = nc.dram_tensor("ow", [64, 128, D], F32, kind="ExternalInput").ap()
    repl_d = nc.dram_tensor("repl", [64, 64 * 128], BF16, kind="ExternalInput").ap()
    idb_d = nc.dram_tensor("idb", [128, 128], BF16, kind="ExternalInput").ap()
    xout = nc.dram_tensor("xout", [TC, D], F32, kind="ExternalOutput").ap()

    NT16 = TC // 128
    es = ExitStack()
    sb = lambda *a: es.enter_context(nc.sbuf_tensor(*a))
    ps = lambda *a: es.enter_context(nc.psum_tensor(*a))
    idb = sb("idb_sb", [128, 128], BF16)
    repl = sb("repl_sb", [64, 64 * 128], BF16)
    ctx = sb("ctx", [128, 8, TC], BF16)
    go4 = sb("go4", [128, NT16, 8, 8], F32)
    rcp = sb("rcp", [128, NT16, 8], F32)
    gob = sb("gob", [128, NT16, 64], BF16)
    goT = sb("goT", [64, TC], BF16)
    cT = sb("cT", [128, 64, 512], BF16)
    gbs = sb("gbs", [128, 512], BF16)
    owt = sb("owt", [128, 2, 512], BF16)
    oTsb = sb("oTsb", [128, 8, 512], BF16)
    xt = sb("xt", [128, 2, D], F32)
    xo = sb("xo", [128, 2, D], F32)
    odps = ps("odps", [128, 4, 512], F32)
    gbp = ps("gbp", [128, 512], F32)
    tpA = ps("tpA", [128, 128], BF16)
    tpB = ps("tpB", [128, 128], BF16)
    with nc.Block() as blk:
        p = Prog(nc, blk)
        r_const, r_ctx, r_go4, r_rcp, r_gob, r_goT = Res(), Res(), Res(), Res(), Res(), Res()
        r_cT, r_gbs, r_oTsb = Res(), Res(), Res()
        r_owt = [Res(), Res()]
        r_xt = [Res(), Res()]
        r_xo = [Res(), Res()]
        r_odps, r_gbp = Res(), Res()
        r_tp = [Res(), Res()]
        tp = [tpA, tpB]

        @p.step("sync", writes=[r_const, r_ctx], dma=True)
        def _(e, dd):
            dd(out=idb[:], in_=idb_d[:, :])
            dd(out=repl[:], in_=repl_d[:, :])
            dd(out=ctx[:], in_=ct_d.rearrange("c p q -> p c q"))

        @p.step("gpsimd", writes=[r_go4, r_rcp], dma=True)
        def _(e, dd):
            dd(out=go4[:], in_=go_d.rearrange("(t p) (h e) -> p t h e", p=128, e=8))
            dd(out=rcp[:], in_=rs_d.rearrange("(t p) h -> p t h", p=128))

        @p.step("vector", reads=[r_go4, r_rcp], writes=[r_rcp, r_gob])
        def _(e):
            ch = p.chain(e)
            ch(e.reciprocal(rcp[:], rcp[:]))
            rb = rcp[:].unsqueeze(3).broadcast_to([128, NT16, 8, 8])
            return e.tensor_tensor(gob[:].rearrange("p t (h e) -> p t h e", e=8),
                                   go4[:], rb, ALU.mult)
        for t in range(NT16):
            ti = t % 2
            @p.step("tensor", reads=[r_gob, r_const], writes=[r_tp[ti]])
            def _(e, t=t, ti=ti):
                return e.transpose(tp[ti][0:64, :], gob[:, t, :], idb[:])
            @p.step("vector", reads=[r_tp[ti]], writes=[r_goT])
            def _(e, t=t, ti=ti):
                return e.tensor_copy(goT[:, t * 128:(t + 1) * 128], tp[ti][0:64, :])

        for tsl in range(4):
            for he in range(64):
                h = he // 8
                @p.step("tensor", reads=[r_goT, r_const], writes=[r_gbp])
                def _(e, he=he, tsl=tsl):
                    return e.matmul(gbp[:], repl[:, he * 128:(he + 1) * 128],
                                    goT[:, tsl * 512:(tsl + 1) * 512], start=True, stop=True)
                @p.step("vector", reads=[r_gbp], writes=[r_gbs])
                def _(e):
                    return e.tensor_copy(gbs[:], gbp[:])
                @p.step("vector", reads=[r_gbs, r_ctx], writes=[r_cT])
                def _(e, he=he, h=h, tsl=tsl):
                    return e.tensor_tensor(cT[:, he, :], ctx[:, h, tsl * 512:(tsl + 1) * 512],
                                           gbs[:], ALU.mult)
            for dh in range(2):
                for he in range(64):
                    bo = he % 2
                    @p.step("gpsimd", writes=[r_owt[bo]], dma=True)
                    def _(e, dd, he=he, dh=dh, bo=bo):
                        dd(out=owt[:, bo], in_=ow_d[he][:, dh * 512:(dh + 1) * 512])
                    @p.step("tensor", reads=[r_owt[bo], r_cT], writes=[r_odps])
                    def _(e, he=he, dh=dh, bo=bo):
                        for ds in range(4):
                            last = e.matmul(odps[:, ds, :], owt[:, bo, ds * 128:(ds + 1) * 128],
                                            cT[:, he, :], start=(he == 0), stop=(he == 63))
                        return last
                @p.step("vector", reads=[r_odps], writes=[r_oTsb])
                def _(e, dh=dh):
                    return e.tensor_copy(oTsb[:, 4 * dh:4 * dh + 4, :], odps[:])
            for tt in range(4):
                t = tsl * 4 + tt
                bi = t % 2
                @p.step("sync", writes=[r_xt[bi]], dma=True)
                def _(e, dd, t=t, bi=bi):
                    dd(out=xt[:, bi, :], in_=xin[t * 128:(t + 1) * 128, :])
                for d in range(8):
                    ti = d % 2
                    @p.step("tensor", reads=[r_oTsb, r_const], writes=[r_tp[ti]])
                    def _(e, d=d, tt=tt, ti=ti):
                        return e.transpose(tp[ti][:], oTsb[:, d, tt * 128:(tt + 1) * 128], idb[:])
                    @p.step("vector", reads=[r_tp[ti], r_xt[bi]], writes=[r_xo[bi]])
                    def _(e, d=d, ti=ti, bi=bi):
                        return e.tensor_tensor(xo[:, bi, d * 128:(d + 1) * 128], tp[ti][:],
                                               xt[:, bi, d * 128:(d + 1) * 128], ALU.add)
                @p.step("gpsimd", reads=[r_xo[bi]], dma=True)
                def _(e, dd, t=t, bi=bi):
                    dd(out=xout[t * 128:(t + 1) * 128, :], in_=xo[:, bi, :])
        p.drain()
    es.close()
    return nc


def run_l1ffn(x_full, q_w, k_w, v_w, o_w, sel_v_w, sel_o_w, route_scale):
    nc = _get("l1ffn_pre", build_l1ffn_pre)
    xs = x_full.reshape(8, TC, D)
    selw = np.ascontiguousarray(np.concatenate([sel_v_w, sel_o_w], 0), dtype=np.float32)
    rscr = np.broadcast_to(route_scale.reshape(1, 1), (128, 1)).astype(np.float32)
    im = [{"xin": np.ascontiguousarray(xs[c]), "qw": q_w, "kw": k_w, "selw": selw,
           "vw": v_w, "rsc": rscr, "idb": _EYE_BF} for c in range(NCORES)]
    r1 = run_bass_kernel_spmd(nc, im, core_ids=list(range(NCORES))).results
    qT = np.zeros((2, 8, 128, T), BF)
    kT = np.zeros((2, 8, 128, T), BF)
    Vv = np.zeros((2, T, 8, 129), BF)
    Vv[:, :, :, 128] = 1.0
    go = np.zeros((2, T, 64), np.float32)
    for c in range(NCORES):
        b, sl = c // 4, c % 4
        cols = slice(sl * TC, (sl + 1) * TC)
        for h in range(8):
            qT[b, h][:, cols] = r1[c]["qT_o"][h]
            kT[b, h][:, cols] = r1[c]["kT_o"][h]
        Vv[b, cols.start:cols.stop, :, 0:128] = r1[c]["V_o"]
        go[b, cols.start:cols.stop] = r1[c]["go_o"]
    nc2 = _get("l1ffn_flash", build_l1ffn_flash)
    im2 = []
    for c in range(NCORES):
        prs = [2 * c, 2 * c + 1]
        im2.append({"qT": np.stack([qT[pp // 8, pp % 8] for pp in prs]),
                    "kT": np.stack([kT[pp // 8, pp % 8] for pp in prs]),
                    "V": np.stack([np.ascontiguousarray(Vv[pp // 8, :, pp % 8]) for pp in prs])})
    r2 = run_bass_kernel_spmd(nc2, im2, core_ids=list(range(NCORES))).results
    ctxT = np.zeros((2, 8, 128, T), BF)
    rs = np.zeros((2, 8, T), np.float32)
    for c in range(NCORES):
        for jj in range(2):
            pp = 2 * c + jj
            ctxT[pp // 8, pp % 8] = r2[c]["ctxT_o"][jj]
            rs[pp // 8, pp % 8] = r2[c]["rs_o"][jj]
    nc3 = _get("l1ffn_post", build_l1ffn_post)
    repl = np.zeros((64, 64 * 128), np.float32)
    for he in range(64):
        repl[he, he * 128:(he + 1) * 128] = 1.0
    repl = repl.astype(BF)
    im3 = []
    for c in range(NCORES):
        b, sl = c // 4, c % 4
        cols = slice(sl * TC, (sl + 1) * TC)
        im3.append({"xin": np.ascontiguousarray(xs[c]),
                    "ct": np.ascontiguousarray(ctxT[b][:, :, cols]),
                    "go": np.ascontiguousarray(go[b, cols.start:cols.stop]),
                    "rs": np.ascontiguousarray(rs[b][:, cols.start:cols.stop].T),
                    "ow": o_w, "repl": repl, "idb": _EYE_BF})
    r3 = run_bass_kernel_spmd(nc3, im3, core_ids=list(range(NCORES))).results
    return np.stack([r["xout"] for r in r3]).reshape(2, 8192, 1024)





HTOK = 4096
QTOK = 2048
I32 = mybir.dt.int32


def build_fused(_DBG_STAGE=0):
    nc = bass.Bass("TRN2", target_bir_lowering=False)
    dram = lambda *a, **k: nc.dram_tensor(*a, **k).ap()
    xh_d = dram("xh", [HTOK + 256, D], F32, kind="ExternalInput")
    wqkv_d = dram("wqkv", [D, 3 * D], F32, kind="ExternalInput")
    wout_d = dram("wout", [D, D], F32, kind="ExternalInput")
    bout_d = dram("bout", [1, D], F32, kind="ExternalInput")
    wg_d = dram("wg", [2816, D], F32, kind="ExternalInput")
    wu_d = dram("wu", [2816, D], F32, kind="ExternalInput")
    wd_d = dram("wd", [D, 2816], F32, kind="ExternalInput")
    wq1_d = dram("wq1", [D, D], F32, kind="ExternalInput")
    wk1_d = dram("wk1", [D, D], F32, kind="ExternalInput")
    wv1_d = dram("wv1", [D, D], F32, kind="ExternalInput")
    wo1_d = dram("wo1", [D, D], F32, kind="ExternalInput")
    gb_d = dram("gb", [128, 4 * 64], F32, kind="ExternalInput")
    qw2_d = dram("qw2", [D, D], F32, kind="ExternalInput")
    kw2_d = dram("kw2", [D, D], F32, kind="ExternalInput")
    selw_d = dram("selw", [128, D], F32, kind="ExternalInput")
    vw_d = dram("vw", [64, D, 128], F32, kind="ExternalInput")
    ow_d = dram("ow", [64, 128, D], F32, kind="ExternalInput")
    rsc_d = dram("rsc", [128, 1], F32, kind="ExternalInput")
    idb_d = dram("idb", [128, 128], BF16, kind="ExternalInput")
    idf_d = dram("idf", [128, 128], F32, kind="ExternalInput")
    cos_d = dram("cosd", [HTOK + 256, 32], F32, kind="ExternalInput")
    sin_d = dram("sind", [HTOK + 256, 32], F32, kind="ExternalInput")
    msk_d = dram("mskd", [4, 8 * 128, 384], BF16, kind="ExternalInput")
    tri_d = dram("trid", [128, 128], BF16, kind="ExternalInput")
    repl_d = dram("repl", [64, 64 * 128], BF16, kind="ExternalInput")
    runid_d = dram("runid", [1, 16], I32, kind="ExternalInput")
    sidec_d = dram("sidec", [128, 1], F32, kind="ExternalInput")
    qoff_d = dram("qoff", [1, 16], I32, kind="ExternalInput")
    xout = dram("xout", [QTOK, D], F32, kind="ExternalOutput")
    x1_h = dram("x1_h", [HTOK, D], F32, kind="Internal")
    x2_h = dram("x2_h", [HTOK, D], F32, kind="Internal")
    x3_h = dram("x3_h", [HTOK, D], F32, kind="Internal")
    aT_h = dram("aT_h", [8, 128, 1024], BF16, kind="Internal")
    hq_h = dram("hq_h", [HTOK // 128, 128, 2816], BF16, kind="Internal")
    qT1_h = dram("qT1_h", [8, 128, HTOK], BF16, kind="Internal")
    kT1_h = dram("kT1_h", [8, 128, HTOK], BF16, kind="Internal")
    ktm_h = dram("ktm_h", [HTOK, D], BF16, kind="Internal")
    vtm_h = dram("vtm_h", [HTOK, D], BF16, kind="Internal")
    oT_h = dram("oT_h", [8, 128, HTOK], BF16, kind="Internal")
    qT2_h = dram("qT2_h", [8, 128, HTOK], BF16, kind="Internal")
    go_h = dram("go_h", [HTOK, 64], F32, kind="Internal")
    x3q_h = dram("x3q_h", [QTOK, D], F32, kind="Internal")
    goq_h = dram("goq_h", [QTOK, 64], F32, kind="Internal")
    qTq_h = dram("qTq_h", [8, 128, QTOK], BF16, kind="Internal")
    kvst_s = dram("kvst_s", [2, 64, 16 * 64], F32, kind="Internal", addr_space="Shared")
    kn_s = dram("kn_s", [2, 1, 16], F32, kind="Internal", addr_space="Shared")
    K_s = dram("K_s", [2, 8, 128, HTOK], BF16, kind="Internal", addr_space="Shared")
    V_s = dram("V_s", [2, HTOK, 8, 132], BF16, kind="Internal", addr_space="Shared")
    flg_s = dram("flg_s", [2, 8, 16], I32, kind="Internal", addr_space="Shared")

    es = ExitStack()
    sb = lambda *a: es.enter_context(nc.sbuf_tensor(*a))
    ps = lambda *a: es.enter_context(nc.psum_tensor(*a))
    idb = sb("idb_sb", [128, 128], BF16)
    tri = sb("tri_sb", [128, 128], BF16)
    rsc = sb("rsc_sb", [128, 1], F32)
    sidec = sb("sidec_sb", [128, 1], F32)
    rid = sb("rid_sb", [1, 16], I32)
    qoffs = sb("qoff_sb", [1, 16], I32)
    xt = sb("xt", [128, 2, 1408], F32)
    nbf = sb("nbf", [128, 2, D], BF16)
    ntS = sb("ntS", [128, 2, D], BF16)
    qout = sb("qout", [128, 2, 1408], BF16)
    z_ = sb("z", [128, 1408], F32)
    a_ = sb("a", [128, 1408], F32)
    b_ = sb("bq", [128, 1408], F32)
    c_ = sb("cq", [128, 1408], F32)
    sgn_ = sb("sgn", [128, 1408], BF16)
    amax_ = sb("amax", [128, 88], F32)
    s6_ = sb("s6", [128, 88], F32)
    r2_ = sb("r2", [128, 88], F32)
    nsm = sb("nsm", [128, 16], F32)
    xo = sb("xo", [128, 2, D], F32)
    tpA = ps("tpA", [128, 128], BF16)
    tpB = ps("tpB", [128, 128], BF16)
    mmA = ps("mmA", [128, 512], F32)
    mmB = ps("mmB", [128, 512], F32)

    with nc.Block() as blk:
        p = Prog(nc, blk)
        G = {}
        G.update(locals())
        scr = dict(z=z_, a=a_, b=b_, c=c_, sgn=sgn_, amax=amax_, s6=s6_, r2=r2_)
        st = dict(s1=nsm[:, 0:1], s2=nsm[:, 1:2], mu=nsm[:, 2:3], m2=nsm[:, 3:4],
                  t=nsm[:, 4:5], var=nsm[:, 5:6], vpe=nsm[:, 6:7], std=nsm[:, 7:8],
                  rstd=nsm[:, 8:9], mx=nsm[:, 9:10], bias=nsm[:, 10:11],
                  rs=nsm[:, 11:12], rr=nsm[:, 12:13], sq=z_)
        R = dict(const=Res("const"), xt=[Res(), Res()], nbf=[Res(), Res()],
                 ntS=[Res(), Res()], qout=[Res(), Res()], scr=Res("scr"),
                 sgn=Res("sgn"), nsm=Res("nsm"), tp=[Res(), Res()],
                 mm=[Res(), Res()], xo=[Res(), Res()])
        tp, mm = [tpA, tpB], [mmA, mmB]
        RIDTOK = 1000003

        @p.step("sync", writes=[R['const']], dma=True)
        def _(e, dd):
            dd(out=idb[:], in_=idb_d[:, :])
            dd(out=tri[:], in_=tri_d[:, :])
            dd(out=rsc[:], in_=rsc_d[:, :])
            dd(out=sidec[:], in_=sidec_d[:, :])
            dd(out=rid[0:1, :], in_=runid_d[:, :])
            dd(out=qoffs[0:1, :], in_=qoff_d[:, :])

        def ln_tile(x_res, x_ap, dst_res, dst_ap):
            @p.step("vector", reads=[x_res], writes=[R['nsm'], R['scr']])
            def _(e):
                return emit_ln_stats(e, x_ap, st, 1024, p.chain(e))
            @p.step("scalar", reads=[R['nsm']], writes=[R['nsm']])
            def _(e):
                return e.activation(out=st['std'], in_=st['vpe'], func=AF.Sqrt)
            @p.step("vector", reads=[x_res, R['nsm']], writes=[dst_res, R['nsm']])
            def _(e):
                p.chain(e)(e.reciprocal(st['rstd'], st['std']))
                return e.tensor_scalar(dst_ap, x_ap, st['mu'], st['rstd'],
                                       ALU.subtract, ALU.mult)

        def transpose_blocks(src_res, src_ap_fn, nblk, dst_res, dst_ap_fn):
            for d in range(nblk):
                ti = d % 2
                @p.step("tensor", reads=[src_res, R['const']], writes=[R['tp'][ti]])
                def _(e, d=d, ti=ti):
                    return e.transpose(tp[ti][:], src_ap_fn(d), idb[:])
                @p.step("vector", reads=[R['tp'][ti]], writes=[dst_res])
                def _(e, d=d, ti=ti):
                    return e.tensor_copy(dst_ap_fn(d), tp[ti][:])

        def quant_chain(x_res, x_ap, q_res, q_ap, nb):
            F = nb * 16
            @p.step("scalar", reads=[x_res], writes=[R['sgn'], R['scr']])
            def _(e):
                e.activation(out=sgn_[:, :F], in_=x_ap, func=AF.Sign)
                return e.activation(out=a_[:, :F], in_=x_ap, func=AF.Abs)
            @p.step("vector", reads=[x_res, R['sgn']], writes=[q_res, R['scr']])
            def _(e):
                return emit_quant(e, x_ap, q_ap, scr, nb, p.chain(e))

        bar_regs = {}

        def get_regs(e):
            if not bar_regs:
                bar_regs['t'] = e.alloc_register("bar_t")
                bar_regs['r0'] = e.alloc_register("bar_r0")
                bar_regs['r1'] = e.alloc_register("bar_r1")
                bar_regs['c'] = e.alloc_register("bar_c")
            return bar_regs

        def emit_flag_store(e, sig):
            rg = get_regs(e)
            e.reg_load(rg['t'], rid[0:1, 0:1])
            e.reg_alu(rg['t'], rg['t'], sig + 1, ALU.add)
            for arm in nc.switch_core_id(engines=[e], n=128):
                slot = int(arm) % 2
                e.store(flg_s[slot, sig, 0:1][None], rg['t'])

        def emit_flag_poll(e, sig, both=True):
            rg = get_regs(e)
            e.reg_load(rg['t'], rid[0:1, 0:1])
            e.reg_alu(rg['t'], rg['t'], sig + 1, ALU.add)
            def poll():
                e.reg_load(rg['r0'], flg_s[0, sig, 0:1][None])
                e.reg_alu(rg['r0'], rg['r0'], rg['t'], ALU.not_equal)
                if both:
                    e.reg_load(rg['r1'], flg_s[1, sig, 0:1][None])
                    e.reg_alu(rg['r1'], rg['r1'], rg['t'], ALU.not_equal)
                    e.reg_alu(rg['c'], rg['r0'], rg['r1'], ALU.bitwise_or)
                else:
                    e.reg_alu(rg['c'], rg['r0'], 0, ALU.add)
            poll()
            with e.While(rg['c']):
                poll()

        cos_sb = sb("cos_sb", [128, 34, 32], BF16)
        sin_sb = sb("sin_sb", [128, 34, 32], BF16)

        @p.step("gpsimd", writes=[R['const']], dma=True)
        def _(e, dd):
            dd(out=cos_sb[:], in_=cos_d.rearrange("(t p) c -> p t c", p=128))
            dd(out=sin_sb[:], in_=sin_d.rearrange("(t p) c -> p t c", p=128))

        def emit_l0attn(sub):
            NTE, NQB = 10, 8
            es1 = ExitStack()
            sb1 = lambda *a: es1.enter_context(nc.sbuf_tensor(*a))
            qt_b = sb1("qt%d" % sub, [128, 2, D], BF16)
            kt_b = sb1("kt%d" % sub, [128, 2, D], BF16)
            rt = sb1("rt%d" % sub, [128, 2048], F32)
            vbuf = sb1("vbuf%d" % sub, [128, NTE, D], BF16)
            qT = sb1("qT%d" % sub, [128, 8, NTE * 128], BF16)
            kT = sb1("kT%d" % sub, [128, 8, NTE * 128], BF16)
            pb = sb1("pb%d" % sub, [128, 384], BF16)
            pT = sb1("pT%d" % sub, [128, 384], BF16)
            r_qt = [Res(), Res()]
            r_kt = [Res(), Res()]
            r_rt, r_vbuf, r_qT, r_kT = Res(), Res(), Res(), Res()
            r_pb, r_pT = Res(), Res()
            st2 = dict(st, sq=rt)
            es2 = ExitStack()
            wq_sb = es2.enter_context(nc.sbuf_tensor("wqsb%d" % sub, [128, 8, 3 * D], BF16))
            r_wq = Res()

            @p.step("gpsimd", writes=[r_wq], dma=True)
            def _(e, dd):
                dd(out=wq_sb[:], in_=wqkv_d.rearrange("(c p) o -> p c o", p=128))

            x0 = sub * 1024
            for t in range(NTE):
                bi = t % 2
                @p.step("sync", writes=[R['xt'][bi]], dma=True)
                def _(e, dd, t=t, bi=bi):
                    dd(out=xt[:, bi, :1024], in_=xh_d[x0 + t * 128: x0 + (t + 1) * 128, :])
                @p.step("vector", reads=[R['xt'][bi]], writes=[R['nsm'], r_rt])
                def _(e, bi=bi):
                    return emit_ln_stats(e, xt[:, bi, :1024], st2, 1024, p.chain(e))
                @p.step("scalar", reads=[R['nsm']], writes=[R['nsm']])
                def _(e):
                    return e.activation(out=st['std'], in_=st['vpe'], func=AF.Sqrt)
                @p.step("vector", reads=[R['xt'][bi], R['nsm']], writes=[R['nbf'][bi], R['nsm']])
                def _(e, bi=bi):
                    p.chain(e)(e.reciprocal(st['rstd'], st['std']))
                    return e.tensor_scalar(nbf[:, bi, :], xt[:, bi, :1024],
                                           st['mu'], st['rstd'], ALU.subtract, ALU.mult)
                transpose_blocks(R['nbf'][bi], (lambda d, bi=bi: nbf[:, bi, d * 128:(d + 1) * 128]),
                                 8, R['ntS'][bi], (lambda d, bi=bi: ntS[:, bi, d * 128:(d + 1) * 128]))
                for s6x in range(6):
                    mi = s6x % 2
                    @p.step("tensor", reads=[R['ntS'][bi], r_wq], writes=[R['mm'][mi]])
                    def _(e, bi=bi, s6x=s6x, mi=mi):
                        for d in range(8):
                            last = e.matmul(mm[mi][:], ntS[:, bi, d * 128:(d + 1) * 128],
                                            wq_sb[:, d, s6x * 512:(s6x + 1) * 512],
                                            start=(d == 0), stop=(d == 7))
                        return last
                    dst = [(qt_b, r_qt[bi], 0), (qt_b, r_qt[bi], 512), (kt_b, r_kt[bi], 0),
                           (kt_b, r_kt[bi], 512), (None, r_vbuf, 0), (None, r_vbuf, 512)][s6x]
                    @p.step("vector", reads=[R['mm'][mi]], writes=[dst[1]])
                    def _(e, bi=bi, mi=mi, dst=dst, t=t):
                        if dst[0] is None:
                            return e.tensor_copy(vbuf[:, t, dst[2]:dst[2] + 512], mm[mi][:])
                        return e.tensor_copy(dst[0][:, bi, dst[2]:dst[2] + 512], mm[mi][:])
                for (buf, rres) in ((qt_b, r_qt[bi]), (kt_b, r_kt[bi])):
                    @p.step("vector", reads=[rres, R['const']], writes=[rres, r_rt])
                    def _(e, bi=bi, buf=buf, t=t):
                        v3 = buf[:, bi, :].rearrange("p (h x) -> p h x", x=64)
                        x1a, x2a = v3[:, :, 0:32], v3[:, :, 32:64]
                        ct = sub * 8 + t
                        cb = cos_sb[:, ct, :].unsqueeze(1).broadcast_to([128, 16, 32])
                        sbq = sin_sb[:, ct, :].unsqueeze(1).broadcast_to([128, 16, 32])
                        t1 = rt[:, 0:512].rearrange("p (h x) -> p h x", x=32)
                        t2 = rt[:, 512:1024].rearrange("p (h x) -> p h x", x=32)
                        t3 = rt[:, 1024:1536].rearrange("p (h x) -> p h x", x=32)
                        t4 = rt[:, 1536:2048].rearrange("p (h x) -> p h x", x=32)
                        e.tensor_tensor(t1, x1a, cb, ALU.mult)
                        e.tensor_tensor(t2, x2a, sbq, ALU.mult)
                        e.tensor_tensor(t3, x2a, cb, ALU.mult)
                        e.tensor_tensor(t4, x1a, sbq, ALU.mult)
                        e.tensor_tensor(x1a, t1, t2, ALU.subtract)
                        return e.tensor_tensor(x2a, t3, t4, ALU.add)
                for (buf, rres, dstT, dres) in ((qt_b, r_qt[bi], qT, r_qT), (kt_b, r_kt[bi], kT, r_kT)):
                    transpose_blocks(rres, (lambda d, bi=bi, buf=buf: buf[:, bi, d * 128:(d + 1) * 128]),
                                     8, dres, (lambda d, t=t, dstT=dstT: dstT[:, d, t * 128:(t + 1) * 128]))
            p.fence()
            es2.close()
            es3 = ExitStack()
            msk_sb = es3.enter_context(nc.sbuf_tensor("msksb%d" % sub, [128, 8, 384], BF16))
            aTs = es3.enter_context(nc.sbuf_tensor("aTs%d" % sub, [128, 8, 128], BF16))
            wo_sb = es3.enter_context(nc.sbuf_tensor("wosb%d" % sub, [128, 8, D], BF16))
            aTin = es3.enter_context(nc.sbuf_tensor("aTin%d" % sub, [128, 2, 8, 128], BF16))
            scp = es3.enter_context(nc.psum_tensor("scp%d" % sub, [128, 384], F32))
            ops_ = es3.enter_context(nc.psum_tensor("ops%d" % sub, [64, 128], F32))
            bout_sb = es3.enter_context(nc.sbuf_tensor("boutsb%d" % sub, [1, D], BF16))
            ones_sb = es3.enter_context(nc.sbuf_tensor("onessb%d" % sub, [1, 128], BF16))
            r_msk, r_aTs, r_scp, r_ops, r_wo = Res(), Res(), Res(), Res(), Res()
            r_aTin = [Res(), Res()]
            r_aTh = [Res() for _ in range(NQB)]

            @p.step("sync", writes=[r_msk], dma=True)
            def _(e, dd):
                dd(out=msk_sb[:], in_=msk_d[sub].rearrange("(t p) c -> p t c", p=128))

            @p.step("gpsimd", writes=[r_wo], dma=True)
            def _(e, dd):
                dd(out=wo_sb[:], in_=wout_d.rearrange("(c p) o -> p c o", p=128))
                dd(out=bout_sb[:], in_=bout_d[:, :])

            @p.step("vector", writes=[r_wo])
            def _(e):
                return e.memset(ones_sb[:], 1.0)

            for qb in range(NQB):
                for h in range(16):
                    hp0 = (h % 2) * 64
                    hc = h // 2
                    @p.step("tensor", reads=[r_qT, r_kT], writes=[r_scp])
                    def _(e, h=h, qb=qb, hp0=hp0, hc=hc):
                        return e.matmul(scp[:],
                                        qT[hp0:hp0 + 64, hc, (qb + 1) * 128:(qb + 2) * 128],
                                        kT[hp0:hp0 + 64, hc, qb * 128:qb * 128 + 384],
                                        start=True, stop=True)
                    @p.step("vector", reads=[r_scp], writes=[R['nsm']])
                    def _(e):
                        ch = p.chain(e)
                        ch(e.reduce_max(st['mx'], scp[:], AX.X))
                        return e.tensor_scalar_mul(st['bias'], st['mx'], -0.125)
                    @p.step("scalar", reads=[r_scp, R['nsm']], writes=[r_pb])
                    def _(e):
                        return e.activation(out=pb[:], in_=scp[:], func=AF.Exp,
                                            bias=st['bias'], scale=0.125)
                    @p.step("vector", reads=[r_pb, r_msk], writes=[r_pb, R['nsm']])
                    def _(e, qb=qb):
                        ch = p.chain(e)
                        e.tensor_tensor(pb[:], pb[:], msk_sb[:, qb, :], ALU.mult)
                        ch(e.reduce_sum(st['rs'], pb[:], AX.X))
                        ch(e.reciprocal(st['rr'], st['rs']))
                        return e.tensor_scalar_mul(pb[:], pb[:], st['rr'])
                    transpose_blocks(r_pb, (lambda j: pb[:, j * 128:(j + 1) * 128]), 3,
                                     r_pT, (lambda j: pT[:, j * 128:(j + 1) * 128]))
                    @p.step("tensor", reads=[r_pT, r_vbuf], writes=[r_ops])
                    def _(e, h=h, qb=qb):
                        for j in range(3):
                            last = e.matmul(ops_[:], vbuf[:, qb + j, h * 64:(h + 1) * 64],
                                            pT[:, j * 128:(j + 1) * 128],
                                            start=(j == 0), stop=(j == 2))
                        return last
                    @p.step("vector", reads=[r_ops], writes=[r_aTs])
                    def _(e, hp0=hp0, hc=hc):
                        return e.tensor_copy(aTs[hp0:hp0 + 64, hc, :], ops_[:])
                @p.step("gpsimd", reads=[r_aTs], writes=[r_aTh[qb]], dma=True)
                def _(e, dd, qb=qb):
                    dd(out=aT_h[:, :, qb * 128:(qb + 1) * 128].rearrange("c p q -> p c q"),
                       in_=aTs[:])
            for t in range(NQB):
                bi = t % 2
                @p.step("sync", reads=[r_aTh[t]], writes=[R['xt'][bi], r_aTin[bi]], dma=True)
                def _(e, dd, t=t, bi=bi):
                    dd(out=xt[:, bi, :1024], in_=xh_d[x0 + 128 + t * 128: x0 + 128 + (t + 1) * 128, :])
                    dd(out=aTin[:, bi, :, :],
                       in_=aT_h[:, :, t * 128:(t + 1) * 128].rearrange("c p q -> p c q"))
                for oi in range(2):
                    mi = oi % 2
                    @p.step("tensor", reads=[r_aTin[bi], r_wo], writes=[R['mm'][mi]])
                    def _(e, oi=oi, mi=mi, bi=bi):
                        for d in range(8):
                            e.matmul(mm[mi][:], aTin[:, bi, d, :],
                                     wo_sb[:, d, oi * 512:(oi + 1) * 512],
                                     start=(d == 0), stop=False)
                        return e.matmul(mm[mi][:], ones_sb[:, 0:128],
                                        bout_sb[:, oi * 512:(oi + 1) * 512],
                                        start=False, stop=True)
                    @p.step("vector", reads=[R['mm'][mi], R['xt'][bi]], writes=[R['xo'][bi]])
                    def _(e, oi=oi, mi=mi, bi=bi):
                        return e.tensor_tensor(xo[:, bi, oi * 512:(oi + 1) * 512], mm[mi][:],
                                               xt[:, bi, oi * 512:(oi + 1) * 512], ALU.add)
                @p.step("gpsimd", reads=[R['xo'][bi]], writes=[R['x1'][sub * 8 + t]], dma=True)
                def _(e, dd, t=t, bi=bi):
                    dd(out=x1_h[sub * 1024 + t * 128: sub * 1024 + (t + 1) * 128, :],
                       in_=xo[:, bi, :])
            p.fence()
            es3.close()
            es1.close()

        R['x1'] = [Res() for _ in range(32)]
        for _sub in range(4):
            emit_l0attn(_sub)

        def emit_l0ffn():
            NT32 = HTOK // 128
            NI = 22
            r_hqh = [Res() for _ in range(NT32)]
            R['x2'] = [Res() for _ in range(NT32)]
            es1 = ExitStack()
            sb1 = lambda *a: es1.enter_context(nc.sbuf_tensor(*a))
            wgT = sb1("wgT", [128, 8, 2816], BF16)
            wuT = sb1("wuT", [128, 8, 2816], BF16)
            xqTt = sb1("xqTt", [128, 2, 1024], BF16)
            hqTt = sb1("hqTt", [128, 2816], BF16)
            hbuf = sb1("hbuf", [128, 2816], F32)
            gs = sb1("gs", [128, 512], BF16)
            gps = es1.enter_context(nc.psum_tensor("gps", [128, 512], F32))
            ups = es1.enter_context(nc.psum_tensor("ups", [128, 512], F32))
            r_wgT, r_wuT = Res(), Res()
            r_xqTt = [Res(), Res()]
            r_hqt, r_hbuf, r_gs = Res(), Res(), Res()
            r_gps, r_ups = Res(), Res()

            for wdram, wT, wres in ((wg_d, wgT, r_wgT), (wu_d, wuT, r_wuT)):
                for i in range(NI):
                    bi = i % 2
                    @p.step("sync", writes=[R['xt'][bi]], dma=True)
                    def _(e, dd, i=i, bi=bi, wdram=wdram):
                        dd(out=xt[:, bi, :1024], in_=wdram[i * 128:(i + 1) * 128, :])
                    quant_chain(R['xt'][bi], xt[:, bi, :1024], R['qout'][bi],
                                qout[:, bi, :1024], 64)
                    transpose_blocks(R['qout'][bi],
                                     (lambda d, bi=bi: qout[:, bi, d * 128:(d + 1) * 128]),
                                     8, wres,
                                     (lambda d, i=i, wT=wT: wT[:, d, i * 128:(i + 1) * 128]))

            otiles = [(0, 512), (512, 512), (1024, 512), (1536, 512), (2048, 512), (2560, 256)]
            for t in range(NT32):
                bi = t % 2
                @p.step("sync", reads=[R['x1'][t]], writes=[R['xt'][bi]], dma=True)
                def _(e, dd, t=t, bi=bi):
                    dd(out=xt[:, bi, :1024], in_=x1_h[t * 128:(t + 1) * 128, :])
                ln_tile(R['xt'][bi], xt[:, bi, :1024], R['nbf'][bi], nbf[:, bi, :])
                quant_chain(R['nbf'][bi], nbf[:, bi, :], R['qout'][bi], qout[:, bi, :1024], 64)
                transpose_blocks(R['qout'][bi],
                                 (lambda d, bi=bi: qout[:, bi, d * 128:(d + 1) * 128]),
                                 8, r_xqTt[bi],
                                 (lambda d, bi=bi: xqTt[:, bi, d * 128:(d + 1) * 128]))
                for (o0, ow) in otiles:
                    @p.step("tensor", reads=[r_xqTt[bi], r_wgT, r_wuT], writes=[r_gps, r_ups])
                    def _(e, bi=bi, o0=o0, ow=ow):
                        for d in range(8):
                            e.matmul(gps[:, :ow], xqTt[:, bi, d * 128:(d + 1) * 128],
                                     wgT[:, d, o0:o0 + ow], start=(d == 0), stop=(d == 7))
                        for d in range(8):
                            last = e.matmul(ups[:, :ow], xqTt[:, bi, d * 128:(d + 1) * 128],
                                            wuT[:, d, o0:o0 + ow], start=(d == 0), stop=(d == 7))
                        return last
                    @p.step("scalar", reads=[r_gps], writes=[r_gs])
                    def _(e, ow=ow):
                        return e.activation(out=gs[:, :ow], in_=gps[:, :ow], func=AF.Silu)
                    @p.step("vector", reads=[r_gs, r_ups], writes=[r_hbuf])
                    def _(e, o0=o0, ow=ow):
                        return e.tensor_tensor(hbuf[:, o0:o0 + ow], ups[:, :ow],
                                               gs[:, :ow], ALU.mult)
                for hf in range(2):
                    h0 = hf * 1408
                    quant_chain(r_hbuf, hbuf[:, h0:h0 + 1408], R['qout'][hf],
                                qout[:, hf, :1408], 88)
                    transpose_blocks(R['qout'][hf],
                                     (lambda d, hf=hf: qout[:, hf, d * 128:(d + 1) * 128]),
                                     11, r_hqt,
                                     (lambda d, h0=h0: hqTt[:, h0 + d * 128: h0 + (d + 1) * 128]))
                @p.step("gpsimd", reads=[r_hqt], writes=[r_hqh[t]], dma=True)
                def _(e, dd, t=t):
                    dd(out=hq_h[t], in_=hqTt[:, :])
            p.fence()
            es1.close()
            es2 = ExitStack()
            wdT = es2.enter_context(nc.sbuf_tensor("wdT", [128, NI, 1024], BF16))
            hqin = es2.enter_context(nc.sbuf_tensor("hqin", [128, 2, 2816], BF16))
            dps = es2.enter_context(nc.psum_tensor("dps", [128, 2, 512], F32))
            r_wdT = Res()
            r_hqin = [Res(), Res()]
            r_dps = [Res(), Res()]
            for i in range(8):
                for hf in range(2):
                    h0 = hf * 1408
                    bi = (2 * i + hf) % 2
                    @p.step("sync", writes=[R['xt'][bi]], dma=True)
                    def _(e, dd, i=i, h0=h0, bi=bi):
                        dd(out=xt[:, bi, :1408], in_=wd_d[i * 128:(i + 1) * 128, h0:h0 + 1408])
                    quant_chain(R['xt'][bi], xt[:, bi, :1408], R['qout'][bi],
                                qout[:, bi, :1408], 88)
                    transpose_blocks(R['qout'][bi],
                                     (lambda d, bi=bi: qout[:, bi, d * 128:(d + 1) * 128]),
                                     11, r_wdT,
                                     (lambda d, i=i, h0=h0: wdT[:, h0 // 128 + d, i * 128:(i + 1) * 128]))
            for t in range(NT32):
                bi = t % 2
                @p.step("sync", reads=[r_hqh[t], R['x1'][t]], writes=[r_hqin[bi], R['xt'][bi]], dma=True)
                def _(e, dd, t=t, bi=bi):
                    dd(out=hqin[:, bi, :], in_=hq_h[t])
                    dd(out=xt[:, bi, :1024], in_=x1_h[t * 128:(t + 1) * 128, :])
                for oi, (o0, ow) in enumerate([(0, 512), (512, 512)]):
                    @p.step("tensor", reads=[r_hqin[bi], r_wdT], writes=[r_dps[oi]])
                    def _(e, bi=bi, o0=o0, ow=ow, oi=oi):
                        for i in range(NI):
                            last = e.matmul(dps[:, oi, :ow], hqin[:, bi, i * 128:(i + 1) * 128],
                                            wdT[:, i, o0:o0 + ow], start=(i == 0), stop=(i == NI - 1))
                        return last
                    @p.step("vector", reads=[r_dps[oi], R['xt'][bi]], writes=[R['xo'][bi]])
                    def _(e, bi=bi, o0=o0, ow=ow, oi=oi):
                        return e.tensor_tensor(xo[:, bi, o0:o0 + ow], dps[:, oi, :ow],
                                               xt[:, bi, o0:o0 + ow], ALU.add)
                @p.step("gpsimd", reads=[R['xo'][bi]], writes=[R['x2'][t]], dma=True)
                def _(e, dd, t=t, bi=bi):
                    dd(out=x2_h[t * 128:(t + 1) * 128, :], in_=xo[:, bi, :])
            p.fence()
            es2.close()

        if _DBG_STAGE == 1:
            dbg_x1 = dram("dbg_x1", [HTOK, D], F32, kind="ExternalOutput")
            for t in range(HTOK // 128):
                bi = t % 2
                @p.step("sync", reads=[R['x1'][t]], writes=[R['xt'][bi]], dma=True)
                def _(e, dd, t=t, bi=bi):
                    dd(out=xt[:, bi, :1024], in_=x1_h[t * 128:(t + 1) * 128, :])
                @p.step("gpsimd", reads=[R['xt'][bi]], dma=True)
                def _(e, dd, t=t, bi=bi):
                    dd(out=dbg_x1[t * 128:(t + 1) * 128, :], in_=xt[:, bi, :1024])
            p.drain()
            es.close()
            return nc

        emit_l0ffn()

        if _DBG_STAGE == 2:
            dbg_x2 = dram("dbg_x2", [HTOK, D], F32, kind="ExternalOutput")
            for t in range(HTOK // 128):
                bi = t % 2
                @p.step("sync", reads=[R['x2'][t]], writes=[R['xt'][bi]], dma=True)
                def _(e, dd, t=t, bi=bi):
                    dd(out=xt[:, bi, :1024], in_=x2_h[t * 128:(t + 1) * 128, :])
                @p.step("gpsimd", reads=[R['xt'][bi]], dma=True)
                def _(e, dd, t=t, bi=bi):
                    dd(out=dbg_x2[t * 128:(t + 1) * 128, :], in_=xt[:, bi, :1024])
            p.drain()
            es.close()
            return nc

        def emit_l1attn():
            NT32 = HTOK // 128
            es1 = ExitStack()
            sb1 = lambda *a: es1.enter_context(nc.sbuf_tensor(*a))
            wln = sb1("wln1", [128, 3, 8, 1024], BF16)
            wT = sb1("wT1", [128, 3, 8, 1024], BF16)
            gb = sb1("gb_sb", [128, 4 * 64], F32)
            qtm = sb1("qtm", [128, 2, D], BF16)
            ktm = sb1("ktm", [128, 2, D], BF16)
            vtm = sb1("vtm", [128, 2, D], BF16)
            sca = sb1("sca1", [128, 2048], F32)
            hst = sb1("hst", [128, 8, 16], F32)
            sT = sb1("sT", [128, 8, 128], BF16)
            Ttot = sb1("Ttot", [64, 16, 64], F32)
            r_wln, r_wT, r_gb = Res(), Res(), Res()
            r_qtm = [Res(), Res()]
            r_ktm = [Res(), Res()]
            r_vtm = [Res(), Res()]
            r_sca, r_hst, r_sT, r_Ttot = Res(), Res(), Res(), Res()
            r_qkv_h = [Res() for _ in range(NT32)]
            st3 = dict(st, sq=sca)
            hs = dict(s1=hst[:, 0, :], s2=hst[:, 1, :], mu=hst[:, 2, :], m2=hst[:, 3, :],
                      t=hst[:, 4, :], var=hst[:, 5, :], vpe=hst[:, 6, :], std=hst[:, 7, :])

            @p.step("sync", writes=[r_gb], dma=True)
            def _(e, dd):
                dd(out=gb[:], in_=gb_d[:, :])

            @p.step("gpsimd", writes=[r_wln], dma=True)
            def _(e, dd):
                dd(out=wln[:, 0], in_=wq1_d.rearrange("(c p) o -> p c o", p=128))
                dd(out=wln[:, 1], in_=wk1_d.rearrange("(c p) o -> p c o", p=128))
                dd(out=wln[:, 2], in_=wv1_d.rearrange("(c p) o -> p c o", p=128))

            for w in range(3):
                for oc in range(8):
                    transpose_blocks(r_wln, (lambda ic, w=w, oc=oc: wln[:, w, oc, ic * 128:(ic + 1) * 128]),
                                     8, r_wT,
                                     (lambda ic, w=w, oc=oc: wT[:, w, ic, oc * 128:(oc + 1) * 128]))

            @p.step("vector", writes=[r_Ttot])
            def _(e):
                return e.memset(Ttot[:], 0.0)

            def headln(e, src_ap, g0):
                ch = p.chain(e)
                x3 = src_ap.rearrange("p (h x) -> p h x", x=64)
                e.tensor_reduce(hs['s1'], x3, AX.X, ALU.add)
                e.tensor_tensor(sca[:, 0:1024], src_ap, src_ap, ALU.mult)
                ch(e.tensor_reduce(hs['s2'], sca[:, 0:1024].rearrange("p (h x) -> p h x", x=64),
                                   AX.X, ALU.add))
                e.tensor_scalar_mul(hs['mu'], hs['s1'], 1.0 / 64)
                ch(e.tensor_scalar_mul(hs['m2'], hs['s2'], 1.0 / 64))
                ch(e.tensor_tensor(hs['t'], hs['mu'], hs['mu'], ALU.mult))
                ch(e.tensor_tensor(hs['var'], hs['m2'], hs['t'], ALU.subtract))
                return e.tensor_scalar_add(hs['vpe'], hs['var'], 1e-5)

            def headln_apply(e, src_ap, g0):
                ch = p.chain(e)
                ch(e.reciprocal(hs['s1'], hs['std']))
                x3 = src_ap.rearrange("p (h x) -> p h x", x=64)
                a3 = sca[:, 0:1024].rearrange("p (h x) -> p h x", x=64)
                mu_b = hs['mu'].unsqueeze(2).broadcast_to([128, 16, 64])
                rs_b = hs['s1'].unsqueeze(2).broadcast_to([128, 16, 64])
                gam = gb[:, g0 * 64:(g0 + 1) * 64].unsqueeze(1).broadcast_to([128, 16, 64])
                bet = gb[:, (g0 + 1) * 64:(g0 + 2) * 64].unsqueeze(1).broadcast_to([128, 16, 64])
                e.tensor_tensor(a3, x3, mu_b, ALU.subtract)
                e.tensor_tensor(a3, a3, rs_b, ALU.mult)
                e.tensor_tensor(a3, a3, gam, ALU.mult)
                return e.tensor_tensor(x3, a3, bet, ALU.add)

            for t in range(NT32):
                bi = t % 2
                @p.step("sync", reads=[R['x2'][t]], writes=[R['xt'][bi]], dma=True)
                def _(e, dd, t=t, bi=bi):
                    dd(out=xt[:, bi, :1024], in_=x2_h[t * 128:(t + 1) * 128, :])
                @p.step("vector", reads=[R['xt'][bi]], writes=[R['nsm'], r_sca])
                def _(e, bi=bi):
                    return emit_ln_stats(e, xt[:, bi, :1024], st3, 1024, p.chain(e))
                @p.step("scalar", reads=[R['nsm']], writes=[R['nsm']])
                def _(e):
                    return e.activation(out=st['std'], in_=st['vpe'], func=AF.Sqrt)
                @p.step("vector", reads=[R['xt'][bi], R['nsm']], writes=[R['nbf'][bi], R['nsm']])
                def _(e, bi=bi):
                    p.chain(e)(e.reciprocal(st['rstd'], st['std']))
                    return e.tensor_scalar(nbf[:, bi, :], xt[:, bi, :1024],
                                           st['mu'], st['rstd'], ALU.subtract, ALU.mult)
                transpose_blocks(R['nbf'][bi], (lambda d, bi=bi: nbf[:, bi, d * 128:(d + 1) * 128]),
                                 8, R['ntS'][bi], (lambda d, bi=bi: ntS[:, bi, d * 128:(d + 1) * 128]))
                for (w, dstb, dres) in ((0, qtm, r_qtm[bi]), (1, ktm, r_ktm[bi]), (2, vtm, r_vtm[bi])):
                    for oi in range(2):
                        mi = oi % 2
                        @p.step("tensor", reads=[R['ntS'][bi], r_wT], writes=[R['mm'][mi]])
                        def _(e, bi=bi, w=w, oi=oi, mi=mi):
                            for d in range(8):
                                last = e.matmul(mm[mi][:], ntS[:, bi, d * 128:(d + 1) * 128],
                                                wT[:, w, d, oi * 512:(oi + 1) * 512],
                                                start=(d == 0), stop=(d == 7))
                            return last
                        @p.step("vector", reads=[R['mm'][mi]], writes=[dres])
                        def _(e, bi=bi, oi=oi, mi=mi, dstb=dstb):
                            return e.tensor_copy(dstb[:, bi, oi * 512:(oi + 1) * 512], mm[mi][:])
                for (buf, rres, g0) in ((qtm, r_qtm[bi], 0), (ktm, r_ktm[bi], 2)):
                    @p.step("vector", reads=[rres], writes=[r_hst, r_sca])
                    def _(e, bi=bi, buf=buf, g0=g0):
                        return headln(e, buf[:, bi, :], g0)
                    @p.step("scalar", reads=[r_hst], writes=[r_hst])
                    def _(e):
                        return e.activation(out=hs['std'], in_=hs['vpe'], func=AF.Sqrt)
                    @p.step("vector", reads=[rres, r_hst, r_gb], writes=[rres, r_hst, r_sca])
                    def _(e, bi=bi, buf=buf, g0=g0):
                        return headln_apply(e, buf[:, bi, :], g0)
                @p.step("gpsimd", reads=[r_ktm[bi], r_vtm[bi]], writes=[r_qkv_h[t]], dma=True)
                def _(e, dd, t=t, bi=bi):
                    dd(out=ktm_h[t * 128:(t + 1) * 128, :], in_=ktm[:, bi, :])
                    dd(out=vtm_h[t * 128:(t + 1) * 128, :], in_=vtm[:, bi, :])
                for (buf, rres, dram2) in ((qtm, r_qtm[bi], qT1_h), (ktm, r_ktm[bi], kT1_h)):
                    transpose_blocks(rres, (lambda d, bi=bi, buf=buf: buf[:, bi, d * 128:(d + 1) * 128]),
                                     8, r_sT, (lambda d: sT[:, d, :]))
                    @p.step("gpsimd", reads=[r_sT], writes=[r_qkv_h[t]], dma=True)
                    def _(e, dd, t=t, dram2=dram2):
                        dd(out=dram2[:, :, t * 128:(t + 1) * 128].rearrange("c p q -> p c q"),
                           in_=sT[:])
                for h4 in range(4):
                    mi = h4 % 2
                    @p.step("tensor", reads=[r_ktm[bi], r_vtm[bi]], writes=[R['mm'][mi]])
                    def _(e, bi=bi, h4=h4, mi=mi):
                        for hh in range(4):
                            h = h4 * 4 + hh
                            last = e.matmul(mm[mi][0:64, hh * 64:(hh + 1) * 64],
                                            ktm[:, bi, h * 64:(h + 1) * 64],
                                            vtm[:, bi, h * 64:(h + 1) * 64],
                                            start=True, stop=True)
                        return last
                    @p.step("vector", reads=[R['mm'][mi]], writes=[r_Ttot])
                    def _(e, h4=h4, mi=mi):
                        t4 = Ttot[:, h4 * 4:(h4 + 1) * 4, :].rearrange("p h x -> p (h x)")
                        return e.tensor_tensor(t4, t4, mm[mi][0:64, 0:256], ALU.add)
            @p.step("vector", reads=[r_Ttot, R['const']], writes=[r_Ttot])
            def _(e):
                t2d = Ttot[:].rearrange("p h x -> p (h x)")
                p.chain(e)(e.tensor_scalar(nsm[0:64, 14:15], sidec[0:64, 0:1], -1.0, 1.0,
                                           ALU.mult, ALU.add))
                return e.tensor_scalar_mul(t2d, t2d, nsm[0:64, 14:15])
            @p.step("gpsimd", reads=[r_Ttot], writes=[R['const']], dma=True, fixed_inc=16)
            def _(e, dd):
                for arm in nc.switch_core_id(engines=[e], n=128):
                    slot = int(arm) % 2
                    dd(out=kvst_s[slot], in_=Ttot[:].rearrange("p h x -> p (h x)"))
            @p.step("gpsimd", reads=[R['const']], writes=[R['const']])
            def _(e):
                emit_flag_store(e, 0)
                emit_flag_poll(e, 0, both=True)
                return e.nop()
            Sini = sb1("Sini", [64, 16, 64], F32)
            Sini2 = sb1("Sini2", [64, 16, 64], F32)
            r_Sini = Res()
            @p.step("sync", reads=[R['const']], writes=[r_Sini], dma=True)
            def _(e, dd):
                dd(out=Sini[:].rearrange("p h x -> p (h x)"), in_=kvst_s[0])
                dd(out=Sini2[:].rearrange("p h x -> p (h x)"), in_=kvst_s[1])
            @p.step("vector", reads=[r_Sini, R['const']], writes=[r_Sini])
            def _(e):
                s2d = Sini[:].rearrange("p h x -> p (h x)")
                s2b = Sini2[:].rearrange("p h x -> p (h x)")
                p.chain(e)(e.tensor_tensor(s2d, s2d, s2b, ALU.add))
                return e.tensor_scalar_mul(s2d, s2d, sidec[0:64, 0:1])
            p.fence()
            es1.close()

            es2 = ExitStack()
            sb2 = lambda *a: es2.enter_context(nc.sbuf_tensor(*a))
            qTs = sb2("qTs1", [64, 2, HTOK], BF16)
            kTs = sb2("kTs1", [64, 2, HTOK], BF16)
            ktms = sb2("ktms", [128, 2, NT32, 64], BF16)
            vtms = sb2("vtms", [128, 2, NT32, 64], BF16)
            Sf = sb2("Sf", [64, 64], F32)
            Sb = sb2("Sb", [64, 64], BF16)
            asb = sb2("asb", [128, 2, 128], BF16)
            ost = sb2("ost", [64, 2, 128], BF16)
            aps_ = es2.enter_context(nc.psum_tensor("apsS", [128, 2, 128], F32))
            ops2 = es2.enter_context(nc.psum_tensor("opsS", [64, 2, 128], F32))
            tps2 = es2.enter_context(nc.psum_tensor("tpsS", [64, 2, 64], F32))
            r_hin = [Res(), Res()]
            r_S = Res()
            r_asb = [Res(), Res()]
            r_ost = [Res(), Res()]
            r_aps = [Res(), Res()]
            r_ops = [Res(), Res()]
            r_tps = [Res(), Res()]
            r_oT_h = [Res() for _ in range(16)]
            for h in range(16):
                bj = h % 2
                @p.step("sync", reads=[r_Sini] + r_qkv_h, writes=[r_hin[bj]], dma=True)
                def _(e, dd, h=h, bj=bj):
                    dd(out=qTs[:, bj, :], in_=qT1_h[h // 2, (h % 2) * 64:(h % 2) * 64 + 64, :])
                    dd(out=kTs[:, bj, :], in_=kT1_h[h // 2, (h % 2) * 64:(h % 2) * 64 + 64, :])
                    dd(out=ktms[:, bj], in_=ktm_h[:, h * 64:(h + 1) * 64].rearrange("(c p) x -> p c x", p=128))
                    dd(out=vtms[:, bj], in_=vtm_h[:, h * 64:(h + 1) * 64].rearrange("(c p) x -> p c x", p=128))
                @p.step("vector", reads=[r_Sini], writes=[r_S])
                def _(e, h=h):
                    e.tensor_copy(Sf[:], Sini[:, h, :])
                    return e.tensor_copy(Sb[:], Sini[:, h, :])
                for cch in range(NT32):
                    ci = cch % 2
                    @p.step("tensor", reads=[r_hin[bj]], writes=[r_aps[ci]])
                    def _(e, bj=bj, cch=cch, ci=ci):
                        return e.matmul(aps_[:, ci, :], kTs[:, bj, cch * 128:(cch + 1) * 128],
                                        qTs[:, bj, cch * 128:(cch + 1) * 128], start=True, stop=True)
                    @p.step("vector", reads=[r_aps[ci], R['const']], writes=[r_asb[ci]])
                    def _(e, ci=ci):
                        return e.tensor_tensor(asb[:, ci, :], aps_[:, ci, :], tri[:], ALU.mult)
                    @p.step("tensor", reads=[r_asb[ci], r_hin[bj], r_S], writes=[r_ops[ci]])
                    def _(e, bj=bj, cch=cch, ci=ci):
                        e.matmul(ops2[:, ci, :], vtms[:, bj, cch, :], asb[:, ci, :],
                                 start=True, stop=False)
                        return e.matmul(ops2[:, ci, :], Sb[:, :], qTs[:, bj, cch * 128:(cch + 1) * 128],
                                        start=False, stop=True)
                    @p.step("tensor", reads=[r_hin[bj]], writes=[r_tps[ci]])
                    def _(e, bj=bj, cch=cch, ci=ci):
                        return e.matmul(tps2[:, ci, :], ktms[:, bj, cch, :], vtms[:, bj, cch, :],
                                        start=True, stop=True)
                    @p.step("vector", reads=[r_ops[ci]], writes=[r_ost[ci]])
                    def _(e, ci=ci):
                        return e.tensor_copy(ost[:, ci, :], ops2[:, ci, :])
                    @p.step("gpsimd", reads=[r_ost[ci]], writes=[r_oT_h[h]], dma=True)
                    def _(e, dd, h=h, cch=cch, ci=ci):
                        dd(out=oT_h[h // 2, (h % 2) * 64:(h % 2) * 64 + 64,
                                    cch * 128:(cch + 1) * 128], in_=ost[:, ci, :])
                    @p.step("vector", reads=[r_tps[ci], r_S], writes=[r_S])
                    def _(e, ci=ci):
                        ch = p.chain(e)
                        ch(e.tensor_tensor(Sf[:], Sf[:], tps2[:, ci, :], ALU.add))
                        return e.tensor_copy(Sb[:], Sf[:])
            p.fence()
            es2.close()

            es3 = ExitStack()
            wln2 = es3.enter_context(nc.sbuf_tensor("wlnP", [128, 8, 1024], BF16))
            wTo = es3.enter_context(nc.sbuf_tensor("wTP", [128, 8, 1024], BF16))
            oTs2 = es3.enter_context(nc.sbuf_tensor("oTin", [128, 2, 8, 128], BF16))
            r_wln2, r_wTo = Res(), Res()
            r_oTin = [Res(), Res()]
            R['x3'] = [Res() for _ in range(NT32)]

            @p.step("gpsimd", writes=[r_wln2], dma=True)
            def _(e, dd):
                dd(out=wln2[:], in_=wo1_d.rearrange("(c p) o -> p c o", p=128))
            for oc in range(8):
                transpose_blocks(r_wln2, (lambda ic, oc=oc: wln2[:, oc, ic * 128:(ic + 1) * 128]),
                                 8, r_wTo, (lambda ic, oc=oc: wTo[:, ic, oc * 128:(oc + 1) * 128]))
            for t in range(NT32):
                bi = t % 2
                @p.step("sync", reads=[R['x2'][t]] + r_oT_h, writes=[R['xt'][bi], r_oTin[bi]], dma=True)
                def _(e, dd, t=t, bi=bi):
                    dd(out=xt[:, bi, :1024], in_=x2_h[t * 128:(t + 1) * 128, :])
                    dd(out=oTs2[:, bi], in_=oT_h[:, :, t * 128:(t + 1) * 128].rearrange("c p q -> p c q"))
                for oi in range(2):
                    mi = oi % 2
                    @p.step("tensor", reads=[r_oTin[bi], r_wTo], writes=[R['mm'][mi]])
                    def _(e, t=t, oi=oi, mi=mi, bi=bi):
                        for d in range(8):
                            last = e.matmul(mm[mi][:], oTs2[:, bi, d, :],
                                            wTo[:, d, oi * 512:(oi + 1) * 512],
                                            start=(d == 0), stop=(d == 7))
                        return last
                    @p.step("vector", reads=[R['mm'][mi], R['xt'][bi]], writes=[R['xo'][bi]])
                    def _(e, oi=oi, mi=mi, bi=bi):
                        return e.tensor_tensor(xo[:, bi, oi * 512:(oi + 1) * 512], mm[mi][:],
                                               xt[:, bi, oi * 512:(oi + 1) * 512], ALU.add)
                @p.step("gpsimd", reads=[R['xo'][bi]], writes=[R['x3'][t]], dma=True)
                def _(e, dd, t=t, bi=bi):
                    dd(out=x3_h[t * 128:(t + 1) * 128, :], in_=xo[:, bi, :])
            p.fence()
            es3.close()

        emit_l1attn()


        if _DBG_STAGE == 3:
            dbg_x3 = dram("dbg_x3", [HTOK, D], F32, kind="ExternalOutput")
            for t in range(HTOK // 128):
                bi = t % 2
                @p.step("sync", reads=[R['x3'][t]], writes=[R['xt'][bi]], dma=True)
                def _(e, dd, t=t, bi=bi):
                    dd(out=xt[:, bi, :1024], in_=x3_h[t * 128:(t + 1) * 128, :])
                @p.step("gpsimd", reads=[R['xt'][bi]], dma=True)
                def _(e, dd, t=t, bi=bi):
                    dd(out=dbg_x3[t * 128:(t + 1) * 128, :], in_=xt[:, bi, :1024])
            p.drain()
            es.close()
            return nc

        kT2_h = dram("kT2_h", [8, 128, HTOK], BF16, kind="Internal")
        Vloc_h = dram("Vloc_h", [HTOK, 8, 132], BF16, kind="Internal")

        def emit_l1ffn_pre():
            NT32 = HTOK // 128
            es1 = ExitStack()
            sb1 = lambda *a: es1.enter_context(nc.sbuf_tensor(*a))
            nT = sb1("nT2", [128, 8, HTOK], BF16)
            selT = sb1("selT", [128, 8, 128], BF16)
            selw = sb1("selw_sb", [128, 1024], BF16)
            probs = sb1("probs", [128, NT32, 128], F32)
            gates = sb1("gates", [128, NT32, 16, 8], F32)
            qst = sb1("qst", [128, 2, 512], BF16)
            knr = sb1("knr", [1, 16], F32)
            lgp = es1.enter_context(nc.psum_tensor("lgp", [128, 128], F32))
            nrp2 = es1.enter_context(nc.psum_tensor("nrp2", [1, 512], F32))
            r_nT, r_selT, r_selw = Res(), Res(), Res()
            r_probs, r_gates, r_knr = Res(), Res(), Res()
            r_qst = [Res(), Res()]
            r_lgp, r_nrp = Res(), Res()
            R['kT2'] = Res()
            R['qT2'] = Res()
            R['Vloc'] = Res()
            R['goh'] = Res()
            R['gates'] = r_gates

            @p.step("gpsimd", writes=[r_selw], dma=True)
            def _(e, dd):
                dd(out=selw[:], in_=selw_d[:, :])
            transpose_blocks(r_selw, (lambda ic: selw[:, ic * 128:(ic + 1) * 128]), 8,
                             r_selT, (lambda ic: selT[:, ic, :]))

            for t in range(NT32):
                bi = t % 2
                @p.step("sync", reads=[R['x3'][t]], writes=[R['xt'][bi]], dma=True)
                def _(e, dd, t=t, bi=bi):
                    dd(out=xt[:, bi, :1024], in_=x3_h[t * 128:(t + 1) * 128, :])
                ln_tile(R['xt'][bi], xt[:, bi, :1024], R['nbf'][bi], nbf[:, bi, :])
                transpose_blocks(R['nbf'][bi], (lambda d, bi=bi: nbf[:, bi, d * 128:(d + 1) * 128]),
                                 8, r_nT, (lambda d, t=t: nT[:, d, t * 128:(t + 1) * 128]))
                @p.step("tensor", reads=[r_nT, r_selT], writes=[r_lgp])
                def _(e, t=t):
                    for d in range(8):
                        last = e.matmul(lgp[:], nT[:, d, t * 128:(t + 1) * 128],
                                        selT[:, d, :], start=(d == 0), stop=(d == 7))
                    return last
                @p.step("scalar", reads=[r_lgp], writes=[r_probs])
                def _(e, t=t):
                    return e.activation(out=probs[:, t, :], in_=lgp[:], func=AF.Sigmoid)

            es3 = ExitStack()
            scaG = es3.enter_context(nc.sbuf_tensor("scaG", [128, 4608], F32))
            rank = es3.enter_context(nc.sbuf_tensor("rank", [128, NT32, 128], F32))
            r_scaG, r_rank = Res(), Res()
            NG = NT32 * 16
            @p.step("vector", reads=[r_probs, R['const']], writes=[r_rank, r_gates, r_scaG])
            def _(e):
                ch = p.chain(e)
                p3 = probs[:].rearrange("p t (g e) -> p (t g) e", e=8)
                r3 = rank[:].rearrange("p t (g e) -> p (t g) e", e=8)
                e.memset(rank[:], 0.0)
                t1 = scaG[:, 0:NG * 8].rearrange("p (g e) -> p g e", e=8)
                for dlt in range(1, 8):
                    wdt = 8 - dlt
                    aa = p3[:, :, 0:wdt]
                    bb = p3[:, :, dlt:8]
                    ch(e.tensor_tensor(t1[:, :, 0:wdt], bb, aa, ALU.is_gt))
                    ch(e.tensor_tensor(r3[:, :, 0:wdt], r3[:, :, 0:wdt], t1[:, :, 0:wdt], ALU.add))
                    ch(e.tensor_tensor(t1[:, :, 0:wdt], aa, bb, ALU.is_ge))
                    ch(e.tensor_tensor(r3[:, :, dlt:8], r3[:, :, dlt:8], t1[:, :, 0:wdt], ALU.add))
                ch(e.tensor_single_scalar(r3, r3, 2.0, ALU.is_lt))
                ch(e.tensor_tensor(t1, p3, r3, ALU.mult))
                s_ = scaG[:, NG * 8:NG * 9]
                ch(e.tensor_reduce(s_.rearrange("p (g x) -> p g x", x=1), t1, AX.X, ALU.add))
                ch(e.tensor_scalar_max(s_, s_, 1e-9))
                ch(e.reciprocal(s_, s_))
                ch(e.tensor_scalar_mul(s_, s_, rsc[:, 0:1]))
                sb_ = s_.rearrange("p (g x) -> p g x", x=1).broadcast_to([128, NG, 8])
                g4 = gates[:].rearrange("p t g e -> p (t g) e")
                return e.tensor_tensor(g4, t1, sb_, ALU.mult)
            @p.step("gpsimd", reads=[r_gates], writes=[R['goh']], dma=True)
            def _(e, dd):
                dd(out=go_h.rearrange("(t p) (h e) -> p t h e", p=128, e=8),
                   in_=gates[:, :, 8:16, :])
            p.fence()
            es3.close()

            es2 = ExitStack()
            wqkT = es2.enter_context(nc.sbuf_tensor("wqkT", [128, 2, 8, 1024], BF16))
            wln3 = es2.enter_context(nc.sbuf_tensor("wln3", [128, 2, 1024], BF16))
            sqb2 = es2.enter_context(nc.sbuf_tensor("sqb2", [128, 512], BF16))
            onec2 = es2.enter_context(nc.sbuf_tensor("onec2", [128, 1], BF16))
            r_wqkT, r_sqb = Res(), Res()
            r_wln3 = [Res(), Res()]
            r_kTh = Res()
            r_qTh = Res()
            @p.step("vector", writes=[r_sqb])
            def _(e):
                return e.memset(onec2[:], 1.0)
            for w, wd2 in ((0, qw2_d), (1, kw2_d)):
                for oc in range(8):
                    bo = oc % 2
                    @p.step("gpsimd", writes=[r_wln3[bo]], dma=True)
                    def _(e, dd, wd2=wd2, oc=oc, bo=bo):
                        dd(out=wln3[:, bo, :], in_=wd2[oc * 128:(oc + 1) * 128, :])
                    transpose_blocks(r_wln3[bo], (lambda ic, bo=bo: wln3[:, bo, ic * 128:(ic + 1) * 128]),
                                     8, r_wqkT, (lambda ic, w=w, oc=oc: wqkT[:, w, ic, oc * 128:(oc + 1) * 128]))
            @p.step("vector", writes=[r_knr])
            def _(e):
                return e.memset(knr[:], 0.0)
            for w, dram2, rdr in ((0, qT2_h, R['qT2']), (1, kT2_h, r_kTh)):
                for hp in range(8):
                    for tq in range(8):
                        mi = (hp * 8 + tq) % 2
                        @p.step("tensor", reads=[r_nT, r_wqkT], writes=[R['mm'][mi]])
                        def _(e, w=w, hp=hp, tq=tq, mi=mi):
                            for d in range(8):
                                last = e.matmul(mm[mi][:], wqkT[:, w, d, hp * 128:(hp + 1) * 128],
                                                nT[:, d, tq * 512:(tq + 1) * 512],
                                                start=(d == 0), stop=(d == 7))
                            return last
                        @p.step("vector", reads=[R['mm'][mi]], writes=[r_qst[mi]])
                        def _(e, mi=mi):
                            return e.tensor_copy(qst[:, mi, :], mm[mi][:])
                        @p.step("gpsimd", reads=[r_qst[mi]], writes=[rdr], dma=True)
                        def _(e, dd, hp=hp, tq=tq, mi=mi, dram2=dram2):
                            dd(out=dram2[hp][:, tq * 512:(tq + 1) * 512], in_=qst[:, mi, :])
                        if w == 1:
                            @p.step("scalar", reads=[r_qst[mi]], writes=[r_sqb])
                            def _(e, mi=mi):
                                return e.activation(out=sqb2[:], in_=qst[:, mi, :], func=AF.Square)
                            @p.step("tensor", reads=[r_sqb], writes=[r_nrp])
                            def _(e):
                                return e.matmul(nrp2[:], onec2[:], sqb2[:], start=True, stop=True)
                            @p.step("vector", reads=[r_nrp, r_knr], writes=[r_knr, R['nsm']])
                            def _(e, hp=hp):
                                ch = p.chain(e)
                                ch(e.reduce_max(nsm[0:1, 13:14], nrp2[:], AX.X))
                                return e.tensor_tensor(knr[0:1, hp:hp + 1], knr[0:1, hp:hp + 1],
                                                       nsm[0:1, 13:14], ALU.max)
            es2.close()

            es4 = ExitStack()
            wvh = es4.enter_context(nc.sbuf_tensor("wvh", [128, 2, 8, 1024], BF16))
            Vh = es4.enter_context(nc.sbuf_tensor("Vh", [128, NT32, 132], BF16))
            r_wvh = [Res(), Res()]
            r_Vh = Res()
            for h in range(8):
                bh = h % 2
                @p.step("gpsimd", writes=[r_wvh[bh]], dma=True)
                def _(e, dd, h=h, bh=bh):
                    for l in range(8):
                        dd(out=wvh[:, bh, :, l * 128:(l + 1) * 128],
                           in_=vw_d[8 * h + l].rearrange("(c p) x -> p c x", p=128))
                @p.step("vector", writes=[r_Vh])
                def _(e):
                    return e.memset(Vh[:], 1.0)
                for t in range(NT32):
                    for sl in range(2):
                        mi = sl % 2
                        @p.step("tensor", reads=[r_nT, r_wvh[bh]], writes=[R['mm'][mi]])
                        def _(e, h=h, t=t, sl=sl, mi=mi, bh=bh):
                            for d in range(8):
                                last = e.matmul(mm[mi][:], nT[:, d, t * 128:(t + 1) * 128],
                                                wvh[:, bh, d, sl * 512:(sl + 1) * 512],
                                                start=(d == 0), stop=(d == 7))
                            return last
                        @p.step("vector", reads=[R['mm'][mi], r_gates], writes=[r_Vh])
                        def _(e, h=h, t=t, sl=sl, mi=mi):
                            last = None
                            for l in range(4):
                                ei = sl * 4 + l
                                gcol = gates[:, t, h, ei:ei + 1]
                                dst = Vh[:, t, 0:128]
                                if ei == 0:
                                    last = e.tensor_scalar_mul(dst, mm[mi][:, l * 128:(l + 1) * 128], gcol)
                                else:
                                    last = e.scalar_tensor_tensor(dst, mm[mi][:, l * 128:(l + 1) * 128],
                                                                  gcol, dst, ALU.mult, ALU.add)
                            return last
                @p.step("gpsimd", reads=[r_Vh], writes=[R['Vloc']], dma=True)
                def _(e, dd, h=h):
                    dd(out=Vloc_h[:, h, :].rearrange("(t p) x -> p t x", p=128), in_=Vh[:])
            p.fence()
            es4.close()

            @p.step("gpsimd", reads=[r_kTh, R['Vloc'], r_knr], writes=[R['const']], dma=True, fixed_inc=48)
            def _(e, dd):
                for arm in nc.switch_core_id(engines=[e], n=128):
                    slot = int(arm) % 2
                    dd(out=K_s[slot], in_=kT2_h[:, :, :])
                    dd(out=V_s[slot], in_=Vloc_h[:, :, :])
                    dd(out=kn_s[slot], in_=knr[0:1, :])
            @p.step("gpsimd", reads=[R['const']], writes=[R['const']])
            def _(e):
                emit_flag_store(e, 1)
                emit_flag_poll(e, 1, both=True)
                return e.nop()
            @p.step("gpsimd", reads=[R['goh'], R['qT2']] + R['x3'], writes=[R['const']], dma=True)
            def _(e, dd):
                o_x3 = e.value_load(qoffs[0:1, 0:1])
                o_go = e.value_load(qoffs[0:1, 1:2])
                o_qT = e.value_load(qoffs[0:1, 2:3])
                dd(out=x3q_h[:, :], in_=dataclasses.replace(x3_h[0:QTOK, :], offset=o_x3))
                dd(out=goq_h[:, :], in_=dataclasses.replace(go_h[0:QTOK, :], offset=o_go))
                dd(out=qTq_h[:, :, :], in_=dataclasses.replace(qT2_h[:, :, 0:QTOK], offset=o_qT))
            p.fence()
            es1.close()

        emit_l1ffn_pre()

        SCALE2 = float(1.0 / np.sqrt(128.0))
        esC = ExitStack()
        ctxF = esC.enter_context(nc.sbuf_tensor("ctxF", [128, 8, QTOK], BF16))
        rsF = esC.enter_context(nc.sbuf_tensor("rsF", [8, QTOK], F32))
        esF = ExitStack()
        sbF = lambda *a: esF.enter_context(nc.sbuf_tensor(*a))
        qsF = sbF("qsF", [128, 8, QTOK], BF16)
        ksF = sbF("ksF", [128, T], BF16)
        vsF = sbF("vsF", [128, T // 128, 132], BF16)
        pbt = sbF("pbt", [128, 2, 512], BF16)
        cstF = sbF("cstF", [128, 512], BF16)
        rs1 = sbF("rs1", [1, 512], F32)
        sqbF = sbF("sqbF", [128, 512], BF16)
        nrmF = sbF("nrmF", [1, QTOK], F32)
        smaxF = sbF("smaxF", [1, 16], F32)
        bcolF = sbF("bcolF", [128, 1], F32)
        onecF = sbF("onecF", [128, 1], BF16)
        onerF = sbF("onerF", [1, 128], BF16)
        knin = sbF("knin", [1, 2, 16], F32)
        scpF = esF.enter_context(nc.psum_tensor("scpF", [128, 2, 512], F32))
        ctxpF = esF.enter_context(nc.psum_tensor("ctxpF", [128, 512], F32))
        rspF = esF.enter_context(nc.psum_tensor("rspF", [1, 512], F32))

        r_qsF, r_ctxF, r_rsF, r_knin = Res(), Res(), Res(), Res()
        r_ksF = Res()
        r_pbtF = [Res(), Res()]
        r_cstF, r_rs1 = Res(), Res()
        r_scpF = [Res(), Res()]
        r_ctxpF, r_rspF = Res(), Res()
        nrpF, mbpF = mmA, mmB
        r_nrpF, r_mbpF = R['mm'][0], R['mm'][1]
        r_sqbF, r_nrmF, r_smaxF, r_bcolF = Res(), Res(), Res(), Res()

        @p.step("vector", writes=[r_sqbF])
        def _(e):
            e.memset(onecF[:], 1.0)
            return e.memset(onerF[:], 1.0)

        @p.step("sync", reads=[R['const']], writes=[r_qsF, r_knin], dma=True)
        def _(e, dd):
            dd(out=qsF[:], in_=qTq_h.rearrange("c p q -> p c q"))
            dd(out=knin[:], in_=kn_s[:, :, :].rearrange("s o x -> o s x"))

        for h in range(8):
            for sl in range(4):
                @p.step("scalar", reads=[r_qsF], writes=[r_sqbF])
                def _(e, h=h, sl=sl):
                    return e.activation(out=sqbF[:], in_=qsF[:, h, sl * 512:(sl + 1) * 512],
                                        func=AF.Square)
                @p.step("tensor", reads=[r_sqbF], writes=[r_nrpF])
                def _(e):
                    return e.matmul(nrpF[0:1, :], onecF[:], sqbF[:], start=True, stop=True)
                @p.step("vector", reads=[r_nrpF], writes=[r_nrmF])
                def _(e, sl=sl):
                    return e.tensor_copy(nrmF[0:1, sl * 512:(sl + 1) * 512], nrpF[0:1, :])
            @p.step("vector", reads=[r_nrmF, r_knin], writes=[r_smaxF, R['nsm']])
            def _(e, h=h):
                ch = p.chain(e)
                ch(e.reduce_max(smaxF[0:1, h:h + 1], nrmF[0:1, :], AX.X))
                ch(e.tensor_tensor(nsm[0:1, 13:14], knin[0:1, 0, h:h + 1],
                                   knin[0:1, 1, h:h + 1], ALU.max))
                ch(e.tensor_tensor(smaxF[0:1, h:h + 1], smaxF[0:1, h:h + 1],
                                   nsm[0:1, 13:14], ALU.mult))
                return e.tensor_scalar_mul(smaxF[0:1, h:h + 1], smaxF[0:1, h:h + 1],
                                           SCALE2 * SCALE2)
        @p.step("scalar", reads=[r_smaxF], writes=[r_smaxF])
        def _(e):
            return e.activation(out=smaxF[0:1, 8:16], in_=smaxF[0:1, 0:8], func=AF.Sqrt)

        for h in range(8):
            @p.step("sync", reads=[R['const']], writes=[r_ksF], dma=True)
            def _(e, dd, h=h):
                for slot in range(2):
                    dd(out=ksF[:, slot * HTOK:(slot + 1) * HTOK], in_=K_s[slot, h])
                    dd(out=vsF[:, slot * 32:(slot + 1) * 32, :],
                       in_=V_s[slot, :, h, :].rearrange("(c p) x -> p c x", p=128))
            @p.step("vector", reads=[r_smaxF], writes=[r_sqbF, R['nsm']])
            def _(e, h=h):
                p.chain(e)(e.tensor_scalar_mul(nsm[0:1, 13:14], smaxF[0:1, 8 + h:9 + h], -1.0))
                return e.tensor_copy(sqbF[0:1, 0:1], nsm[0:1, 13:14])
            @p.step("tensor", reads=[r_sqbF], writes=[r_mbpF])
            def _(e):
                return e.matmul(mbpF[:, 0:1], onerF[:, :], sqbF[0:1, 0:1], start=True, stop=True)
            @p.step("vector", reads=[r_mbpF], writes=[r_bcolF])
            def _(e):
                return e.tensor_copy(bcolF[:], mbpF[:, 0:1])
            for qt in range(4):
                for kb in range(64):
                    ci = kb % 2
                    @p.step("tensor", reads=[r_ksF, r_qsF], writes=[r_scpF[ci]])
                    def _(e, h=h, qt=qt, kb=kb, ci=ci):
                        return e.matmul(scpF[:, ci, :], ksF[:, kb * 128:(kb + 1) * 128],
                                        qsF[:, h, qt * 512:(qt + 1) * 512], start=True, stop=True)
                    @p.step("scalar", reads=[r_scpF[ci], r_bcolF], writes=[r_pbtF[ci]])
                    def _(e, ci=ci):
                        return e.activation(out=pbt[:, ci, :], in_=scpF[:, ci, :], func=AF.Exp,
                                            bias=bcolF[:, 0:1], scale=SCALE2)
                    @p.step("tensor", reads=[r_pbtF[ci], r_ksF], writes=[r_ctxpF])
                    def _(e, kb=kb, ci=ci):
                        return e.matmul(ctxpF[:], vsF[:, kb, 0:128], pbt[:, ci, :],
                                        start=(kb == 0), stop=(kb == 63))
                    @p.step("tensor", reads=[r_pbtF[ci], r_ksF], writes=[r_rspF])
                    def _(e, kb=kb, ci=ci):
                        return e.matmul(rspF[:], vsF[:, kb, 128:129], pbt[:, ci, :],
                                        start=(kb == 0), stop=(kb == 63))
                @p.step("vector", reads=[r_ctxpF], writes=[r_ctxF])
                def _(e, h=h, qt=qt):
                    return e.tensor_copy(ctxF[:, h, qt * 512:(qt + 1) * 512], ctxpF[:])
                @p.step("vector", reads=[r_rspF], writes=[r_rs1])
                def _(e):
                    return e.tensor_copy(rs1[:], rspF[:])
                @p.step("gpsimd", reads=[r_rs1], writes=[r_rsF], dma=True)
                def _(e, dd, h=h, qt=qt):
                    dd(out=rsF[h:h + 1, qt * 512:(qt + 1) * 512], in_=rs1[0:1, :])

        p.fence()
        esF.close()
        esD = ExitStack()
        sbD = lambda *a: esD.enter_context(nc.sbuf_tensor(*a))
        repl = sbD("repl_sb", [64, 64 * 128], BF16)
        idf_sb = sbD("idf_sb", [128, 128], F32)
        go4 = sbD("go4", [128, 16, 8, 8], F32)
        rcp = sbD("rcp", [128, 16, 8], F32)
        gob = sbD("gob", [128, 16, 64], BF16)
        goT = sbD("goT", [64, QTOK], BF16)
        cT = sbD("cT", [128, 64, 512], BF16)
        gbs = sbD("gbs", [128, 512], BF16)
        owt = sbD("owt", [128, 2, 512], BF16)
        oTsb = sbD("oTsb", [128, 8, 512], BF16)
        odps = esD.enter_context(nc.psum_tensor("odps", [128, 4, 512], F32))
        gbp, rtp = mmA, mmB
        r_repl, r_go4, r_rcp, r_gob, r_goT = Res(), Res(), Res(), Res(), Res()
        r_cT, r_gbs, r_oTsb = Res(), Res(), Res()
        r_owt = [Res(), Res()]
        r_odps = Res()
        r_gbp, r_rtp = R['mm'][0], R['mm'][1]

        @p.step("sync", reads=[R['const']], writes=[r_repl, r_go4], dma=True)
        def _(e, dd):
            dd(out=repl[:], in_=repl_d[:, :])
            dd(out=idf_sb[:], in_=idf_d[:, :])
            dd(out=go4[:], in_=goq_h.rearrange("(t p) (h e) -> p t h e", p=128, e=8))
        for t16 in range(16):
            ti = t16 % 2
            @p.step("tensor", reads=[r_rsF, R['const']], writes=[r_rtp])
            def _(e, t16=t16):
                return e.transpose(rtp[:, 0:8], rsF[:, t16 * 128:(t16 + 1) * 128], idf_sb[0:8, 0:8])
            @p.step("vector", reads=[r_rtp], writes=[r_rcp])
            def _(e, t16=t16):
                return e.tensor_copy(rcp[:, t16, :], rtp[:, 0:8])
        @p.step("vector", reads=[r_rcp, r_go4], writes=[r_rcp, r_gob])
        def _(e):
            ch = p.chain(e)
            ch(e.reciprocal(rcp[:], rcp[:]))
            rb = rcp[:].unsqueeze(3).broadcast_to([128, 16, 8, 8])
            return e.tensor_tensor(gob[:].rearrange("p t (h e) -> p t h e", e=8),
                                   go4[:], rb, ALU.mult)
        for t16 in range(16):
            ti = t16 % 2
            @p.step("tensor", reads=[r_gob, R['const']], writes=[R['tp'][ti]])
            def _(e, t16=t16, ti=ti):
                return e.transpose(tp[ti][0:64, :], gob[:, t16, :], idb[:])
            @p.step("vector", reads=[R['tp'][ti]], writes=[r_goT])
            def _(e, t16=t16, ti=ti):
                return e.tensor_copy(goT[:, t16 * 128:(t16 + 1) * 128], tp[ti][0:64, :])
        for tsl in range(4):
            for he in range(64):
                h = he // 8
                @p.step("tensor", reads=[r_goT, r_repl], writes=[r_gbp])
                def _(e, he=he, tsl=tsl):
                    return e.matmul(gbp[:, :], repl[:, he * 128:(he + 1) * 128],
                                    goT[:, tsl * 512:(tsl + 1) * 512], start=True, stop=True)
                @p.step("vector", reads=[r_gbp], writes=[r_gbs])
                def _(e):
                    return e.tensor_copy(gbs[:], gbp[:, :])
                @p.step("vector", reads=[r_gbs, r_ctxF], writes=[r_cT])
                def _(e, he=he, h=h, tsl=tsl):
                    return e.tensor_tensor(cT[:, he, :], ctxF[:, h, tsl * 512:(tsl + 1) * 512],
                                           gbs[:], ALU.mult)
            for dh in range(2):
                for he in range(64):
                    bo = he % 2
                    @p.step("gpsimd", writes=[r_owt[bo]], dma=True)
                    def _(e, dd, he=he, dh=dh, bo=bo):
                        dd(out=owt[:, bo], in_=ow_d[he][:, dh * 512:(dh + 1) * 512])
                    @p.step("tensor", reads=[r_owt[bo], r_cT], writes=[r_odps])
                    def _(e, he=he, dh=dh, bo=bo):
                        for ds in range(4):
                            last = e.matmul(odps[:, ds, :], owt[:, bo, ds * 128:(ds + 1) * 128],
                                            cT[:, he, :], start=(he == 0), stop=(he == 63))
                        return last
                @p.step("vector", reads=[r_odps], writes=[r_oTsb])
                def _(e, dh=dh):
                    return e.tensor_copy(oTsb[:, 4 * dh:4 * dh + 4, :], odps[:])
            for tt in range(4):
                t = tsl * 4 + tt
                bi = t % 2
                @p.step("sync", reads=[R['const']], writes=[R['xt'][bi]], dma=True)
                def _(e, dd, t=t, bi=bi):
                    dd(out=xt[:, bi, :1024], in_=x3q_h[t * 128:(t + 1) * 128, :])
                for d in range(8):
                    ti = d % 2
                    @p.step("tensor", reads=[r_oTsb, R['const']], writes=[R['tp'][ti]])
                    def _(e, d=d, tt=tt, ti=ti):
                        return e.transpose(tp[ti][:], oTsb[:, d, tt * 128:(tt + 1) * 128], idb[:])
                    @p.step("vector", reads=[R['tp'][ti], R['xt'][bi]], writes=[R['xo'][bi]])
                    def _(e, d=d, ti=ti, bi=bi):
                        return e.tensor_tensor(xo[:, bi, d * 128:(d + 1) * 128], tp[ti][:],
                                               xt[:, bi, d * 128:(d + 1) * 128], ALU.add)
                @p.step("gpsimd", reads=[R['xo'][bi]], dma=True)
                def _(e, dd, t=t, bi=bi):
                    dd(out=xout[t * 128:(t + 1) * 128, :], in_=xo[:, bi, :])
        p.drain()
        esD.close()
        esC.close()
    es.close()
    return nc



_BUILD_CACHE = {}


def _get(name, builder):
    if name not in _BUILD_CACHE:
        _BUILD_CACHE[name] = builder()
    return _BUILD_CACHE[name]


_EYE_BF = np.eye(128).astype(BF)
_EYE_F32 = np.eye(128).astype(np.float32)


def run_l0ffn(x_full, wg, wu, wd):
    nc = _get("l0ffn", build_l0ffn)
    xs = x_full.reshape(8, TC, D)
    in_maps = [{"xin": np.ascontiguousarray(xs[c]), "wg": wg, "wu": wu, "wd": wd,
                "idb": _EYE_BF} for c in range(NCORES)]
    res = run_bass_kernel_spmd(nc, in_maps, core_ids=list(range(NCORES)))
    out = np.stack([r["xout"] for r in res.results]).reshape(2, 8192, 1024)
    return out


def _fused_in_maps(inputs, runid):
    x = np.ascontiguousarray(inputs['L0_attn_x'], dtype=np.float32)
    gbrep = np.ascontiguousarray(np.concatenate(
        [np.broadcast_to(np.asarray(v, np.float32).reshape(1, 64), (128, 64)) for v in
         (inputs['L1_attn_q_gamma'], inputs['L1_attn_q_beta'],
          inputs['L1_attn_k_gamma'], inputs['L1_attn_k_beta'])], axis=1))
    selw = np.ascontiguousarray(np.concatenate(
        [inputs['L1_ffn_sel_v_w'], inputs['L1_ffn_sel_o_w']], 0), dtype=np.float32)
    repl = np.zeros((64, 64 * 128), np.float32)
    for he in range(64):
        repl[he, he * 128:(he + 1) * 128] = 1.0
    repl = repl.astype(BF)
    tri = np.triu(np.ones((128, 128), np.float32)).astype(BF)
    inv = (1.0 / 10000.0 ** (np.arange(0, 64, 2, dtype=np.float32) / 64)).astype(np.float32)
    common = {
        "wqkv": inputs['L0_attn_Wqkv'], "wout": inputs['L0_attn_Wout'],
        "bout": np.asarray(inputs['L0_attn_bout'], np.float32).reshape(1, D),
        "wg": inputs['L0_ffn_w_gate'], "wu": inputs['L0_ffn_w_up'], "wd": inputs['L0_ffn_w_down'],
        "wq1": inputs['L1_attn_Wq'], "wk1": inputs['L1_attn_Wk'], "wv1": inputs['L1_attn_Wv'],
        "wo1": inputs['L1_attn_Wo'], "gb": gbrep,
        "qw2": inputs['L1_ffn_q_w'], "kw2": inputs['L1_ffn_k_w'],
        "selw": selw, "vw": inputs['L1_ffn_v_w'], "ow": inputs['L1_ffn_o_w'],
        "rsc": np.ascontiguousarray(np.broadcast_to(
            np.asarray(inputs['L1_ffn_route_scale'], np.float32).reshape(1, 1), (128, 1))),
        "idb": _EYE_BF, "idf": _EYE_F32, "trid": tri, "repl": repl,
        "runid": np.full((1, 16), runid, np.int32),
    }
    ims = []
    for c in range(NCORES):
        b, side, par = c // 4, c % 2, (c % 4) // 2
        t0 = side * HTOK
        xhl = np.zeros((HTOK + 256, D), np.float32)
        lo, hi = max(0, t0 - 128), min(T, t0 + HTOK + 128)
        xhl[lo - (t0 - 128):hi - (t0 - 128)] = x[b, lo:hi]
        pos = np.arange(t0 - 128, t0 + HTOK + 128)
        pv = np.where((pos >= 0) & (pos < T), pos, 0).astype(np.float32)
        fr = np.outer(pv, inv).astype(np.float32)
        msk = np.zeros((4, 8 * 128, 384), np.float32)
        for sub in range(4):
            qb = np.arange(8)[:, None, None]
            qi = np.arange(128)[None, :, None]
            ki = np.arange(384)[None, None, :]
            gq = t0 + sub * 1024 + qb * 128 + qi
            gk = t0 + sub * 1024 - 128 + qb * 128 + ki
            keep = (np.abs(gk - gq) <= 128) & (gk >= 0) & (gk < T)
            msk[sub] = keep.reshape(8 * 128, 384)
        ims.append(dict(common, xh=xhl,
                        cosd=np.cos(fr).astype(np.float32),
                        sind=np.sin(fr).astype(np.float32),
                        mskd=msk.astype(BF),
                        sidec=np.full((128, 1), float(side), np.float32),
                        qoff=np.array([[par * QTOK * D, par * QTOK * 64, par * QTOK]
                                       + [0] * 13], np.int32)))
    return ims


def kernel(**inputs):
    nc = _get("fused", build_fused)
    runid = int(np.random.randint(1 << 20, 1 << 29))
    ims = _fused_in_maps(inputs, runid)
    res = run_bass_kernel_spmd(nc, ims, core_ids=list(range(NCORES))).results
    out = np.zeros((2, T, D), np.float32)
    for c in range(NCORES):
        b, side, par = c // 4, c % 2, (c % 4) // 2
        q0 = side * HTOK + par * QTOK
        out[b, q0:q0 + QTOK] = res[c]["xout"]
    return out


# version 06
# speedup vs baseline: 1.3365x; 1.3365x over previous
from contextlib import ExitStack

import dataclasses

import numpy as np
import ml_dtypes
import concourse.bass as bass
import concourse.mybir as mybir
from concourse.bass_utils import run_bass_kernel_spmd

F32 = mybir.dt.float32
BF16 = mybir.dt.bfloat16
AF = mybir.ActivationFunctionType
ALU = mybir.AluOpType
AX = mybir.AxisListType
BF = ml_dtypes.bfloat16

NCORES = 8
B, T, D = 2, 8192, 1024
TC = T // 4
MAGIC = 12582912.0

ENGS = ("sync", "gpsimd", "vector", "scalar", "tensor")


class Res:
    __slots__ = ("name", "writer", "readers")

    def __init__(self, name=""):
        self.name = name
        self.writer = None
        self.readers = []


class Prog:

    def __init__(self, nc, blk):
        self.nc, self.blk = nc, blk
        self.sem = {}
        self.cnt = {e: 0 for e in ENGS}
        self.clock = {e: {f: 0 for f in ENGS} for e in ENGS}
        self._cms = []
        for e in ENGS:
            cm = nc.semaphore("sem_" + e)
            self.sem[e] = cm.__enter__()
            self._cms.append(cm)
        cm = nc.semaphore("sem_aux")
        self.aux = cm.__enter__()
        self._cms.append(cm)
        self.auxc = [0]

    def chain(self, e):
        aux, auxc = self.aux, self.auxc
        def ch(instr):
            auxc[0] += 1
            instr.then_inc(aux, 1)
            e.wait_ge(aux, auxc[0])
            return instr
        return ch

    def step(self, eng, reads=(), writes=(), dma=False, fixed_inc=None):
        def deco(fn):
            needs = {}

            def need(tok, raw=False):
                if tok is None:
                    return
                e2, c2, _clk = tok
                if e2 == eng and not (raw or eng in ("sync", "gpsimd")):
                    return
                if self.clock[eng][e2] >= c2:
                    return
                needs[e2] = max(needs.get(e2, 0), c2)

            for r in reads:
                need(r.writer, raw=True)
            for w in writes:
                need(w.writer)
                for t in w.readers:
                    need(t)

            emitted = {"n": 0}
            prog = self

            section = getattr(self.blk, eng)

            def body(e):
                for e2, c2 in sorted(needs.items()):
                    e.wait_ge(prog.sem[e2], c2)
                if dma:
                    def dd(out, in_, **kw):
                        ins = e.dma_start(out=out, in_=in_, **kw)
                        ins.then_inc(prog.sem[eng], 16)
                        if fixed_inc is None:
                            emitted["n"] += 16
                        return ins
                    fn(e, dd)
                    if fixed_inc is not None:
                        emitted["n"] += fixed_inc
                else:
                    last = fn(e)
                    assert last is not None, "compute step must return last instr"
                    last.then_inc(prog.sem[eng], 1)
                    emitted["n"] += 1
            section(body)

            for e2, c2 in needs.items():
                if c2 > self.clock[eng][e2]:
                    self.clock[eng][e2] = c2
            self.cnt[eng] += emitted["n"]
            tok = (eng, self.cnt[eng], dict(self.clock[eng]))
            for w in writes:
                w.writer = tok
                w.readers = []
            for r in reads:
                r.readers.append(tok)
            return tok
        return deco

    def fence(self):
        snap = dict(self.cnt)
        for eng in ENGS:
            section = getattr(self.blk, eng)
            def body(e, eng=eng):
                for e2 in ENGS:
                    if e2 == eng and eng not in ("sync", "gpsimd"):
                        continue
                    if snap[e2] > self.clock[eng].get(e2, 0) or e2 == eng:
                        e.wait_ge(self.sem[e2], snap[e2])
            section(body)
            for e2 in ENGS:
                if e2 != eng:
                    self.clock[eng][e2] = max(self.clock[eng][e2], snap[e2])
        return snap

    def drain(self):
        cnts = dict(self.cnt)
        @self.blk.gpsimd
        def _(e):
            for e2 in ENGS:
                if cnts[e2] > 0:
                    e.wait_ge(self.sem[e2], cnts[e2])



def emit_quant(v, x_ap, out_ap, scr, nb, ch):
    P = 128
    F = nb * 16
    x3 = x_ap.rearrange("p (n b) -> p n b", b=16)
    z3 = scr['z'][:, :F].rearrange("p (n b) -> p n b", b=16)
    v.tensor_reduce(scr['amax'][:, :nb], x3, AX.X, ALU.max, apply_absolute_value=True)
    ch(v.tensor_scalar(scr['s6'][:, :nb], scr['amax'][:, :nb], 1.0 / 6.0, 1e-12, ALU.mult, ALU.max))
    ch(v.reciprocal(scr['r2'][:, :nb], scr['s6'][:, :nb]))
    ch(v.tensor_scalar_mul(scr['r2'][:, :nb], scr['r2'][:, :nb], 2.0))
    r2b = scr['r2'][:, :nb].unsqueeze(2).broadcast_to([P, nb, 16])
    s6b = scr['s6'][:, :nb].unsqueeze(2).broadcast_to([P, nb, 16])
    z = scr['z'][:, :F]
    a = scr['a'][:, :F]
    b = scr['b'][:, :F]
    c = scr['c'][:, :F]
    a3v = a.rearrange("p (n b) -> p n b", b=16)
    v.tensor_tensor(z3, a3v, r2b, ALU.mult)
    v.tensor_scalar(a, z, MAGIC, -MAGIC, ALU.add, ALU.add)
    v.tensor_scalar(b, z, 0.5, MAGIC, ALU.mult, ALU.add)
    v.tensor_scalar(c, b, -(MAGIC + 2.0), 0.0, ALU.add, ALU.max)
    v.tensor_scalar(b, a, 4.0, 0.5, ALU.min, ALU.mult)
    v.scalar_tensor_tensor(a, c, 2.0, b, ALU.min, ALU.add)
    v.tensor_single_scalar(c, z, 10.0, ALU.is_gt)
    v.scalar_tensor_tensor(b, c, 2.0, a, ALU.mult, ALU.add)
    v.tensor_tensor(a, b, scr['sgn'][:, :F], ALU.mult)
    a3 = a.rearrange("p (n b) -> p n b", b=16)
    o3 = out_ap.rearrange("p (n b) -> p n b", b=16)
    return v.scalar_tensor_tensor(o3, a3, 0.0, s6b, ALU.add, ALU.mult)


def emit_ln_stats(v, x_ap, st, F, ch):
    v.reduce_sum(st['s1'], x_ap, AX.X)
    v.tensor_tensor(st['sq'][:, :F], x_ap, x_ap, ALU.mult)
    ch(v.reduce_sum(st['s2'], st['sq'][:, :F], AX.X))
    v.tensor_scalar_mul(st['mu'], st['s1'], 1.0 / F)
    ch(v.tensor_scalar_mul(st['m2'], st['s2'], 1.0 / F))
    ch(v.tensor_tensor(st['t'], st['mu'], st['mu'], ALU.mult))
    ch(v.tensor_tensor(st['var'], st['m2'], st['t'], ALU.subtract))
    return v.tensor_scalar_add(st['vpe'], st['var'], 1e-5)



def build_l0ffn(debug=False):
    nc = bass.Bass("TRN2", target_bir_lowering=False)
    xin = nc.dram_tensor("xin", [TC, D], F32, kind="ExternalInput").ap()
    wg_d = nc.dram_tensor("wg", [2816, D], F32, kind="ExternalInput").ap()
    wu_d = nc.dram_tensor("wu", [2816, D], F32, kind="ExternalInput").ap()
    wd_d = nc.dram_tensor("wd", [D, 2816], F32, kind="ExternalInput").ap()
    idb_d = nc.dram_tensor("idb", [128, 128], BF16, kind="ExternalInput").ap()
    xout = nc.dram_tensor("xout", [TC, D], F32, kind="ExternalOutput").ap()
    hq_hbm = nc.dram_tensor("hqscratch", [TC // 128, 128, 2816], BF16, kind="Internal").ap()
    if debug:
        dbg_n = nc.dram_tensor("dbg_n", [128, 1024], F32, kind="ExternalOutput").ap()
        dbg_xq = nc.dram_tensor("dbg_xq", [128, 1024], F32, kind="ExternalOutput").ap()
        dbg_wgq = nc.dram_tensor("dbg_wgq", [128, 2816], F32, kind="ExternalOutput").ap()
        dbg_h = nc.dram_tensor("dbg_h", [128, 2816], F32, kind="ExternalOutput").ap()
        dbg_hq = nc.dram_tensor("dbg_hq", [128, 2816], F32, kind="ExternalOutput").ap()

    NT = TC // 128
    NI = 2816 // 128

    es = ExitStack()
    sb = lambda *a: es.enter_context(nc.sbuf_tensor(*a))
    ps = lambda *a: es.enter_context(nc.psum_tensor(*a))
    idb = sb("idb_sb", [128, 128], BF16)
    xt = sb("xt", [128, 2, 1408], F32)
    z_ = sb("z", [128, 1408], F32)
    a_ = sb("a", [128, 1408], F32)
    b_ = sb("bq", [128, 1408], F32)
    c_ = sb("cq", [128, 1408], F32)
    sgn_ = sb("sgn", [128, 1408], BF16)
    qout = sb("qout", [128, 2, 1408], BF16)
    nsm = sb("nsm", [128, 16], F32)
    amax_ = sb("amax", [128, 88], F32)
    s6_ = sb("s6", [128, 88], F32)
    r2_ = sb("r2", [128, 88], F32)
    nbuf = sb("nbuf", [128, 2, 1024], F32)
    xo = sb("xo", [128, 2, 1024], F32)
    tp0 = ps("tp0", [128, 128], BF16)
    tp1 = ps("tp1", [128, 128], BF16)
    gps = ps("gps", [128, 512], F32)
    ups = ps("ups", [128, 512], F32)
    dps = ps("dps", [128, 2, 512], F32)
    with nc.Block() as blk:
        p = Prog(nc, blk)
        scr = dict(z=z_, a=a_, b=b_, c=c_, sgn=sgn_, amax=amax_, s6=s6_, r2=r2_)
        st = dict(s1=nsm[:, 0:1], s2=nsm[:, 1:2], mu=nsm[:, 2:3], m2=nsm[:, 3:4],
                  t=nsm[:, 4:5], var=nsm[:, 5:6], vpe=nsm[:, 6:7], std=nsm[:, 7:8],
                  rstd=nsm[:, 8:9], sq=z_)
        r_idb = Res("idb")
        r_xt = [Res("xt0"), Res("xt1")]
        r_scr = Res("scr")
        r_sgn = Res("sgn")
        r_qout = [Res("q0"), Res("q1")]
        r_nbuf = [Res("n0"), Res("n1")]
        r_tp = [Res("tp0"), Res("tp1")]
        r_gps, r_ups = Res("gps"), Res("ups")
        r_dps = [Res("dps0"), Res("dps1")]
        r_xo = [Res("xo0"), Res("xo1")]
        r_hqt = Res("hqTt")
        r_hqhbm = [Res() for _ in range(16)]
        r_hbuf = Res("hbuf")
        r_gs = Res("gs")

        @p.step("sync", writes=[r_idb], dma=True)
        def _(e, dd):
            dd(out=idb[:], in_=idb_d[:, :])

        tp = [tp0, tp1]
        tpi = [0]

        def quant_chain(x_res, x_ap, q_res, q_ap, nb):
            F = nb * 16
            @p.step("scalar", reads=[x_res], writes=[r_sgn, r_scr])
            def _(e):
                e.activation(out=sgn_[:, :F], in_=x_ap, func=AF.Sign)
                return e.activation(out=a_[:, :F], in_=x_ap, func=AF.Abs)
            @p.step("vector", reads=[x_res, r_sgn], writes=[q_res, r_scr])
            def _(e):
                return emit_quant(e, x_ap, q_ap, scr, nb, p.chain(e))

        def transpose_to(q_res, q_ap_fn, nblk, dst_res, dst_ap_fn):
            for d in range(nblk):
                i = tpi[0] % 2
                tpi[0] += 1
                @p.step("tensor", reads=[q_res, r_idb], writes=[r_tp[i]])
                def _(e, d=d, i=i):
                    return e.transpose(tp[i][:], q_ap_fn(d), idb[:])
                @p.step("vector", reads=[r_tp[i]], writes=[dst_res])
                def _(e, d=d, i=i):
                    return e.tensor_copy(dst_ap_fn(d), tp[i][:])

        es1 = ExitStack()
        sb1 = lambda *a: es1.enter_context(nc.sbuf_tensor(*a))
        wgT = sb1("wgT", [128, 8, 2816], BF16)
        wuT = sb1("wuT", [128, 8, 2816], BF16)
        xqTt = sb1("xqTt", [128, 2, 1024], BF16)
        hqTt = sb1("hqTt", [128, 2816], BF16)
        hbuf = sb1("hbuf", [128, 2816], F32)
        gs = sb1("gs", [128, 512], BF16)
        if True:
            r_wgT, r_wuT = Res("wgT"), Res("wuT")
            r_xqTt = [Res("xqTt0"), Res("xqTt1")]

            for wdram, wT, wres in ((wg_d, wgT, r_wgT), (wu_d, wuT, r_wuT)):
                for i in range(NI):
                    bi = i % 2
                    @p.step("sync", writes=[r_xt[bi]], dma=True)
                    def _(e, dd, i=i, bi=bi, wdram=wdram):
                        dd(out=xt[:, bi, :1024], in_=wdram[i * 128:(i + 1) * 128, :])
                    quant_chain(r_xt[bi], xt[:, bi, :1024], r_qout[bi],
                                qout[:, bi, :1024], 64)
                    transpose_to(r_qout[bi], (lambda d, bi=bi: qout[:, bi, d * 128:(d + 1) * 128]),
                                 8, wres,
                                 (lambda d, i=i, wT=wT: wT[:, d, i * 128:(i + 1) * 128]))

            if debug:
                @p.step("gpsimd", reads=[r_wgT], dma=True)
                def _(e, dd):
                    dd(out=dbg_wgq[:, :], in_=wgT[:, 0, :])
            otiles = [(0, 512), (512, 512), (1024, 512), (1536, 512), (2048, 512), (2560, 256)]
            for t in range(NT):
                bi = t % 2
                @p.step("sync", writes=[r_xt[bi]], dma=True)
                def _(e, dd, t=t, bi=bi):
                    dd(out=xt[:, bi, :1024], in_=xin[t * 128:(t + 1) * 128, :])
                @p.step("vector", reads=[r_xt[bi]], writes=[r_scr])
                def _(e, bi=bi):
                    return emit_ln_stats(e, xt[:, bi, :1024], st, 1024, p.chain(e))
                @p.step("scalar", reads=[r_scr], writes=[r_scr])
                def _(e):
                    return e.activation(out=st['std'], in_=st['vpe'], func=AF.Sqrt)
                @p.step("vector", reads=[r_xt[bi], r_scr], writes=[r_nbuf[bi], r_scr])
                def _(e, bi=bi):
                    p.chain(e)(e.reciprocal(st['rstd'], st['std']))
                    return e.tensor_scalar(nbuf[:, bi, :], xt[:, bi, :1024],
                                           st['mu'], st['rstd'], ALU.subtract, ALU.mult)
                if debug and t == 0:
                    @p.step("gpsimd", reads=[r_nbuf[bi]], dma=True)
                    def _(e, dd, bi=bi):
                        dd(out=dbg_n[:, :], in_=nbuf[:, bi, :])
                quant_chain(r_nbuf[bi], nbuf[:, bi, :], r_qout[bi], qout[:, bi, :1024], 64)
                if debug and t == 0:
                    @p.step("gpsimd", reads=[r_qout[bi]], dma=True)
                    def _(e, dd, bi=bi):
                        dd(out=dbg_xq[:, :], in_=qout[:, bi, :1024])
                transpose_to(r_qout[bi], (lambda d, bi=bi: qout[:, bi, d * 128:(d + 1) * 128]),
                             8, r_xqTt[bi],
                             (lambda d, bi=bi: xqTt[:, bi, d * 128:(d + 1) * 128]))
                for (o0, ow) in otiles:
                    @p.step("tensor", reads=[r_xqTt[bi], r_wgT, r_wuT], writes=[r_gps, r_ups])
                    def _(e, bi=bi, o0=o0, ow=ow):
                        for d in range(8):
                            e.matmul(gps[:, :ow], xqTt[:, bi, d * 128:(d + 1) * 128],
                                     wgT[:, d, o0:o0 + ow], start=(d == 0), stop=(d == 7))
                        for d in range(8):
                            last = e.matmul(ups[:, :ow], xqTt[:, bi, d * 128:(d + 1) * 128],
                                            wuT[:, d, o0:o0 + ow], start=(d == 0), stop=(d == 7))
                        return last
                    @p.step("scalar", reads=[r_gps], writes=[r_gs])
                    def _(e, ow=ow):
                        return e.activation(out=gs[:, :ow], in_=gps[:, :ow], func=AF.Silu)
                    @p.step("vector", reads=[r_gs, r_ups], writes=[r_hbuf])
                    def _(e, o0=o0, ow=ow):
                        return e.tensor_tensor(hbuf[:, o0:o0 + ow], ups[:, :ow],
                                               gs[:, :ow], ALU.mult)
                for hf in range(2):
                    h0 = hf * 1408
                    quant_chain(r_hbuf, hbuf[:, h0:h0 + 1408], r_qout[hf],
                                qout[:, hf, :1408], 88)
                    transpose_to(r_qout[hf], (lambda d, hf=hf: qout[:, hf, d * 128:(d + 1) * 128]),
                                 11, r_hqt,
                                 (lambda d, h0=h0: hqTt[:, h0 + d * 128: h0 + (d + 1) * 128]))
                if debug and t == 0:
                    @p.step("gpsimd", reads=[r_hbuf], dma=True)
                    def _(e, dd):
                        dd(out=dbg_h[:, :], in_=hbuf[:, :])
                    @p.step("gpsimd", reads=[r_hqt], dma=True)
                    def _(e, dd):
                        dd(out=dbg_hq[:, :], in_=hqTt[:, :])
                @p.step("gpsimd", reads=[r_hqt], writes=[r_hqhbm[t]], dma=True)
                def _(e, dd, t=t):
                    dd(out=hq_hbm[t], in_=hqTt[:, :])

        es1.close()
        es2 = ExitStack()
        wdT = es2.enter_context(nc.sbuf_tensor("wdT", [128, NI, 1024], BF16))
        hqin = es2.enter_context(nc.sbuf_tensor("hqin", [128, 2, 2816], BF16))
        if True:
            r_wdT = Res("wdT")
            r_hqin = [Res("hq0"), Res("hq1")]
            for i in range(8):
                for hf in range(2):
                    h0 = hf * 1408
                    bi = (2 * i + hf) % 2
                    @p.step("sync", writes=[r_xt[bi]], dma=True)
                    def _(e, dd, i=i, h0=h0, bi=bi):
                        dd(out=xt[:, bi, :1408], in_=wd_d[i * 128:(i + 1) * 128, h0:h0 + 1408])
                    quant_chain(r_xt[bi], xt[:, bi, :1408], r_qout[bi],
                                qout[:, bi, :1408], 88)
                    transpose_to(r_qout[bi], (lambda d, bi=bi: qout[:, bi, d * 128:(d + 1) * 128]),
                                 11, r_wdT,
                                 (lambda d, i=i, h0=h0: wdT[:, h0 // 128 + d, i * 128:(i + 1) * 128]))
            for t in range(NT):
                bi = t % 2
                @p.step("sync", reads=[r_hqhbm[t]], writes=[r_hqin[bi]], dma=True)
                def _(e, dd, t=t, bi=bi):
                    dd(out=hqin[:, bi, :], in_=hq_hbm[t])
                @p.step("sync", writes=[r_xt[bi]], dma=True)
                def _(e, dd, t=t, bi=bi):
                    dd(out=xt[:, bi, :1024], in_=xin[t * 128:(t + 1) * 128, :])
                for oi, (o0, ow) in enumerate([(0, 512), (512, 512)]):
                    @p.step("tensor", reads=[r_hqin[bi], r_wdT], writes=[r_dps[oi]])
                    def _(e, bi=bi, o0=o0, ow=ow, oi=oi):
                        for i in range(NI):
                            last = e.matmul(dps[:, oi, :ow], hqin[:, bi, i * 128:(i + 1) * 128],
                                            wdT[:, i, o0:o0 + ow], start=(i == 0), stop=(i == NI - 1))
                        return last
                    @p.step("vector", reads=[r_dps[oi], r_xt[bi]], writes=[r_xo[bi]])
                    def _(e, bi=bi, o0=o0, ow=ow, oi=oi):
                        return e.tensor_tensor(xo[:, bi, o0:o0 + ow], dps[:, oi, :ow],
                                               xt[:, bi, o0:o0 + ow], ALU.add)
                @p.step("gpsimd", reads=[r_xo[bi]], writes=[], dma=True)
                def _(e, dd, t=t, bi=bi):
                    dd(out=xout[t * 128:(t + 1) * 128, :], in_=xo[:, bi, :])
        es2.close()
        p.drain()
    es.close()
    return nc





def build_l0attn(debug=False):
    nc = bass.Bass("TRN2", target_bir_lowering=False)
    NTE = 18
    xin = nc.dram_tensor("xin", [NTE * 128, D], F32, kind="ExternalInput").ap()
    wqkv_d = nc.dram_tensor("wqkv", [D, 3 * D], F32, kind="ExternalInput").ap()
    wout_d = nc.dram_tensor("wout", [D, D], F32, kind="ExternalInput").ap()
    bout_d = nc.dram_tensor("bout", [1, D], F32, kind="ExternalInput").ap()
    cos_d = nc.dram_tensor("cosd", [NTE * 128, 32], F32, kind="ExternalInput").ap()
    sin_d = nc.dram_tensor("sind", [NTE * 128, 32], F32, kind="ExternalInput").ap()
    msk_d = nc.dram_tensor("mskd", [16 * 128, 384], BF16, kind="ExternalInput").ap()
    idb_d = nc.dram_tensor("idb", [128, 128], BF16, kind="ExternalInput").ap()
    xout = nc.dram_tensor("xout", [TC, D], F32, kind="ExternalOutput").ap()
    if debug:
        dbg_q = nc.dram_tensor("dbg_q", [128, D], F32, kind="ExternalOutput").ap()
        dbg_k = nc.dram_tensor("dbg_k", [128, D], F32, kind="ExternalOutput").ap()
        dbg_v = nc.dram_tensor("dbg_v", [128, D], F32, kind="ExternalOutput").ap()
        dbg_p = nc.dram_tensor("dbg_p", [128, 384], F32, kind="ExternalOutput").ap()
        dbg_at = nc.dram_tensor("dbg_at", [128, 8 * 128], F32, kind="ExternalOutput").ap()

    aT_hbm = nc.dram_tensor("aTscratch", [8, 128, TC], BF16, kind="Internal").ap()
    es = ExitStack()
    sb = lambda *a: es.enter_context(nc.sbuf_tensor(*a))
    ps = lambda *a: es.enter_context(nc.psum_tensor(*a))
    idb = sb("idb_sb", [128, 128], BF16)
    bout_sb = sb("bout_sb", [1, D], BF16)
    ones_sb = sb("ones_sb", [1, 128], BF16)
    cos_sb = sb("cos_sb", [128, NTE, 32], BF16)
    sin_sb = sb("sin_sb", [128, NTE, 32], BF16)
    xt = sb("xt", [128, 2, D], F32)
    nbf = sb("nbf", [128, 2, D], BF16)
    nf32 = sb("nf32", [128, 2, D], F32)
    ntS = sb("ntS", [128, 2, D], BF16)
    qt = sb("qt", [128, 2, D], BF16)
    kt = sb("kt", [128, 2, D], BF16)
    rt = sb("rt", [128, 2048], F32)
    vbuf = sb("vbuf", [128, NTE, D], BF16)
    qT = sb("qT", [128, 8, NTE * 128], BF16)
    kT = sb("kT", [128, 8, NTE * 128], BF16)
    pb = sb("pb", [128, 384], BF16)
    pT = sb("pT", [128, 384], BF16)
    nsm = sb("nsm2", [128, 16], F32)
    xo = sb("xo", [128, 2, D], F32)
    es1 = ExitStack()
    wq_sb = es1.enter_context(nc.sbuf_tensor("wq_sb", [128, 8, 3 * D], BF16))
    tpW = ps("tpW", [128, 1024], BF16)
    mmA = ps("mmA", [128, 512], F32)
    mmB = ps("mmB", [128, 512], F32)
    scp = ps("scp", [128, 384], F32)
    ops_ = ps("ops", [64, 128], F32)
    with nc.Block() as blk:
        p = Prog(nc, blk)
        st = dict(s1=nsm[:, 0:1], s2=nsm[:, 1:2], mu=nsm[:, 2:3], m2=nsm[:, 3:4],
                  t=nsm[:, 4:5], var=nsm[:, 5:6], vpe=nsm[:, 6:7], std=nsm[:, 7:8],
                  rstd=nsm[:, 8:9], mx=nsm[:, 9:10], bias=nsm[:, 10:11],
                  rs=nsm[:, 11:12], rr=nsm[:, 12:13], sq=rt)
        r_const = Res("const")
        r_xt = [Res(), Res()]
        r_nbf = [Res(), Res()]
        r_ntS = [Res(), Res()]
        r_qt = [Res(), Res()]
        r_kt = [Res(), Res()]
        r_rt = Res("rt")
        r_vbuf = Res("vbuf")
        r_qT, r_kT = Res("qT"), Res("kT")
        r_pb, r_pT = Res("pb"), Res("pT")
        r_nsm = Res("nsm")
        r_tp = [Res(), Res()]
        r_mm = [Res(), Res()]
        r_scp, r_ops = Res("scp"), Res("ops")
        r_xo = [Res(), Res()]
        tp = [tpA, tpB]
        mm = [mmA, mmB]

        @p.step("sync", writes=[r_const], dma=True)
        def _(e, dd):
            dd(out=idb[:], in_=idb_d[:, :])

        @p.step("gpsimd", writes=[r_const], dma=True)
        def _(e, dd):
            dd(out=cos_sb[:], in_=cos_d.rearrange("(t p) c -> p t c", p=128))
            dd(out=sin_sb[:], in_=sin_d.rearrange("(t p) c -> p t c", p=128))
            dd(out=wq_sb[:], in_=wqkv_d.rearrange("(c p) o -> p c o", p=128))
            dd(out=bout_sb[:], in_=bout_d[:, :])

        @p.step("vector", writes=[r_const])
        def _(e):
            return e.memset(ones_sb[:], 1.0)

        for t in range(NTE):
            bi = t % 2
            @p.step("sync", writes=[r_xt[bi]], dma=True)
            def _(e, dd, t=t):
                dd(out=xt[:, t % 2, :], in_=xin[t * 128:(t + 1) * 128, :])
            @p.step("vector", reads=[r_xt[bi]], writes=[r_nsm, r_rt])
            def _(e, bi=bi):
                return emit_ln_stats(e, xt[:, bi, :], st, 1024, p.chain(e))
            @p.step("scalar", reads=[r_nsm], writes=[r_nsm])
            def _(e):
                return e.activation(out=st['std'], in_=st['vpe'], func=AF.Sqrt)
            @p.step("vector", reads=[r_xt[bi], r_nsm], writes=[r_nbf[bi], r_nsm])
            def _(e, bi=bi):
                p.chain(e)(e.reciprocal(st['rstd'], st['std']))
                return e.tensor_scalar(nbf[:, bi, :], xt[:, bi, :],
                                       st['mu'], st['rstd'], ALU.subtract, ALU.mult)
            for d in range(8):
                ti = d % 2
                @p.step("tensor", reads=[r_nbf[bi], r_const], writes=[r_tp[ti]])
                def _(e, bi=bi, d=d, ti=ti):
                    return e.transpose(tp[ti][:], nbf[:, bi, d * 128:(d + 1) * 128], idb[:])
                @p.step("vector", reads=[r_tp[ti]], writes=[r_ntS[bi]])
                def _(e, bi=bi, d=d, ti=ti):
                    return e.tensor_copy(ntS[:, bi, d * 128:(d + 1) * 128], tp[ti][:])
            for s6 in range(6):
                mi = s6 % 2
                @p.step("tensor", reads=[r_ntS[bi], r_const], writes=[r_mm[mi]])
                def _(e, bi=bi, s6=s6, mi=mi):
                    for d in range(8):
                        last = e.matmul(mm[mi][:], ntS[:, bi, d * 128:(d + 1) * 128],
                                        wq_sb[:, d, s6 * 512:(s6 + 1) * 512],
                                        start=(d == 0), stop=(d == 7))
                    return last
                dst = [(qt, r_qt[bi], 0), (qt, r_qt[bi], 512), (kt, r_kt[bi], 0),
                       (kt, r_kt[bi], 512), (None, r_vbuf, 0), (None, r_vbuf, 512)][s6]
                @p.step("vector", reads=[r_mm[mi]], writes=[dst[1]])
                def _(e, bi=bi, s6=s6, mi=mi, dst=dst, t=t):
                    if dst[0] is None:
                        return e.tensor_copy(vbuf[:, t, dst[2]:dst[2] + 512], mm[mi][:])
                    return e.tensor_copy(dst[0][:, bi, dst[2]:dst[2] + 512], mm[mi][:])
            for (buf, rres) in ((qt, r_qt[bi]), (kt, r_kt[bi])):
                @p.step("vector", reads=[rres, r_const], writes=[rres, r_rt])
                def _(e, bi=bi, buf=buf, t=t):
                    v3 = buf[:, bi, :].rearrange("p (h x) -> p h x", x=64)
                    x1, x2 = v3[:, :, 0:32], v3[:, :, 32:64]
                    cb = cos_sb[:, t, :].unsqueeze(1).broadcast_to([128, 16, 32])
                    sbq = sin_sb[:, t, :].unsqueeze(1).broadcast_to([128, 16, 32])
                    t1 = rt[:, 0:512].rearrange("p (h x) -> p h x", x=32)
                    t2 = rt[:, 512:1024].rearrange("p (h x) -> p h x", x=32)
                    t3 = rt[:, 1024:1536].rearrange("p (h x) -> p h x", x=32)
                    t4 = rt[:, 1536:2048].rearrange("p (h x) -> p h x", x=32)
                    e.tensor_tensor(t1, x1, cb, ALU.mult)
                    e.tensor_tensor(t2, x2, sbq, ALU.mult)
                    e.tensor_tensor(t3, x2, cb, ALU.mult)
                    e.tensor_tensor(t4, x1, sbq, ALU.mult)
                    e.tensor_tensor(x1, t1, t2, ALU.subtract)
                    return e.tensor_tensor(x2, t3, t4, ALU.add)
            if debug and t == 2:
                @p.step("gpsimd", reads=[r_qt[bi], r_kt[bi], r_vbuf], dma=True)
                def _(e, dd, bi=bi, t=t):
                    dd(out=dbg_q[:, :], in_=qt[:, bi, :])
                    dd(out=dbg_k[:, :], in_=kt[:, bi, :])
                    dd(out=dbg_v[:, :], in_=vbuf[:, t, :])
            for (buf, rres, dstT, dres) in ((qt, r_qt[bi], qT, r_qT), (kt, r_kt[bi], kT, r_kT)):
                for d in range(8):
                    ti = d % 2
                    @p.step("tensor", reads=[rres, r_const], writes=[r_tp[ti]])
                    def _(e, bi=bi, d=d, ti=ti, buf=buf):
                        return e.transpose(tp[ti][:], buf[:, bi, d * 128:(d + 1) * 128], idb[:])
                    @p.step("vector", reads=[r_tp[ti]], writes=[dres])
                    def _(e, d=d, ti=ti, t=t, dstT=dstT):
                        return e.tensor_copy(dstT[:, d, t * 128:(t + 1) * 128], tp[ti][:])

        es1.close()
        es2 = ExitStack()
        msk_sb = es2.enter_context(nc.sbuf_tensor("msk_sb", [128, 16, 384], BF16))
        wo_sb = es2.enter_context(nc.sbuf_tensor("wo_sb", [128, 8, D], BF16))
        aTs = es2.enter_context(nc.sbuf_tensor("aTs", [128, 8, 128], BF16))
        aTin = es2.enter_context(nc.sbuf_tensor("aTin", [128, 2, 8, 128], BF16))
        r_aTs = Res("aTs")
        r_aTin = [Res(), Res()]
        r_aThbm = [Res() for _ in range(16)]

        @p.step("sync", writes=[r_const], dma=True)
        def _(e, dd):
            dd(out=msk_sb[:], in_=msk_d.rearrange("(t p) c -> p t c", p=128))

        @p.step("gpsimd", writes=[r_const], dma=True)
        def _(e, dd):
            dd(out=wo_sb[:], in_=wout_d.rearrange("(c p) o -> p c o", p=128))

        for qb in range(16):
            for h in range(16):
                hp0 = (h % 2) * 64
                hc = h // 2
                @p.step("tensor", reads=[r_qT, r_kT], writes=[r_scp])
                def _(e, h=h, qb=qb, hp0=hp0, hc=hc):
                    return e.matmul(scp[:],
                                    qT[hp0:hp0 + 64, hc, (qb + 1) * 128:(qb + 2) * 128],
                                    kT[hp0:hp0 + 64, hc, qb * 128:qb * 128 + 384],
                                    start=True, stop=True)
                @p.step("vector", reads=[r_scp], writes=[r_nsm])
                def _(e):
                    ch = p.chain(e)
                    ch(e.reduce_max(st['mx'], scp[:], AX.X))
                    return e.tensor_scalar_mul(st['bias'], st['mx'], -0.125)
                @p.step("scalar", reads=[r_scp, r_nsm], writes=[r_pb])
                def _(e):
                    return e.activation(out=pb[:], in_=scp[:], func=AF.Exp,
                                        bias=st['bias'], scale=0.125)
                @p.step("vector", reads=[r_pb, r_const], writes=[r_pb, r_nsm])
                def _(e, qb=qb):
                    ch = p.chain(e)
                    e.tensor_tensor(pb[:], pb[:], msk_sb[:, qb, :], ALU.mult)
                    ch(e.reduce_sum(st['rs'], pb[:], AX.X))
                    ch(e.reciprocal(st['rr'], st['rs']))
                    return e.tensor_scalar_mul(pb[:], pb[:], st['rr'])
                if debug and qb == 4 and h == 3:
                    @p.step("gpsimd", reads=[r_pb], dma=True)
                    def _(e, dd):
                        dd(out=dbg_p[:, :], in_=pb[:])
                for j in range(3):
                    ti = j % 2
                    @p.step("tensor", reads=[r_pb, r_const], writes=[r_tp[ti]])
                    def _(e, j=j, ti=ti):
                        return e.transpose(tp[ti][:], pb[:, j * 128:(j + 1) * 128], idb[:])
                    @p.step("vector", reads=[r_tp[ti]], writes=[r_pT])
                    def _(e, j=j, ti=ti):
                        return e.tensor_copy(pT[:, j * 128:(j + 1) * 128], tp[ti][:])
                @p.step("tensor", reads=[r_pT, r_vbuf], writes=[r_ops])
                def _(e, h=h, qb=qb):
                    for j in range(3):
                        last = e.matmul(ops_[:], vbuf[:, qb + j, h * 64:(h + 1) * 64],
                                        pT[:, j * 128:(j + 1) * 128],
                                        start=(j == 0), stop=(j == 2))
                    return last
                @p.step("vector", reads=[r_ops], writes=[r_aTs])
                def _(e, h=h, hp0=hp0, hc=hc):
                    return e.tensor_copy(aTs[hp0:hp0 + 64, hc, :], ops_[:])
            @p.step("gpsimd", reads=[r_aTs], writes=[r_aThbm[qb]], dma=True)
            def _(e, dd, qb=qb):
                dd(out=aT_hbm[:, :, qb * 128:(qb + 1) * 128].rearrange("c p q -> p c q"),
                   in_=aTs[:])
            if debug and qb == 4:
                @p.step("gpsimd", reads=[r_aTs], dma=True)
                def _(e, dd):
                    dd(out=dbg_at.rearrange("p (c q) -> p c q", q=128), in_=aTs[:])

        for t in range(16):
            bi = t % 2
            @p.step("sync", reads=[r_aThbm[t]], writes=[r_xt[bi], r_aTin[bi]], dma=True)
            def _(e, dd, t=t, bi=bi):
                dd(out=xt[:, bi, :], in_=xin[(t + 1) * 128:(t + 2) * 128, :])
                dd(out=aTin[:, bi, :, :],
                   in_=aT_hbm[:, :, t * 128:(t + 1) * 128].rearrange("c p q -> p c q"))
            for oi in range(2):
                mi = oi % 2
                @p.step("tensor", reads=[r_aTin[bi], r_const], writes=[r_mm[mi]])
                def _(e, t=t, oi=oi, mi=mi, bi=bi):
                    for d in range(8):
                        e.matmul(mm[mi][:], aTin[:, bi, d, :],
                                 wo_sb[:, d, oi * 512:(oi + 1) * 512],
                                 start=(d == 0), stop=False)
                    return e.matmul(mm[mi][:], ones_sb[:, 0:128],
                                    bout_sb[:, oi * 512:(oi + 1) * 512],
                                    start=False, stop=True)
                @p.step("vector", reads=[r_mm[mi], r_xt[bi]], writes=[r_xo[bi]])
                def _(e, t=t, oi=oi, mi=mi, bi=bi):
                    return e.tensor_tensor(xo[:, bi, oi * 512:(oi + 1) * 512], mm[mi][:],
                                           xt[:, bi, oi * 512:(oi + 1) * 512], ALU.add)
            @p.step("gpsimd", reads=[r_xo[bi]], dma=True)
            def _(e, dd, t=t, bi=bi):
                dd(out=xout[t * 128:(t + 1) * 128, :], in_=xo[:, bi, :])
        es2.close()
        p.drain()
    es.close()
    return nc


def _halo_slice(x_full, c):
    b, sl = c // 4, c % 4
    start, end = sl * TC, (sl + 1) * TC
    out = np.zeros((TC + 256, D), np.float32)
    lo, hi = max(0, start - 128), min(T, end + 128)
    out[lo - (start - 128):hi - (start - 128)] = x_full[b, lo:hi]
    return out


def _l0attn_consts(c):
    b, sl = c // 4, c % 4
    start = sl * TC
    pos = np.arange(start - 128, start + TC + 128)
    inv = (1.0 / 10000.0 ** (np.arange(0, 64, 2, dtype=np.float32) / 64)).astype(np.float32)
    pv = np.where((pos >= 0) & (pos < T), pos, 0).astype(np.float32)
    fr = np.outer(pv, inv).astype(np.float32)
    cos, sin = np.cos(fr).astype(np.float32), np.sin(fr).astype(np.float32)
    qb = np.arange(16)[:, None, None]
    qi = np.arange(128)[None, :, None]
    ki = np.arange(384)[None, None, :]
    gq = start + qb * 128 + qi
    gk = start - 128 + qb * 128 + ki
    keep = (np.abs(gk - gq) <= 128) & (gk >= 0) & (gk < T)
    msk = keep.astype(BF).reshape(16 * 128, 384)
    return cos, sin, msk


def run_l0attn(x_full, wqkv, wout, bout):
    nc = _get("l0attn", build_l0attn)
    in_maps = []
    for c in range(NCORES):
        cos, sin, msk = _l0attn_consts(c)
        in_maps.append({"xin": _halo_slice(x_full, c), "wqkv": wqkv, "wout": wout,
                        "bout": bout.reshape(1, D), "cosd": cos, "sind": sin,
                        "mskd": msk, "idb": _EYE_BF})
    res = run_bass_kernel_spmd(nc, in_maps, core_ids=list(range(NCORES)))
    return np.stack([r["xout"] for r in res.results]).reshape(2, 8192, 1024)





def build_l1attn_pre():
    nc = bass.Bass("TRN2", target_bir_lowering=False)
    xin = nc.dram_tensor("xin", [TC, D], F32, kind="ExternalInput").ap()
    wq_d = nc.dram_tensor("wq", [D, D], F32, kind="ExternalInput").ap()
    wk_d = nc.dram_tensor("wk", [D, D], F32, kind="ExternalInput").ap()
    wv_d = nc.dram_tensor("wv", [D, D], F32, kind="ExternalInput").ap()
    gb_d = nc.dram_tensor("gb", [128, 4 * 64], F32, kind="ExternalInput").ap()
    idb_d = nc.dram_tensor("idb", [128, 128], BF16, kind="ExternalInput").ap()
    qT_o = nc.dram_tensor("qT_o", [8, 128, TC], BF16, kind="ExternalOutput").ap()
    kT_o = nc.dram_tensor("kT_o", [8, 128, TC], BF16, kind="ExternalOutput").ap()
    ktm_o = nc.dram_tensor("ktm_o", [TC, D], BF16, kind="ExternalOutput").ap()
    vtm_o = nc.dram_tensor("vtm_o", [TC, D], BF16, kind="ExternalOutput").ap()

    es = ExitStack()
    sb = lambda *a: es.enter_context(nc.sbuf_tensor(*a))
    ps = lambda *a: es.enter_context(nc.psum_tensor(*a))
    idb = sb("idb_sb", [128, 128], BF16)
    gb = sb("gb_sb", [128, 4 * 64], F32)
    wln = sb("wln", [128, 3, 8, 1024], BF16)
    wT = sb("wT", [128, 3, 8, 1024], BF16)
    xt = sb("xt", [128, 2, D], F32)
    nbf = sb("nbf", [128, 2, D], BF16)
    nf32 = sb("nf32", [128, 2, D], F32)
    ntS = sb("ntS", [128, 2, D], BF16)
    qtm = sb("qtm", [128, 2, D], BF16)
    ktm = sb("ktm", [128, 2, D], BF16)
    vtm = sb("vtm", [128, 2, D], BF16)
    sca = sb("sca", [128, 2048], F32)
    hst = sb("hst", [128, 8, 16], F32)
    sT = sb("sT", [128, 8, 128], BF16)
    nsm = sb("nsm3", [128, 16], F32)
    tpW = ps("tpW", [128, 1024], BF16)
    mmA = ps("mmA", [128, 512], F32)
    mmB = ps("mmB", [128, 512], F32)
    with nc.Block() as blk:
        p = Prog(nc, blk)
        st = dict(s1=nsm[:, 0:1], s2=nsm[:, 1:2], mu=nsm[:, 2:3], m2=nsm[:, 3:4],
                  t=nsm[:, 4:5], var=nsm[:, 5:6], vpe=nsm[:, 6:7], std=nsm[:, 7:8],
                  rstd=nsm[:, 8:9], sq=sca)
        hs = dict(s1=hst[:, 0, :], s2=hst[:, 1, :], mu=hst[:, 2, :], m2=hst[:, 3, :],
                  t=hst[:, 4, :], var=hst[:, 5, :], vpe=hst[:, 6, :], std=hst[:, 7, :])
        r_const = Res()
        r_wln, r_wT = Res(), Res()
        r_xt = [Res(), Res()]
        r_nbf = [Res(), Res()]
        r_ntS = [Res(), Res()]
        r_qtm = [Res(), Res()]
        r_ktm = [Res(), Res()]
        r_vtm = [Res(), Res()]
        r_sca, r_hst, r_nsm, r_sT = Res(), Res(), Res(), Res()
        r_tp = [Res(), Res()]
        r_mm = [Res(), Res()]
        tp, mm = [tpA, tpB], [mmA, mmB]

        @p.step("sync", writes=[r_const], dma=True)
        def _(e, dd):
            dd(out=idb[:], in_=idb_d[:, :])
            dd(out=gb[:], in_=gb_d[:, :])

        @p.step("gpsimd", writes=[r_wln], dma=True)
        def _(e, dd):
            dd(out=wln[:, 0], in_=wq_d.rearrange("(c p) o -> p c o", p=128))
            dd(out=wln[:, 1], in_=wk_d.rearrange("(c p) o -> p c o", p=128))
            dd(out=wln[:, 2], in_=wv_d.rearrange("(c p) o -> p c o", p=128))

        for w in range(3):
            for oc in range(8):
                for ic in range(8):
                    ti = ic % 2
                    @p.step("tensor", reads=[r_wln, r_const], writes=[r_tp[ti]])
                    def _(e, w=w, oc=oc, ic=ic, ti=ti):
                        return e.transpose(tp[ti][:], wln[:, w, oc, ic * 128:(ic + 1) * 128], idb[:])
                    @p.step("vector", reads=[r_tp[ti]], writes=[r_wT])
                    def _(e, w=w, oc=oc, ic=ic, ti=ti):
                        return e.tensor_copy(wT[:, w, ic, oc * 128:(oc + 1) * 128], tp[ti][:])

        def headln(e, src_ap, dst_ap, g0):
            ch = p.chain(e)
            x3 = src_ap.rearrange("p (h x) -> p h x", x=64)
            e.tensor_reduce(hs['s1'], x3, AX.X, ALU.add)
            e.tensor_tensor(sca[:, 0:1024], src_ap, src_ap, ALU.mult)
            ch(e.tensor_reduce(hs['s2'], sca[:, 0:1024].rearrange("p (h x) -> p h x", x=64),
                               AX.X, ALU.add))
            e.tensor_scalar_mul(hs['mu'], hs['s1'], 1.0 / 64)
            ch(e.tensor_scalar_mul(hs['m2'], hs['s2'], 1.0 / 64))
            ch(e.tensor_tensor(hs['t'], hs['mu'], hs['mu'], ALU.mult))
            ch(e.tensor_tensor(hs['var'], hs['m2'], hs['t'], ALU.subtract))
            return e.tensor_scalar_add(hs['vpe'], hs['var'], 1e-5)

        def headln_apply(e, src_ap, dst_ap, g0):
            ch = p.chain(e)
            ch(e.reciprocal(hs['s1'], hs['std']))
            x3 = src_ap.rearrange("p (h x) -> p h x", x=64)
            a3 = sca[:, 0:1024].rearrange("p (h x) -> p h x", x=64)
            mu_b = hs['mu'].unsqueeze(2).broadcast_to([128, 16, 64])
            rs_b = hs['s1'].unsqueeze(2).broadcast_to([128, 16, 64])
            gam = gb[:, g0 * 64:(g0 + 1) * 64].unsqueeze(1).broadcast_to([128, 16, 64])
            bet = gb[:, (g0 + 1) * 64:(g0 + 2) * 64].unsqueeze(1).broadcast_to([128, 16, 64])
            e.tensor_tensor(a3, x3, mu_b, ALU.subtract)
            e.tensor_tensor(a3, a3, rs_b, ALU.mult)
            e.tensor_tensor(a3, a3, gam, ALU.mult)
            d3 = dst_ap.rearrange("p (h x) -> p h x", x=64)
            return e.tensor_tensor(d3, a3, bet, ALU.add)

        for t in range(NT_L1 := TC // 128):
            bi = t % 2
            @p.step("sync", writes=[r_xt[bi]], dma=True)
            def _(e, dd, t=t, bi=bi):
                dd(out=xt[:, bi, :], in_=xin[t * 128:(t + 1) * 128, :])
            @p.step("vector", reads=[r_xt[bi]], writes=[r_nsm, r_sca])
            def _(e, bi=bi):
                return emit_ln_stats(e, xt[:, bi, :], st, 1024, p.chain(e))
            @p.step("scalar", reads=[r_nsm], writes=[r_nsm])
            def _(e):
                return e.activation(out=st['std'], in_=st['vpe'], func=AF.Sqrt)
            @p.step("vector", reads=[r_xt[bi], r_nsm], writes=[r_nbf[bi], r_nsm])
            def _(e, bi=bi):
                p.chain(e)(e.reciprocal(st['rstd'], st['std']))
                return e.tensor_scalar(nbf[:, bi, :], xt[:, bi, :],
                                       st['mu'], st['rstd'], ALU.subtract, ALU.mult)
            for d in range(8):
                ti = d % 2
                @p.step("tensor", reads=[r_nbf[bi], r_const], writes=[r_tp[ti]])
                def _(e, bi=bi, d=d, ti=ti):
                    return e.transpose(tp[ti][:], nbf[:, bi, d * 128:(d + 1) * 128], idb[:])
                @p.step("vector", reads=[r_tp[ti]], writes=[r_ntS[bi]])
                def _(e, bi=bi, d=d, ti=ti):
                    return e.tensor_copy(ntS[:, bi, d * 128:(d + 1) * 128], tp[ti][:])
            for (w, dstb, dres) in ((0, qtm, r_qtm[bi]), (1, ktm, r_ktm[bi]), (2, vtm, r_vtm[bi])):
                for oi in range(2):
                    mi = oi % 2
                    @p.step("tensor", reads=[r_ntS[bi], r_wT], writes=[r_mm[mi]])
                    def _(e, bi=bi, w=w, oi=oi, mi=mi):
                        for d in range(8):
                            last = e.matmul(mm[mi][:], ntS[:, bi, d * 128:(d + 1) * 128],
                                            wT[:, w, d, oi * 512:(oi + 1) * 512],
                                            start=(d == 0), stop=(d == 7))
                        return last
                    @p.step("vector", reads=[r_mm[mi]], writes=[dres])
                    def _(e, bi=bi, oi=oi, mi=mi, dstb=dstb):
                        return e.tensor_copy(dstb[:, bi, oi * 512:(oi + 1) * 512], mm[mi][:])
            for (buf, rres, g0) in ((qtm, r_qtm[bi], 0), (ktm, r_ktm[bi], 2)):
                @p.step("vector", reads=[rres], writes=[r_hst, r_sca])
                def _(e, bi=bi, buf=buf, g0=g0):
                    return headln(e, buf[:, bi, :], None, g0)
                @p.step("scalar", reads=[r_hst], writes=[r_hst])
                def _(e):
                    return e.activation(out=hs['std'], in_=hs['vpe'], func=AF.Sqrt)
                @p.step("vector", reads=[rres, r_hst, r_const], writes=[rres, r_hst, r_sca])
                def _(e, bi=bi, buf=buf, g0=g0):
                    return headln_apply(e, buf[:, bi, :], buf[:, bi, :], g0)
            @p.step("gpsimd", reads=[r_ktm[bi], r_vtm[bi]], dma=True)
            def _(e, dd, t=t, bi=bi):
                dd(out=ktm_o[t * 128:(t + 1) * 128, :], in_=ktm[:, bi, :])
                dd(out=vtm_o[t * 128:(t + 1) * 128, :], in_=vtm[:, bi, :])
            for (buf, rres, dram) in ((qtm, r_qtm[bi], qT_o), (ktm, r_ktm[bi], kT_o)):
                for d in range(8):
                    ti = d % 2
                    @p.step("tensor", reads=[rres, r_const], writes=[r_tp[ti]])
                    def _(e, bi=bi, d=d, ti=ti, buf=buf):
                        return e.transpose(tp[ti][:], buf[:, bi, d * 128:(d + 1) * 128], idb[:])
                    @p.step("vector", reads=[r_tp[ti]], writes=[r_sT])
                    def _(e, d=d, ti=ti):
                        return e.tensor_copy(sT[:, d, :], tp[ti][:])
                @p.step("gpsimd", reads=[r_sT], dma=True)
                def _(e, dd, t=t, dram=dram):
                    dd(out=dram[:, :, t * 128:(t + 1) * 128].rearrange("c p q -> p c q"),
                       in_=sT[:])
        p.drain()
    es.close()
    return nc


def build_l1attn_scan():
    nc = bass.Bass("TRN2", target_bir_lowering=False)
    NP = 4
    qT_d = nc.dram_tensor("qT", [NP, 64, T], BF16, kind="ExternalInput").ap()
    kT_d = nc.dram_tensor("kT", [NP, 64, T], BF16, kind="ExternalInput").ap()
    ktm_d = nc.dram_tensor("ktm", [NP, T, 64], BF16, kind="ExternalInput").ap()
    vtm_d = nc.dram_tensor("vtm", [NP, T, 64], BF16, kind="ExternalInput").ap()
    tri_d = nc.dram_tensor("tri", [128, 128], BF16, kind="ExternalInput").ap()
    oT_o = nc.dram_tensor("oT_o", [NP, 64, T], BF16, kind="ExternalOutput").ap()

    NCH = T // 128
    es = ExitStack()
    sb = lambda *a: es.enter_context(nc.sbuf_tensor(*a))
    ps = lambda *a: es.enter_context(nc.psum_tensor(*a))
    tri = sb("tri_sb", [128, 128], BF16)
    qTs = sb("qTs", [64, 2, T], BF16)
    kTs = sb("kTs", [64, 2, T], BF16)
    ktms = sb("ktms", [128, 2, NCH, 64], BF16)
    vtms = sb("vtms", [128, 2, NCH, 64], BF16)
    Sf = sb("Sf", [64, 64], F32)
    Sb = sb("Sb", [64, 64], BF16)
    asb = sb("asb", [128, 2, 128], BF16)
    ost = sb("ost", [64, 2, 128], BF16)
    aps_ = ps("aps", [128, 2, 128], F32)
    ops2 = ps("ops2", [64, 2, 128], F32)
    tps = ps("tps", [64, 2, 64], F32)
    with nc.Block() as blk:
        p = Prog(nc, blk)
        r_const = Res()
        r_in = [Res(), Res()]
        r_S = Res()
        r_asb = [Res(), Res()]
        r_ost = [Res(), Res()]
        r_aps = [Res(), Res()]
        r_ops = [Res(), Res()]
        r_tps = [Res(), Res()]

        @p.step("sync", writes=[r_const], dma=True)
        def _(e, dd):
            dd(out=tri[:], in_=tri_d[:, :])

        for j in range(NP):
            bj = j % 2
            @p.step("sync", writes=[r_in[bj]], dma=True)
            def _(e, dd, j=j, bj=bj):
                dd(out=qTs[:, bj, :], in_=qT_d[j])
                dd(out=kTs[:, bj, :], in_=kT_d[j])
                dd(out=ktms[:, bj], in_=ktm_d[j].rearrange("(c p) x -> p c x", p=128))
                dd(out=vtms[:, bj], in_=vtm_d[j].rearrange("(c p) x -> p c x", p=128))
            @p.step("vector", writes=[r_S])
            def _(e):
                e.memset(Sf[:], 0.0)
                return e.memset(Sb[:], 0.0)
            for c in range(NCH):
                ci = c % 2
                @p.step("tensor", reads=[r_in[bj]], writes=[r_aps[ci]])
                def _(e, bj=bj, c=c, ci=ci):
                    return e.matmul(aps_[:, ci, :], kTs[:, bj, c * 128:(c + 1) * 128],
                                    qTs[:, bj, c * 128:(c + 1) * 128], start=True, stop=True)
                @p.step("vector", reads=[r_aps[ci], r_const], writes=[r_asb[ci]])
                def _(e, ci=ci):
                    return e.tensor_tensor(asb[:, ci, :], aps_[:, ci, :], tri[:], ALU.mult)
                @p.step("tensor", reads=[r_asb[ci], r_in[bj], r_S], writes=[r_ops[ci]])
                def _(e, bj=bj, c=c, ci=ci):
                    e.matmul(ops2[:, ci, :], vtms[:, bj, c, :], asb[:, ci, :],
                             start=True, stop=False)
                    return e.matmul(ops2[:, ci, :], Sb[:, :], qTs[:, bj, c * 128:(c + 1) * 128],
                                    start=False, stop=True)
                @p.step("tensor", reads=[r_in[bj]], writes=[r_tps[ci]])
                def _(e, bj=bj, c=c, ci=ci):
                    return e.matmul(tps[:, ci, :], ktms[:, bj, c, :], vtms[:, bj, c, :],
                                    start=True, stop=True)
                @p.step("vector", reads=[r_ops[ci]], writes=[r_ost[ci]])
                def _(e, ci=ci):
                    return e.tensor_copy(ost[:, ci, :], ops2[:, ci, :])
                @p.step("gpsimd", reads=[r_ost[ci]], dma=True)
                def _(e, dd, j=j, c=c, ci=ci):
                    dd(out=oT_o[j][:, c * 128:(c + 1) * 128], in_=ost[:, ci, :])
                @p.step("vector", reads=[r_tps[ci], r_S], writes=[r_S])
                def _(e, ci=ci):
                    ch = p.chain(e)
                    ch(e.tensor_tensor(Sf[:], Sf[:], tps[:, ci, :], ALU.add))
                    return e.tensor_copy(Sb[:], Sf[:])
        p.drain()
    es.close()
    return nc


def build_l1attn_post():
    nc = bass.Bass("TRN2", target_bir_lowering=False)
    xin = nc.dram_tensor("xin", [TC, D], F32, kind="ExternalInput").ap()
    oT_d = nc.dram_tensor("oT", [8, 128, TC], BF16, kind="ExternalInput").ap()
    wo_d = nc.dram_tensor("wo", [D, D], F32, kind="ExternalInput").ap()
    idb_d = nc.dram_tensor("idb", [128, 128], BF16, kind="ExternalInput").ap()
    xout = nc.dram_tensor("xout", [TC, D], F32, kind="ExternalOutput").ap()

    es = ExitStack()
    sb = lambda *a: es.enter_context(nc.sbuf_tensor(*a))
    ps = lambda *a: es.enter_context(nc.psum_tensor(*a))
    idb = sb("idb_sb", [128, 128], BF16)
    wln = sb("wln", [128, 8, 1024], BF16)
    wT = sb("wT", [128, 8, 1024], BF16)
    oTs = sb("oTs", [128, 8, TC], BF16)
    xt = sb("xt", [128, 2, D], F32)
    xo = sb("xo", [128, 2, D], F32)
    tpW = ps("tpW", [128, 1024], BF16)
    mmA = ps("mmA", [128, 512], F32)
    mmB = ps("mmB", [128, 512], F32)
    with nc.Block() as blk:
        p = Prog(nc, blk)
        r_const, r_wln, r_wT, r_oTs = Res(), Res(), Res(), Res()
        r_xt = [Res(), Res()]
        r_xo = [Res(), Res()]
        r_tp = [Res(), Res()]
        r_mm = [Res(), Res()]
        tp, mm = [tpA, tpB], [mmA, mmB]

        @p.step("sync", writes=[r_const, r_oTs], dma=True)
        def _(e, dd):
            dd(out=idb[:], in_=idb_d[:, :])
            dd(out=oTs[:], in_=oT_d.rearrange("c p q -> p c q"))

        @p.step("gpsimd", writes=[r_wln], dma=True)
        def _(e, dd):
            dd(out=wln[:], in_=wo_d.rearrange("(c p) o -> p c o", p=128))

        for oc in range(8):
            for ic in range(8):
                ti = ic % 2
                @p.step("tensor", reads=[r_wln, r_const], writes=[r_tp[ti]])
                def _(e, oc=oc, ic=ic, ti=ti):
                    return e.transpose(tp[ti][:], wln[:, oc, ic * 128:(ic + 1) * 128], idb[:])
                @p.step("vector", reads=[r_tp[ti]], writes=[r_wT])
                def _(e, oc=oc, ic=ic, ti=ti):
                    return e.tensor_copy(wT[:, ic, oc * 128:(oc + 1) * 128], tp[ti][:])

        for t in range(TC // 128):
            bi = t % 2
            @p.step("sync", writes=[r_xt[bi]], dma=True)
            def _(e, dd, t=t, bi=bi):
                dd(out=xt[:, bi, :], in_=xin[t * 128:(t + 1) * 128, :])
            for oi in range(2):
                mi = oi % 2
                @p.step("tensor", reads=[r_oTs, r_wT], writes=[r_mm[mi]])
                def _(e, t=t, oi=oi, mi=mi):
                    for d in range(8):
                        last = e.matmul(mm[mi][:], oTs[:, d, t * 128:(t + 1) * 128],
                                        wT[:, d, oi * 512:(oi + 1) * 512],
                                        start=(d == 0), stop=(d == 7))
                    return last
                @p.step("vector", reads=[r_mm[mi], r_xt[bi]], writes=[r_xo[bi]])
                def _(e, oi=oi, mi=mi, bi=bi):
                    return e.tensor_tensor(xo[:, bi, oi * 512:(oi + 1) * 512], mm[mi][:],
                                           xt[:, bi, oi * 512:(oi + 1) * 512], ALU.add)
            @p.step("gpsimd", reads=[r_xo[bi]], dma=True)
            def _(e, dd, t=t, bi=bi):
                dd(out=xout[t * 128:(t + 1) * 128, :], in_=xo[:, bi, :])
        p.drain()
    es.close()
    return nc


def run_l1attn(x_full, Wq, Wk, Wv, Wo, qg, qb_, kg, kb_):
    nc = _get("l1attn_pre", build_l1attn_pre)
    gbrep = np.concatenate([np.broadcast_to(v.reshape(1, 64), (128, 64)) for v in
                            (qg, qb_, kg, kb_)], axis=1).astype(np.float32)
    xs = x_full.reshape(8, TC, D)
    im = [{"xin": np.ascontiguousarray(xs[c]), "wq": Wq, "wk": Wk, "wv": Wv,
           "gb": gbrep, "idb": _EYE_BF} for c in range(NCORES)]
    r1 = run_bass_kernel_spmd(nc, im, core_ids=list(range(NCORES))).results
    qT = np.zeros((2, 16, 64, T), BF)
    kT = np.zeros((2, 16, 64, T), BF)
    ktm = np.zeros((2, T, 16, 64), BF)
    vtm = np.zeros((2, T, 16, 64), BF)
    for c in range(NCORES):
        b, sl = c // 4, c % 4
        cols = slice(sl * TC, (sl + 1) * TC)
        for h in range(16):
            qT[b, h, :, cols] = r1[c]["qT_o"][h // 2, (h % 2) * 64:(h % 2) * 64 + 64, :]
            kT[b, h, :, cols] = r1[c]["kT_o"][h // 2, (h % 2) * 64:(h % 2) * 64 + 64, :]
        ktm[b, cols.start:cols.stop] = r1[c]["ktm_o"].reshape(TC, 16, 64)
        vtm[b, cols.start:cols.stop] = r1[c]["vtm_o"].reshape(TC, 16, 64)
    nc2 = _get("l1attn_scan", build_l1attn_scan)
    tri = np.triu(np.ones((128, 128), np.float32)).astype(BF)
    im2 = []
    for c in range(NCORES):
        prs = [4 * c + j for j in range(4)]
        im2.append({
            "qT": np.stack([qT[pp // 16, pp % 16] for pp in prs]),
            "kT": np.stack([kT[pp // 16, pp % 16] for pp in prs]),
            "ktm": np.stack([np.ascontiguousarray(ktm[pp // 16, :, pp % 16]) for pp in prs]),
            "vtm": np.stack([np.ascontiguousarray(vtm[pp // 16, :, pp % 16]) for pp in prs]),
            "tri": tri,
        })
    r2 = run_bass_kernel_spmd(nc2, im2, core_ids=list(range(NCORES))).results
    oT = np.zeros((2, 16, 64, T), BF)
    for c in range(NCORES):
        for j in range(4):
            pp = 4 * c + j
            oT[pp // 16, pp % 16] = r2[c]["oT_o"][j]
    nc3 = _get("l1attn_post", build_l1attn_post)
    im3 = []
    for c in range(NCORES):
        b, sl = c // 4, c % 4
        cols = slice(sl * TC, (sl + 1) * TC)
        oTi = np.zeros((8, 128, TC), BF)
        for h in range(16):
            oTi[h // 2, (h % 2) * 64:(h % 2) * 64 + 64, :] = oT[b, h][:, cols]
        im3.append({"xin": np.ascontiguousarray(xs[c]), "oT": oTi, "wo": Wo, "idb": _EYE_BF})
    r3 = run_bass_kernel_spmd(nc3, im3, core_ids=list(range(NCORES))).results
    return np.stack([r["xout"] for r in r3]).reshape(2, 8192, 1024)





def build_l1ffn_pre(debug=False):
    nc = bass.Bass("TRN2", target_bir_lowering=False)
    xin = nc.dram_tensor("xin", [TC, D], F32, kind="ExternalInput").ap()
    qw_d = nc.dram_tensor("qw", [D, D], F32, kind="ExternalInput").ap()
    kw_d = nc.dram_tensor("kw", [D, D], F32, kind="ExternalInput").ap()
    selw_d = nc.dram_tensor("selw", [128, D], F32, kind="ExternalInput").ap()
    vw_d = nc.dram_tensor("vw", [64, D, 128], F32, kind="ExternalInput").ap()
    rsc_d = nc.dram_tensor("rsc", [128, 1], F32, kind="ExternalInput").ap()
    idb_d = nc.dram_tensor("idb", [128, 128], BF16, kind="ExternalInput").ap()
    qT_o = nc.dram_tensor("qT_o", [8, 128, TC], BF16, kind="ExternalOutput").ap()
    kT_o = nc.dram_tensor("kT_o", [8, 128, TC], BF16, kind="ExternalOutput").ap()
    V_o = nc.dram_tensor("V_o", [TC, 8, 128], BF16, kind="ExternalOutput").ap()
    go_o = nc.dram_tensor("go_o", [TC, 64], F32, kind="ExternalOutput").ap()
    if debug:
        dbg_probs = nc.dram_tensor("dbg_probs", [128, 16 * 128], F32, kind="ExternalOutput").ap()
        dbg_rank = nc.dram_tensor("dbg_rank", [128, 16 * 128], F32, kind="ExternalOutput").ap()
        dbg_gates = nc.dram_tensor("dbg_gates", [128, 16 * 128], F32, kind="ExternalOutput").ap()

    NT16 = TC // 128
    es = ExitStack()
    sb = lambda *a: es.enter_context(nc.sbuf_tensor(*a))
    ps = lambda *a: es.enter_context(nc.psum_tensor(*a))
    idb = sb("idb_sb", [128, 128], BF16)
    rsc = sb("rsc_sb", [128, 1], F32)
    selT = sb("selT", [128, 8, 128], BF16)
    selw = sb("selw_sb", [128, 1024], BF16)
    nT = sb("nT", [128, 8, TC], BF16)
    qst = sb("qst", [128, 2, 512], BF16)
    probs = sb("probs", [128, NT16, 128], F32)
    rank = sb("rank", [128, NT16, 128], F32)
    gates = sb("gates", [128, NT16, 16, 8], F32)
    Vb = sb("Vb", [128, NT16, 8, 128], BF16)
    xt = sb("xt", [128, 2, D], F32)
    nbf = sb("nbf", [128, 2, D], BF16)
    nf32 = sb("nf32", [128, 2, D], F32)
    wvq = sb("wvq", [128, 2, 8, 512], BF16)
    nsm = sb("nsm4", [128, 16], F32)
    sca = sb("sca", [128, 2304], F32)
    tpW = ps("tpW", [128, 1024], BF16)
    mmA = ps("mmA", [128, 512], F32)
    mmB = ps("mmB", [128, 512], F32)
    lgp = ps("lgp", [128, 128], F32)
    with nc.Block() as blk:
        p = Prog(nc, blk)
        st = dict(s1=nsm[:, 0:1], s2=nsm[:, 1:2], mu=nsm[:, 2:3], m2=nsm[:, 3:4],
                  t=nsm[:, 4:5], var=nsm[:, 5:6], vpe=nsm[:, 6:7], std=nsm[:, 7:8],
                  rstd=nsm[:, 8:9], sq=sca)
        r_const, r_selw, r_selT = Res(), Res(), Res()
        r_nT = Res()
        r_probs, r_rank, r_gates, r_Vb = Res(), Res(), Res(), Res()
        r_xt = [Res(), Res()]
        r_nbf = [Res(), Res()]
        r_wvq = [Res(), Res()]
        r_nsm, r_sca = Res(), Res()
        r_tp = [Res(), Res()]
        r_mm = [Res(), Res()]
        r_lgp = Res()
        tp, mm = [tpA, tpB], [mmA, mmB]

        @p.step("sync", writes=[r_const], dma=True)
        def _(e, dd):
            dd(out=idb[:], in_=idb_d[:, :])
            dd(out=rsc[:], in_=rsc_d[:, :])

        @p.step("gpsimd", writes=[r_selw], dma=True)
        def _(e, dd):
            dd(out=selw[:], in_=selw_d[:, :])

        for ic in range(8):
            ti = ic % 2
            @p.step("tensor", reads=[r_selw, r_const], writes=[r_tp[ti]])
            def _(e, ic=ic, ti=ti):
                return e.transpose(tp[ti][:], selw[:, ic * 128:(ic + 1) * 128], idb[:])
            @p.step("vector", reads=[r_tp[ti]], writes=[r_selT])
            def _(e, ic=ic, ti=ti):
                return e.tensor_copy(selT[:, ic, :], tp[ti][:])

        es1 = ExitStack()
        wqkT = es1.enter_context(nc.sbuf_tensor("wqkT", [128, 2, 8, 1024], BF16))
        wln = es1.enter_context(nc.sbuf_tensor("wln1", [128, 8, 1024], BF16))
        r_wqkT, r_wln = Res(), Res()
        for w, wd in ((0, qw_d), (1, kw_d)):
            @p.step("gpsimd", writes=[r_wln], dma=True)
            def _(e, dd, wd=wd):
                dd(out=wln[:], in_=wd.rearrange("(c p) o -> p c o", p=128))
            for oc in range(8):
                for ic in range(8):
                    ti = ic % 2
                    @p.step("tensor", reads=[r_wln, r_const], writes=[r_tp[ti]])
                    def _(e, oc=oc, ic=ic, ti=ti):
                        return e.transpose(tp[ti][:], wln[:, oc, ic * 128:(ic + 1) * 128], idb[:])
                    @p.step("vector", reads=[r_tp[ti]], writes=[r_wqkT])
                    def _(e, w=w, oc=oc, ic=ic, ti=ti):
                        return e.tensor_copy(wqkT[:, w, ic, oc * 128:(oc + 1) * 128], tp[ti][:])

        for t in range(NT16):
            bi = t % 2
            @p.step("sync", writes=[r_xt[bi]], dma=True)
            def _(e, dd, t=t, bi=bi):
                dd(out=xt[:, bi, :], in_=xin[t * 128:(t + 1) * 128, :])
            @p.step("vector", reads=[r_xt[bi]], writes=[r_nsm, r_sca])
            def _(e, bi=bi):
                return emit_ln_stats(e, xt[:, bi, :], st, 1024, p.chain(e))
            @p.step("scalar", reads=[r_nsm], writes=[r_nsm])
            def _(e):
                return e.activation(out=st['std'], in_=st['vpe'], func=AF.Sqrt)
            @p.step("vector", reads=[r_xt[bi], r_nsm], writes=[r_nbf[bi], r_nsm])
            def _(e, bi=bi):
                p.chain(e)(e.reciprocal(st['rstd'], st['std']))
                return e.tensor_scalar(nbf[:, bi, :], xt[:, bi, :],
                                       st['mu'], st['rstd'], ALU.subtract, ALU.mult)
            for d in range(8):
                ti = d % 2
                @p.step("tensor", reads=[r_nbf[bi], r_const], writes=[r_tp[ti]])
                def _(e, bi=bi, d=d, ti=ti):
                    return e.transpose(tp[ti][:], nbf[:, bi, d * 128:(d + 1) * 128], idb[:])
                @p.step("vector", reads=[r_tp[ti]], writes=[r_nT])
                def _(e, d=d, ti=ti, t=t):
                    return e.tensor_copy(nT[:, d, t * 128:(t + 1) * 128], tp[ti][:])
            @p.step("tensor", reads=[r_nT, r_selT], writes=[r_lgp])
            def _(e, t=t):
                for d in range(8):
                    last = e.matmul(lgp[:], nT[:, d, t * 128:(t + 1) * 128],
                                    selT[:, d, :], start=(d == 0), stop=(d == 7))
                return last
            @p.step("scalar", reads=[r_lgp], writes=[r_probs])
            def _(e, t=t):
                return e.activation(out=probs[:, t, :], in_=lgp[:], func=AF.Sigmoid)

        r_qst = [Res(), Res()]
        for w, dram in ((0, qT_o), (1, kT_o)):
            for hp in range(8):
                for tq in range(4):
                    mi = (hp * 4 + tq) % 2
                    @p.step("tensor", reads=[r_nT, r_wqkT], writes=[r_mm[mi]])
                    def _(e, w=w, hp=hp, tq=tq, mi=mi):
                        for d in range(8):
                            last = e.matmul(mm[mi][:], wqkT[:, w, d, hp * 128:(hp + 1) * 128],
                                            nT[:, d, tq * 512:(tq + 1) * 512],
                                            start=(d == 0), stop=(d == 7))
                        return last
                    @p.step("vector", reads=[r_mm[mi]], writes=[r_qst[mi]])
                    def _(e, mi=mi):
                        return e.tensor_copy(qst[:, mi, :], mm[mi][:])
                    @p.step("gpsimd", reads=[r_qst[mi]], dma=True)
                    def _(e, dd, hp=hp, tq=tq, mi=mi, dram=dram):
                        dd(out=dram[hp][:, tq * 512:(tq + 1) * 512], in_=qst[:, mi, :])
        es1.close()

        @p.step("vector", reads=[r_probs, r_const], writes=[r_rank, r_gates, r_sca])
        def _(e):
            ch = p.chain(e)
            p3 = probs[:].rearrange("p t (g e) -> p (t g) e", e=8)
            r3 = rank[:].rearrange("p t (g e) -> p (t g) e", e=8)
            e.memset(rank[:], 0.0)
            NG = NT16 * 16
            t1 = sca[:, 0:NG * 8].rearrange("p (g e) -> p g e", e=8)
            for dlt in range(1, 8):
                wdt = 8 - dlt
                a = p3[:, :, 0:wdt]
                b = p3[:, :, dlt:8]
                ch(e.tensor_tensor(t1[:, :, 0:wdt], b, a, ALU.is_gt))
                ch(e.tensor_tensor(r3[:, :, 0:wdt], r3[:, :, 0:wdt], t1[:, :, 0:wdt], ALU.add))
                ch(e.tensor_tensor(t1[:, :, 0:wdt], a, b, ALU.is_ge))
                ch(e.tensor_tensor(r3[:, :, dlt:8], r3[:, :, dlt:8], t1[:, :, 0:wdt], ALU.add))
            ch(e.tensor_single_scalar(r3, r3, 2.0, ALU.is_lt))
            ch(e.tensor_tensor(t1, p3, r3, ALU.mult))
            s_ = sca[:, NG * 8:NG * 9]
            ch(e.tensor_reduce(s_.rearrange("p (g x) -> p g x", x=1), t1, AX.X, ALU.add))
            ch(e.tensor_scalar_max(s_, s_, 1e-9))
            ch(e.reciprocal(s_, s_))
            ch(e.tensor_scalar_mul(s_, s_, rsc[:, 0:1]))
            sb_ = s_.rearrange("p (g x) -> p g x", x=1).broadcast_to([128, NG, 8])
            g4 = gates[:].rearrange("p t g e -> p (t g) e")
            return e.tensor_tensor(g4, t1, sb_, ALU.mult)

        @p.step("gpsimd", reads=[r_gates], dma=True)
        def _(e, dd):
            dd(out=go_o.rearrange("(t p) (h e) -> p t h e", p=128, e=8),
               in_=gates[:, :, 8:16, :])
        if debug:
            @p.step("gpsimd", reads=[r_probs, r_rank, r_gates], dma=True)
            def _(e, dd):
                dd(out=dbg_probs.rearrange("p (t x) -> p t x", x=128), in_=probs[:])
                dd(out=dbg_rank.rearrange("p (t x) -> p t x", x=128), in_=rank[:])
                dd(out=dbg_gates.rearrange("p (t h e) -> p t h e", h=16, e=8), in_=gates[:])

        for Q in range(16):
            bq = Q % 2
            @p.step("gpsimd", writes=[r_wvq[bq]], dma=True)
            def _(e, dd, Q=Q, bq=bq):
                for l in range(4):
                    dd(out=wvq[:, bq, :, l * 128:(l + 1) * 128],
                       in_=vw_d[4 * Q + l].rearrange("(c p) x -> p c x", p=128))
            for t in range(NT16):
                mi = t % 2
                @p.step("tensor", reads=[r_nT, r_wvq[bq]], writes=[r_mm[mi]])
                def _(e, Q=Q, t=t, mi=mi, bq=bq):
                    for d in range(8):
                        last = e.matmul(mm[mi][:], nT[:, d, t * 128:(t + 1) * 128],
                                        wvq[:, bq, d, :], start=(d == 0), stop=(d == 7))
                    return last
                @p.step("vector", reads=[r_mm[mi], r_gates], writes=[r_Vb])
                def _(e, Q=Q, t=t, mi=mi):
                    last = None
                    for l in range(4):
                        he = 4 * Q + l
                        h, ei = he // 8, he % 8
                        gcol = gates[:, t, h, ei:ei + 1]
                        dst = Vb[:, t, h, :]
                        if ei == 0:
                            last = e.tensor_scalar_mul(dst, mm[mi][:, l * 128:(l + 1) * 128], gcol)
                        else:
                            last = e.scalar_tensor_tensor(dst, mm[mi][:, l * 128:(l + 1) * 128],
                                                          gcol, dst, ALU.mult, ALU.add)
                    return last

        @p.step("gpsimd", reads=[r_Vb], dma=True)
        def _(e, dd):
            dd(out=V_o.rearrange("(t p) h x -> p t h x", p=128), in_=Vb[:])
        p.drain()
    es.close()
    return nc


def build_l1ffn_flash(debug=False):
    nc = bass.Bass("TRN2", target_bir_lowering=False)
    NP = 2
    SCALE = float(1.0 / np.sqrt(128.0))
    qT_d = nc.dram_tensor("qT", [NP, 128, T], BF16, kind="ExternalInput").ap()
    kT_d = nc.dram_tensor("kT", [NP, 128, T], BF16, kind="ExternalInput").ap()
    V_d = nc.dram_tensor("V", [NP, T, 129], BF16, kind="ExternalInput").ap()
    ctxT_o = nc.dram_tensor("ctxT_o", [NP, 128, T], BF16, kind="ExternalOutput").ap()
    rs_o = nc.dram_tensor("rs_o", [NP, T], F32, kind="ExternalOutput").ap()
    if debug:
        dbg_nrm = nc.dram_tensor("dbg_nrm", [1, 2 * T], F32, kind="ExternalOutput").ap()
        dbg_smax = nc.dram_tensor("dbg_smax", [1, 8], F32, kind="ExternalOutput").ap()
        dbg_bcol = nc.dram_tensor("dbg_bcol", [128, 1], F32, kind="ExternalOutput").ap()

    NKB = T // 128
    es = ExitStack()
    sb = lambda *a: es.enter_context(nc.sbuf_tensor(*a))
    ps = lambda *a: es.enter_context(nc.psum_tensor(*a))
    qs = sb("qs", [128, 2, T], BF16)
    ks = sb("ks", [128, 2, T], BF16)
    vs = sb("vs", [128, 2, NKB, 129], BF16)
    onec = sb("onec", [128, 1], BF16)
    oner = sb("oner", [1, 128], BF16)
    sqb = sb("sqb", [128, 512], BF16)
    nrm = sb("nrm", [1, 2 * T], F32)
    smax = sb("smax", [1, 8], F32)
    bcol = sb("bcol", [128, 1], F32)
    pbt = sb("pbt", [128, 2, 512], BF16)
    cst = sb("cst", [128, 512], BF16)
    rst = sb("rst", [1, 512], F32)
    scp = ps("scp", [128, 2, 512], F32)
    ctxp = ps("ctxp", [128, 512], F32)
    rsp = ps("rsp", [1, 512], F32)
    nrp = ps("nrp", [1, 512], F32)
    mbp = ps("mbp", [128, 1], F32)
    with nc.Block() as blk:
        p = Prog(nc, blk)
        r_const = Res()
        r_in = [Res(), Res()]
        r_sqb, r_nrm, r_smax, r_bcol = Res(), Res(), Res(), Res()
        r_pbt = [Res(), Res()]
        r_cst, r_rst = Res(), Res()
        r_scp = [Res(), Res()]
        r_ctxp, r_rsp, r_nrp, r_mbp = Res(), Res(), Res(), Res()

        @p.step("vector", writes=[r_const])
        def _(e):
            e.memset(onec[:], 1.0)
            return e.memset(oner[:], 1.0)

        for j in range(NP):
            bj = j % 2
            @p.step("sync", writes=[r_in[bj]], dma=True)
            def _(e, dd, j=j, bj=bj):
                dd(out=qs[:, bj, :], in_=qT_d[j])
                dd(out=ks[:, bj, :], in_=kT_d[j])
                dd(out=vs[:, bj], in_=V_d[j].rearrange("(c p) x -> p c x", p=128))
            for w, buf in ((0, qs), (1, ks)):
                for sl in range(16):
                    @p.step("scalar", reads=[r_in[bj]], writes=[r_sqb])
                    def _(e, bj=bj, buf=buf, sl=sl):
                        return e.activation(out=sqb[:], in_=buf[:, bj, sl * 512:(sl + 1) * 512],
                                            func=AF.Square)
                    @p.step("tensor", reads=[r_sqb, r_const], writes=[r_nrp])
                    def _(e):
                        return e.matmul(nrp[:], onec[:], sqb[:], start=True, stop=True)
                    @p.step("vector", reads=[r_nrp], writes=[r_nrm])
                    def _(e, w=w, sl=sl):
                        return e.tensor_copy(nrm[0:1, w * T + sl * 512: w * T + (sl + 1) * 512],
                                             nrp[:])
                @p.step("vector", reads=[r_nrm], writes=[r_smax])
                def _(e, w=w):
                    return e.reduce_max(smax[0:1, w:w + 1], nrm[0:1, w * T:(w + 1) * T], AX.X)
            @p.step("vector", reads=[r_smax], writes=[r_smax])
            def _(e):
                ch = p.chain(e)
                ch(e.tensor_tensor(smax[0:1, 2:3], smax[0:1, 0:1], smax[0:1, 1:2], ALU.mult))
                return e.tensor_scalar_mul(smax[0:1, 3:4], smax[0:1, 2:3], SCALE * SCALE)
            @p.step("scalar", reads=[r_smax], writes=[r_smax])
            def _(e):
                return e.activation(out=smax[0:1, 4:5], in_=smax[0:1, 3:4], func=AF.Sqrt)
            @p.step("vector", reads=[r_smax], writes=[r_smax])
            def _(e):
                return e.tensor_scalar(smax[0:1, 5:6], smax[0:1, 4:5], -1.0, None,
                                       ALU.mult, out_dtype_hint := ALU.bypass)
            @p.step("vector", reads=[r_smax], writes=[r_smax])
            def _(e):
                return e.tensor_copy(smax[0:1, 6:7].bitcast(F32), smax[0:1, 5:6])
            @p.step("vector", reads=[r_smax], writes=[r_sqb])
            def _(e):
                return e.tensor_copy(sqb[0:1, 0:1], smax[0:1, 5:6])
            @p.step("tensor", reads=[r_sqb, r_const], writes=[r_mbp])
            def _(e):
                return e.matmul(mbp[:], oner[:, :], sqb[0:1, 0:1], start=True, stop=True)
            @p.step("vector", reads=[r_mbp], writes=[r_bcol])
            def _(e):
                return e.tensor_copy(bcol[:], mbp[:])
            if debug and j == 0:
                @p.step("gpsimd", reads=[r_nrm, r_smax, r_bcol], dma=True)
                def _(e, dd):
                    dd(out=dbg_nrm[:, :], in_=nrm[:])
                    dd(out=dbg_smax[:, :], in_=smax[:])
                    dd(out=dbg_bcol[:, :], in_=bcol[:])
            for qt in range(16):
                for kb in range(NKB):
                    ci = kb % 2
                    @p.step("tensor", reads=[r_in[bj]], writes=[r_scp[ci]])
                    def _(e, bj=bj, qt=qt, kb=kb, ci=ci):
                        return e.matmul(scp[:, ci, :], ks[:, bj, kb * 128:(kb + 1) * 128],
                                        qs[:, bj, qt * 512:(qt + 1) * 512], start=True, stop=True)
                    @p.step("scalar", reads=[r_scp[ci], r_bcol], writes=[r_pbt[ci]])
                    def _(e, ci=ci):
                        return e.activation(out=pbt[:, ci, :], in_=scp[:, ci, :], func=AF.Exp,
                                            bias=bcol[:, 0:1], scale=SCALE)
                    @p.step("tensor", reads=[r_pbt[ci], r_in[bj]],
                            writes=[r_ctxp] if kb < NKB - 1 else [r_ctxp],
                            )
                    def _(e, bj=bj, kb=kb, ci=ci):
                        return e.matmul(ctxp[:], vs[:, bj, kb, 0:128], pbt[:, ci, :],
                                        start=(kb == 0), stop=(kb == NKB - 1))
                    @p.step("tensor", reads=[r_pbt[ci], r_in[bj]], writes=[r_rsp])
                    def _(e, bj=bj, kb=kb, ci=ci):
                        return e.matmul(rsp[:], vs[:, bj, kb, 128:129], pbt[:, ci, :],
                                        start=(kb == 0), stop=(kb == NKB - 1))
                @p.step("vector", reads=[r_ctxp], writes=[r_cst])
                def _(e):
                    return e.tensor_copy(cst[:], ctxp[:])
                @p.step("vector", reads=[r_rsp], writes=[r_rst])
                def _(e):
                    return e.tensor_copy(rst[:], rsp[:])
                @p.step("gpsimd", reads=[r_cst, r_rst], dma=True)
                def _(e, dd, j=j, qt=qt):
                    dd(out=ctxT_o[j][:, qt * 512:(qt + 1) * 512], in_=cst[:])
                    dd(out=rs_o[j:j + 1, qt * 512:(qt + 1) * 512], in_=rst[:])
        p.drain()
    es.close()
    return nc


def build_l1ffn_post():
    nc = bass.Bass("TRN2", target_bir_lowering=False)
    xin = nc.dram_tensor("xin", [TC, D], F32, kind="ExternalInput").ap()
    ct_d = nc.dram_tensor("ct", [8, 128, TC], BF16, kind="ExternalInput").ap()
    go_d = nc.dram_tensor("go", [TC, 64], F32, kind="ExternalInput").ap()
    rs_d = nc.dram_tensor("rs", [TC, 8], F32, kind="ExternalInput").ap()
    ow_d = nc.dram_tensor("ow", [64, 128, D], F32, kind="ExternalInput").ap()
    repl_d = nc.dram_tensor("repl", [64, 64 * 128], BF16, kind="ExternalInput").ap()
    idb_d = nc.dram_tensor("idb", [128, 128], BF16, kind="ExternalInput").ap()
    xout = nc.dram_tensor("xout", [TC, D], F32, kind="ExternalOutput").ap()

    NT16 = TC // 128
    es = ExitStack()
    sb = lambda *a: es.enter_context(nc.sbuf_tensor(*a))
    ps = lambda *a: es.enter_context(nc.psum_tensor(*a))
    idb = sb("idb_sb", [128, 128], BF16)
    repl = sb("repl_sb", [64, 64 * 128], BF16)
    ctx = sb("ctx", [128, 8, TC], BF16)
    go4 = sb("go4", [128, NT16, 8, 8], F32)
    rcp = sb("rcp", [128, NT16, 8], F32)
    gob = sb("gob", [128, NT16, 64], BF16)
    goT = sb("goT", [64, TC], BF16)
    cT = sb("cT", [128, 64, 512], BF16)
    gbs = sb("gbs", [128, 512], BF16)
    owt = sb("owt", [128, 2, 512], BF16)
    oTsb = sb("oTsb", [128, 8, 512], BF16)
    xt = sb("xt", [128, 2, D], F32)
    xo = sb("xo", [128, 2, D], F32)
    odps = ps("odps", [128, 4, 512], F32)
    gbp = ps("gbp", [128, 512], F32)
    tpA = ps("tpA", [128, 128], BF16)
    tpB = ps("tpB", [128, 128], BF16)
    with nc.Block() as blk:
        p = Prog(nc, blk)
        r_const, r_ctx, r_go4, r_rcp, r_gob, r_goT = Res(), Res(), Res(), Res(), Res(), Res()
        r_cT, r_gbs, r_oTsb = Res(), Res(), Res()
        r_owt = [Res(), Res()]
        r_xt = [Res(), Res()]
        r_xo = [Res(), Res()]
        r_odps, r_gbp = Res(), Res()
        r_tp = [Res(), Res()]
        tp = [tpA, tpB]

        @p.step("sync", writes=[r_const, r_ctx], dma=True)
        def _(e, dd):
            dd(out=idb[:], in_=idb_d[:, :])
            dd(out=repl[:], in_=repl_d[:, :])
            dd(out=ctx[:], in_=ct_d.rearrange("c p q -> p c q"))

        @p.step("gpsimd", writes=[r_go4, r_rcp], dma=True)
        def _(e, dd):
            dd(out=go4[:], in_=go_d.rearrange("(t p) (h e) -> p t h e", p=128, e=8))
            dd(out=rcp[:], in_=rs_d.rearrange("(t p) h -> p t h", p=128))

        @p.step("vector", reads=[r_go4, r_rcp], writes=[r_rcp, r_gob])
        def _(e):
            ch = p.chain(e)
            ch(e.reciprocal(rcp[:], rcp[:]))
            rb = rcp[:].unsqueeze(3).broadcast_to([128, NT16, 8, 8])
            return e.tensor_tensor(gob[:].rearrange("p t (h e) -> p t h e", e=8),
                                   go4[:], rb, ALU.mult)
        for t in range(NT16):
            ti = t % 2
            @p.step("tensor", reads=[r_gob, r_const], writes=[r_tp[ti]])
            def _(e, t=t, ti=ti):
                return e.transpose(tp[ti][0:64, :], gob[:, t, :], idb[:])
            @p.step("vector", reads=[r_tp[ti]], writes=[r_goT])
            def _(e, t=t, ti=ti):
                return e.tensor_copy(goT[:, t * 128:(t + 1) * 128], tp[ti][0:64, :])

        for tsl in range(4):
            for he in range(64):
                h = he // 8
                @p.step("tensor", reads=[r_goT, r_const], writes=[r_gbp])
                def _(e, he=he, tsl=tsl):
                    return e.matmul(gbp[:], repl[:, he * 128:(he + 1) * 128],
                                    goT[:, tsl * 512:(tsl + 1) * 512], start=True, stop=True)
                @p.step("vector", reads=[r_gbp], writes=[r_gbs])
                def _(e):
                    return e.tensor_copy(gbs[:], gbp[:])
                @p.step("vector", reads=[r_gbs, r_ctx], writes=[r_cT])
                def _(e, he=he, h=h, tsl=tsl):
                    return e.tensor_tensor(cT[:, he, :], ctx[:, h, tsl * 512:(tsl + 1) * 512],
                                           gbs[:], ALU.mult)
            for dh in range(2):
                for he in range(64):
                    bo = he % 2
                    @p.step("gpsimd", writes=[r_owt[bo]], dma=True)
                    def _(e, dd, he=he, dh=dh, bo=bo):
                        dd(out=owt[:, bo], in_=ow_d[he][:, dh * 512:(dh + 1) * 512])
                    @p.step("tensor", reads=[r_owt[bo], r_cT], writes=[r_odps])
                    def _(e, he=he, dh=dh, bo=bo):
                        for ds in range(4):
                            last = e.matmul(odps[:, ds, :], owt[:, bo, ds * 128:(ds + 1) * 128],
                                            cT[:, he, :], start=(he == 0), stop=(he == 63))
                        return last
                @p.step("vector", reads=[r_odps], writes=[r_oTsb])
                def _(e, dh=dh):
                    return e.tensor_copy(oTsb[:, 4 * dh:4 * dh + 4, :], odps[:])
            for tt in range(4):
                t = tsl * 4 + tt
                bi = t % 2
                @p.step("sync", writes=[r_xt[bi]], dma=True)
                def _(e, dd, t=t, bi=bi):
                    dd(out=xt[:, bi, :], in_=xin[t * 128:(t + 1) * 128, :])
                for d in range(8):
                    ti = d % 2
                    @p.step("tensor", reads=[r_oTsb, r_const], writes=[r_tp[ti]])
                    def _(e, d=d, tt=tt, ti=ti):
                        return e.transpose(tp[ti][:], oTsb[:, d, tt * 128:(tt + 1) * 128], idb[:])
                    @p.step("vector", reads=[r_tp[ti], r_xt[bi]], writes=[r_xo[bi]])
                    def _(e, d=d, ti=ti, bi=bi):
                        return e.tensor_tensor(xo[:, bi, d * 128:(d + 1) * 128], tp[ti][:],
                                               xt[:, bi, d * 128:(d + 1) * 128], ALU.add)
                @p.step("gpsimd", reads=[r_xo[bi]], dma=True)
                def _(e, dd, t=t, bi=bi):
                    dd(out=xout[t * 128:(t + 1) * 128, :], in_=xo[:, bi, :])
        p.drain()
    es.close()
    return nc


def run_l1ffn(x_full, q_w, k_w, v_w, o_w, sel_v_w, sel_o_w, route_scale):
    nc = _get("l1ffn_pre", build_l1ffn_pre)
    xs = x_full.reshape(8, TC, D)
    selw = np.ascontiguousarray(np.concatenate([sel_v_w, sel_o_w], 0), dtype=np.float32)
    rscr = np.broadcast_to(route_scale.reshape(1, 1), (128, 1)).astype(np.float32)
    im = [{"xin": np.ascontiguousarray(xs[c]), "qw": q_w, "kw": k_w, "selw": selw,
           "vw": v_w, "rsc": rscr, "idb": _EYE_BF} for c in range(NCORES)]
    r1 = run_bass_kernel_spmd(nc, im, core_ids=list(range(NCORES))).results
    qT = np.zeros((2, 8, 128, T), BF)
    kT = np.zeros((2, 8, 128, T), BF)
    Vv = np.zeros((2, T, 8, 129), BF)
    Vv[:, :, :, 128] = 1.0
    go = np.zeros((2, T, 64), np.float32)
    for c in range(NCORES):
        b, sl = c // 4, c % 4
        cols = slice(sl * TC, (sl + 1) * TC)
        for h in range(8):
            qT[b, h][:, cols] = r1[c]["qT_o"][h]
            kT[b, h][:, cols] = r1[c]["kT_o"][h]
        Vv[b, cols.start:cols.stop, :, 0:128] = r1[c]["V_o"]
        go[b, cols.start:cols.stop] = r1[c]["go_o"]
    nc2 = _get("l1ffn_flash", build_l1ffn_flash)
    im2 = []
    for c in range(NCORES):
        prs = [2 * c, 2 * c + 1]
        im2.append({"qT": np.stack([qT[pp // 8, pp % 8] for pp in prs]),
                    "kT": np.stack([kT[pp // 8, pp % 8] for pp in prs]),
                    "V": np.stack([np.ascontiguousarray(Vv[pp // 8, :, pp % 8]) for pp in prs])})
    r2 = run_bass_kernel_spmd(nc2, im2, core_ids=list(range(NCORES))).results
    ctxT = np.zeros((2, 8, 128, T), BF)
    rs = np.zeros((2, 8, T), np.float32)
    for c in range(NCORES):
        for jj in range(2):
            pp = 2 * c + jj
            ctxT[pp // 8, pp % 8] = r2[c]["ctxT_o"][jj]
            rs[pp // 8, pp % 8] = r2[c]["rs_o"][jj]
    nc3 = _get("l1ffn_post", build_l1ffn_post)
    repl = np.zeros((64, 64 * 128), np.float32)
    for he in range(64):
        repl[he, he * 128:(he + 1) * 128] = 1.0
    repl = repl.astype(BF)
    im3 = []
    for c in range(NCORES):
        b, sl = c // 4, c % 4
        cols = slice(sl * TC, (sl + 1) * TC)
        im3.append({"xin": np.ascontiguousarray(xs[c]),
                    "ct": np.ascontiguousarray(ctxT[b][:, :, cols]),
                    "go": np.ascontiguousarray(go[b, cols.start:cols.stop]),
                    "rs": np.ascontiguousarray(rs[b][:, cols.start:cols.stop].T),
                    "ow": o_w, "repl": repl, "idb": _EYE_BF})
    r3 = run_bass_kernel_spmd(nc3, im3, core_ids=list(range(NCORES))).results
    return np.stack([r["xout"] for r in r3]).reshape(2, 8192, 1024)





HTOK = 4096
QTOK = 2048
I32 = mybir.dt.int32


def build_fused(_DBG_STAGE=0):
    nc = bass.Bass("TRN2", target_bir_lowering=False)
    dram = lambda *a, **k: nc.dram_tensor(*a, **k).ap()
    xh_d = dram("xh", [HTOK + 256, D], F32, kind="ExternalInput")
    wqkv_d = dram("wqkv", [D, 3 * D], F32, kind="ExternalInput")
    wout_d = dram("wout", [D, D], F32, kind="ExternalInput")
    bout_d = dram("bout", [1, D], F32, kind="ExternalInput")
    wg_d = dram("wg", [2816, D], F32, kind="ExternalInput")
    wu_d = dram("wu", [2816, D], F32, kind="ExternalInput")
    wd_d = dram("wd", [D, 2816], F32, kind="ExternalInput")
    wq1_d = dram("wq1", [D, D], F32, kind="ExternalInput")
    wk1_d = dram("wk1", [D, D], F32, kind="ExternalInput")
    wv1_d = dram("wv1", [D, D], F32, kind="ExternalInput")
    wo1_d = dram("wo1", [D, D], F32, kind="ExternalInput")
    gb_d = dram("gb", [128, 4 * 64], F32, kind="ExternalInput")
    qw2_d = dram("qw2", [D, D], F32, kind="ExternalInput")
    kw2_d = dram("kw2", [D, D], F32, kind="ExternalInput")
    selw_d = dram("selw", [128, D], F32, kind="ExternalInput")
    vw_d = dram("vw", [64, D, 128], F32, kind="ExternalInput")
    ow_d = dram("ow", [64, 128, D], F32, kind="ExternalInput")
    rsc_d = dram("rsc", [128, 1], F32, kind="ExternalInput")
    idb_d = dram("idb", [128, 128], BF16, kind="ExternalInput")
    idf_d = dram("idf", [128, 128], F32, kind="ExternalInput")
    cos_d = dram("cosd", [HTOK + 256, 32], F32, kind="ExternalInput")
    sin_d = dram("sind", [HTOK + 256, 32], F32, kind="ExternalInput")
    msk_d = dram("mskd", [4, 8 * 128, 384], BF16, kind="ExternalInput")
    tri_d = dram("trid", [128, 128], BF16, kind="ExternalInput")
    repl_d = dram("repl", [64, 64 * 128], BF16, kind="ExternalInput")
    runid_d = dram("runid", [1, 16], I32, kind="ExternalInput")
    sidec_d = dram("sidec", [128, 1], F32, kind="ExternalInput")
    qoff_d = dram("qoff", [1, 16], I32, kind="ExternalInput")
    xout = dram("xout", [QTOK, D], F32, kind="ExternalOutput")
    x1_h = dram("x1_h", [HTOK, D], F32, kind="Internal")
    x2_h = dram("x2_h", [HTOK, D], F32, kind="Internal")
    x3_h = dram("x3_h", [HTOK, D], F32, kind="Internal")
    aT_h = dram("aT_h", [8, 128, 1024], BF16, kind="Internal")
    hq_h = dram("hq_h", [HTOK // 128, 128, 2816], BF16, kind="Internal")
    qT1_h = dram("qT1_h", [8, 128, HTOK], BF16, kind="Internal")
    kT1_h = dram("kT1_h", [8, 128, HTOK], BF16, kind="Internal")
    ktm_h = dram("ktm_h", [HTOK, D], BF16, kind="Internal")
    vtm_h = dram("vtm_h", [HTOK, D], BF16, kind="Internal")
    oT_h = dram("oT_h", [8, 128, HTOK], BF16, kind="Internal")
    qT2_h = dram("qT2_h", [8, 128, HTOK], BF16, kind="Internal")
    go_h = dram("go_h", [HTOK, 64], F32, kind="Internal")
    x3q_h = dram("x3q_h", [QTOK, D], F32, kind="Internal")
    goq_h = dram("goq_h", [QTOK, 64], F32, kind="Internal")
    qTq_h = dram("qTq_h", [8, 128, QTOK], BF16, kind="Internal")
    kvst_s = dram("kvst_s", [2, 64, 16 * 64], F32, kind="Internal", addr_space="Shared")
    kn_s = dram("kn_s", [2, 1, 16], F32, kind="Internal", addr_space="Shared")
    K_s = dram("K_s", [2, 8, 128, HTOK], BF16, kind="Internal", addr_space="Shared")
    V_s = dram("V_s", [2, HTOK, 8, 132], BF16, kind="Internal", addr_space="Shared")
    flg_s = dram("flg_s", [2, 8, 16], I32, kind="Internal", addr_space="Shared")

    es = ExitStack()
    sb = lambda *a: es.enter_context(nc.sbuf_tensor(*a))
    ps = lambda *a: es.enter_context(nc.psum_tensor(*a))
    idb = sb("idb_sb", [128, 128], BF16)
    tri = sb("tri_sb", [128, 128], BF16)
    rsc = sb("rsc_sb", [128, 1], F32)
    sidec = sb("sidec_sb", [128, 1], F32)
    rid = sb("rid_sb", [1, 16], I32)
    qoffs = sb("qoff_sb", [1, 16], I32)
    xt = sb("xt", [128, 2, 1408], F32)
    nbf = sb("nbf", [128, 2, D], BF16)
    ntS = sb("ntS", [128, 2, D], BF16)
    nsm = sb("nsm", [128, 16], F32)
    xo = sb("xo", [128, 2, D], F32)
    tpW = ps("tpW", [128, 1024], BF16)
    mmA = ps("mmA", [128, 512], F32)
    mmB = ps("mmB", [128, 512], F32)
    esQ = ExitStack()
    sbq_ = lambda *a: esQ.enter_context(nc.sbuf_tensor(*a))
    nf32 = sbq_("nf32", [128, 2, D], F32)
    qout = sbq_("qout", [128, 2, 1408], BF16)
    z_ = sbq_("z", [128, 1408], F32)
    a_ = sbq_("a", [128, 1408], F32)
    b_ = sbq_("bq", [128, 1408], F32)
    c_ = sbq_("cq", [128, 1408], F32)
    sgn_ = sbq_("sgn", [128, 1408], BF16)
    amax_ = sbq_("amax", [128, 88], F32)
    s6_ = sbq_("s6", [128, 88], F32)
    r2_ = sbq_("r2", [128, 88], F32)

    with nc.Block() as blk:
        p = Prog(nc, blk)
        G = {}
        G.update(locals())
        scr = dict(z=z_, a=a_, b=b_, c=c_, sgn=sgn_, amax=amax_, s6=s6_, r2=r2_)
        st = dict(s1=nsm[:, 0:1], s2=nsm[:, 1:2], mu=nsm[:, 2:3], m2=nsm[:, 3:4],
                  t=nsm[:, 4:5], var=nsm[:, 5:6], vpe=nsm[:, 6:7], std=nsm[:, 7:8],
                  rstd=nsm[:, 8:9], mx=nsm[:, 9:10], bias=nsm[:, 10:11],
                  rs=nsm[:, 11:12], rr=nsm[:, 12:13], sq=z_)
        R = dict(const=Res("const"), xt=[Res(), Res()], nbf=[Res(), Res()],
                 ntS=[Res(), Res()], qout=[Res(), Res()], scr=Res("scr"),
                 nf32=[Res(), Res()],
                 sgn=Res("sgn"), nsm=Res("nsm"), tpW=Res("tpW"),
                 mm=[Res(), Res()], xo=[Res(), Res()])
        mm = [mmA, mmB]
        RIDTOK = 1000003

        @p.step("sync", writes=[R['const']], dma=True)
        def _(e, dd):
            dd(out=idb[:], in_=idb_d[:, :])
            dd(out=tri[:], in_=tri_d[:, :])
            dd(out=rsc[:], in_=rsc_d[:, :])
            dd(out=sidec[:], in_=sidec_d[:, :])
            dd(out=rid[0:1, :], in_=runid_d[:, :])
            dd(out=qoffs[0:1, :], in_=qoff_d[:, :])

        def ln_tile(x_res, x_ap, dst_res, dst_ap, stv=None):
            stv = stv or st
            @p.step("vector", reads=[x_res], writes=[R['nsm'], R['scr']])
            def _(e):
                return emit_ln_stats(e, x_ap, stv, 1024, p.chain(e))
            @p.step("scalar", reads=[R['nsm']], writes=[R['nsm']])
            def _(e):
                return e.activation(out=stv['std'], in_=stv['vpe'], func=AF.Sqrt)
            @p.step("vector", reads=[x_res, R['nsm']], writes=[dst_res, R['nsm']])
            def _(e):
                p.chain(e)(e.reciprocal(stv['rstd'], stv['std']))
                return e.tensor_scalar(dst_ap, x_ap, stv['mu'], stv['rstd'],
                                       ALU.subtract, ALU.mult)

        def transpose_blocks(src_res, src_ap_fn, nblk, dst_res, dst_ap_fn):
            for g0 in range(0, nblk, 8):
                gn = min(8, nblk - g0)
                @p.step("tensor", reads=[src_res, R['const']], writes=[R['tpW']])
                def _(e, g0=g0, gn=gn):
                    for j in range(gn):
                        last = e.transpose(tpW[:, j * 128:(j + 1) * 128], src_ap_fn(g0 + j), idb[:])
                    return last
                @p.step("vector", reads=[R['tpW']], writes=[dst_res])
                def _(e, g0=g0, gn=gn):
                    for j in range(gn):
                        last = e.tensor_copy(dst_ap_fn(g0 + j), tpW[:, j * 128:(j + 1) * 128])
                    return last

        def quant_chain(x_res, x_ap, q_res, q_ap, nb):
            F = nb * 16
            @p.step("scalar", reads=[x_res], writes=[R['sgn'], R['scr']])
            def _(e):
                e.activation(out=sgn_[:, :F], in_=x_ap, func=AF.Sign)
                return e.activation(out=a_[:, :F], in_=x_ap, func=AF.Abs)
            @p.step("vector", reads=[x_res, R['sgn']], writes=[q_res, R['scr']])
            def _(e):
                return emit_quant(e, x_ap, q_ap, scr, nb, p.chain(e))

        bar_regs = {}

        def get_regs(e):
            if not bar_regs:
                bar_regs['t'] = e.alloc_register("bar_t")
                bar_regs['r0'] = e.alloc_register("bar_r0")
                bar_regs['r1'] = e.alloc_register("bar_r1")
                bar_regs['c'] = e.alloc_register("bar_c")
            return bar_regs

        def emit_flag_store(e, sig):
            rg = get_regs(e)
            e.reg_load(rg['t'], rid[0:1, 0:1])
            e.reg_alu(rg['t'], rg['t'], sig + 1, ALU.add)
            for arm in nc.switch_core_id(engines=[e], n=128):
                slot = int(arm) % 2
                e.store(flg_s[slot, sig, 0:1][None], rg['t'])

        def emit_flag_poll(e, sig, both=True):
            rg = get_regs(e)
            e.reg_load(rg['t'], rid[0:1, 0:1])
            e.reg_alu(rg['t'], rg['t'], sig + 1, ALU.add)
            def poll():
                e.reg_load(rg['r0'], flg_s[0, sig, 0:1][None])
                e.reg_alu(rg['r0'], rg['r0'], rg['t'], ALU.not_equal)
                if both:
                    e.reg_load(rg['r1'], flg_s[1, sig, 0:1][None])
                    e.reg_alu(rg['r1'], rg['r1'], rg['t'], ALU.not_equal)
                    e.reg_alu(rg['c'], rg['r0'], rg['r1'], ALU.bitwise_or)
                else:
                    e.reg_alu(rg['c'], rg['r0'], 0, ALU.add)
            poll()
            with e.While(rg['c']):
                poll()

        cos_sb = sbq_("cos_sb", [128, 34, 32], BF16)
        sin_sb = sbq_("sin_sb", [128, 34, 32], BF16)

        @p.step("gpsimd", writes=[R['const']], dma=True)
        def _(e, dd):
            dd(out=cos_sb[:], in_=cos_d.rearrange("(t p) c -> p t c", p=128))
            dd(out=sin_sb[:], in_=sin_d.rearrange("(t p) c -> p t c", p=128))

        def emit_l0attn(sub):
            NTE, NQB = 10, 8
            es1 = ExitStack()
            sb1 = lambda *a: es1.enter_context(nc.sbuf_tensor(*a))
            qt_b = sb1("qt%d" % sub, [128, 2, D], BF16)
            kt_b = sb1("kt%d" % sub, [128, 2, D], BF16)
            rt = sb1("rt%d" % sub, [128, 2048], F32)
            vbuf = sb1("vbuf%d" % sub, [128, NTE, D], BF16)
            qT = sb1("qT%d" % sub, [128, 8, NTE * 128], BF16)
            kT = sb1("kT%d" % sub, [128, 8, NTE * 128], BF16)
            pb = sb1("pb%d" % sub, [128, 384], BF16)
            pT = sb1("pT%d" % sub, [128, 384], BF16)
            r_qt = [Res(), Res()]
            r_kt = [Res(), Res()]
            r_rt, r_vbuf, r_qT, r_kT = Res(), Res(), Res(), Res()
            r_pb, r_pT = Res(), Res()
            st2 = dict(st, sq=rt)
            es2 = ExitStack()
            wq_sb = es2.enter_context(nc.sbuf_tensor("wqsb%d" % sub, [128, 8, 3 * D], BF16))
            r_wq = Res()

            @p.step("gpsimd", writes=[r_wq], dma=True)
            def _(e, dd):
                dd(out=wq_sb[:], in_=wqkv_d.rearrange("(c p) o -> p c o", p=128))

            x0 = sub * 1024
            for t in range(NTE):
                bi = t % 2
                @p.step("sync", writes=[R['xt'][bi]], dma=True)
                def _(e, dd, t=t, bi=bi):
                    dd(out=xt[:, bi, :1024], in_=xh_d[x0 + t * 128: x0 + (t + 1) * 128, :])
                @p.step("vector", reads=[R['xt'][bi]], writes=[R['nsm'], r_rt])
                def _(e, bi=bi):
                    return emit_ln_stats(e, xt[:, bi, :1024], st2, 1024, p.chain(e))
                @p.step("scalar", reads=[R['nsm']], writes=[R['nsm']])
                def _(e):
                    return e.activation(out=st['std'], in_=st['vpe'], func=AF.Sqrt)
                @p.step("vector", reads=[R['xt'][bi], R['nsm']], writes=[R['nbf'][bi], R['nsm']])
                def _(e, bi=bi):
                    p.chain(e)(e.reciprocal(st['rstd'], st['std']))
                    return e.tensor_scalar(nbf[:, bi, :], xt[:, bi, :1024],
                                           st['mu'], st['rstd'], ALU.subtract, ALU.mult)
                transpose_blocks(R['nbf'][bi], (lambda d, bi=bi: nbf[:, bi, d * 128:(d + 1) * 128]),
                                 8, R['ntS'][bi], (lambda d, bi=bi: ntS[:, bi, d * 128:(d + 1) * 128]))
                for s6x in range(6):
                    mi = s6x % 2
                    @p.step("tensor", reads=[R['ntS'][bi], r_wq], writes=[R['mm'][mi]])
                    def _(e, bi=bi, s6x=s6x, mi=mi):
                        for d in range(8):
                            last = e.matmul(mm[mi][:], ntS[:, bi, d * 128:(d + 1) * 128],
                                            wq_sb[:, d, s6x * 512:(s6x + 1) * 512],
                                            start=(d == 0), stop=(d == 7))
                        return last
                    dst = [(qt_b, r_qt[bi], 0), (qt_b, r_qt[bi], 512), (kt_b, r_kt[bi], 0),
                           (kt_b, r_kt[bi], 512), (None, r_vbuf, 0), (None, r_vbuf, 512)][s6x]
                    @p.step("vector", reads=[R['mm'][mi]], writes=[dst[1]])
                    def _(e, bi=bi, mi=mi, dst=dst, t=t):
                        if dst[0] is None:
                            return e.tensor_copy(vbuf[:, t, dst[2]:dst[2] + 512], mm[mi][:])
                        return e.tensor_copy(dst[0][:, bi, dst[2]:dst[2] + 512], mm[mi][:])
                for (buf, rres) in ((qt_b, r_qt[bi]), (kt_b, r_kt[bi])):
                    @p.step("vector", reads=[rres, R['const']], writes=[rres, r_rt])
                    def _(e, bi=bi, buf=buf, t=t):
                        v3 = buf[:, bi, :].rearrange("p (h x) -> p h x", x=64)
                        x1a, x2a = v3[:, :, 0:32], v3[:, :, 32:64]
                        ct = sub * 8 + t
                        cb = cos_sb[:, ct, :].unsqueeze(1).broadcast_to([128, 16, 32])
                        sbq = sin_sb[:, ct, :].unsqueeze(1).broadcast_to([128, 16, 32])
                        t1 = rt[:, 0:512].rearrange("p (h x) -> p h x", x=32)
                        t2 = rt[:, 512:1024].rearrange("p (h x) -> p h x", x=32)
                        t3 = rt[:, 1024:1536].rearrange("p (h x) -> p h x", x=32)
                        t4 = rt[:, 1536:2048].rearrange("p (h x) -> p h x", x=32)
                        e.tensor_tensor(t1, x1a, cb, ALU.mult)
                        e.tensor_tensor(t2, x2a, sbq, ALU.mult)
                        e.tensor_tensor(t3, x2a, cb, ALU.mult)
                        e.tensor_tensor(t4, x1a, sbq, ALU.mult)
                        e.tensor_tensor(x1a, t1, t2, ALU.subtract)
                        return e.tensor_tensor(x2a, t3, t4, ALU.add)
                for (buf, rres, dstT, dres) in ((qt_b, r_qt[bi], qT, r_qT), (kt_b, r_kt[bi], kT, r_kT)):
                    transpose_blocks(rres, (lambda d, bi=bi, buf=buf: buf[:, bi, d * 128:(d + 1) * 128]),
                                     8, dres, (lambda d, t=t, dstT=dstT: dstT[:, d, t * 128:(t + 1) * 128]))
            p.fence()
            es2.close()
            es3 = ExitStack()
            msk_sb = es3.enter_context(nc.sbuf_tensor("msksb%d" % sub, [128, 8, 384], BF16))
            aTs = es3.enter_context(nc.sbuf_tensor("aTs%d" % sub, [128, 8, 128], BF16))
            wo_sb = es3.enter_context(nc.sbuf_tensor("wosb%d" % sub, [128, 8, D], BF16))
            aTin = es3.enter_context(nc.sbuf_tensor("aTin%d" % sub, [128, 2, 8, 128], BF16))
            scp = es3.enter_context(nc.psum_tensor("scp%d" % sub, [128, 384], F32))
            ops_ = es3.enter_context(nc.psum_tensor("ops%d" % sub, [64, 128], F32))
            bout_sb = es3.enter_context(nc.sbuf_tensor("boutsb%d" % sub, [1, D], BF16))
            ones_sb = es3.enter_context(nc.sbuf_tensor("onessb%d" % sub, [1, 128], BF16))
            r_msk, r_aTs, r_scp, r_ops, r_wo = Res(), Res(), Res(), Res(), Res()
            r_aTin = [Res(), Res()]
            r_aTh = [Res() for _ in range(NQB)]

            @p.step("sync", writes=[r_msk], dma=True)
            def _(e, dd):
                dd(out=msk_sb[:], in_=msk_d[sub].rearrange("(t p) c -> p t c", p=128))

            @p.step("gpsimd", writes=[r_wo], dma=True)
            def _(e, dd):
                dd(out=wo_sb[:], in_=wout_d.rearrange("(c p) o -> p c o", p=128))
                dd(out=bout_sb[:], in_=bout_d[:, :])

            @p.step("vector", writes=[r_wo])
            def _(e):
                return e.memset(ones_sb[:], 1.0)

            for qb in range(NQB):
                for h in range(16):
                    hp0 = (h % 2) * 64
                    hc = h // 2
                    @p.step("tensor", reads=[r_qT, r_kT], writes=[r_scp])
                    def _(e, h=h, qb=qb, hp0=hp0, hc=hc):
                        return e.matmul(scp[:],
                                        qT[hp0:hp0 + 64, hc, (qb + 1) * 128:(qb + 2) * 128],
                                        kT[hp0:hp0 + 64, hc, qb * 128:qb * 128 + 384],
                                        start=True, stop=True)
                    @p.step("vector", reads=[r_scp], writes=[R['nsm']])
                    def _(e):
                        ch = p.chain(e)
                        ch(e.reduce_max(st['mx'], scp[:], AX.X))
                        return e.tensor_scalar_mul(st['bias'], st['mx'], -0.125)
                    @p.step("scalar", reads=[r_scp, R['nsm']], writes=[r_pb])
                    def _(e):
                        return e.activation(out=pb[:], in_=scp[:], func=AF.Exp,
                                            bias=st['bias'], scale=0.125)
                    @p.step("vector", reads=[r_pb, r_msk], writes=[r_pb, R['nsm']])
                    def _(e, qb=qb):
                        ch = p.chain(e)
                        e.tensor_tensor(pb[:], pb[:], msk_sb[:, qb, :], ALU.mult)
                        ch(e.reduce_sum(st['rs'], pb[:], AX.X))
                        ch(e.reciprocal(st['rr'], st['rs']))
                        return e.tensor_scalar_mul(pb[:], pb[:], st['rr'])
                    transpose_blocks(r_pb, (lambda j: pb[:, j * 128:(j + 1) * 128]), 3,
                                     r_pT, (lambda j: pT[:, j * 128:(j + 1) * 128]))
                    @p.step("tensor", reads=[r_pT, r_vbuf], writes=[r_ops])
                    def _(e, h=h, qb=qb):
                        for j in range(3):
                            last = e.matmul(ops_[:], vbuf[:, qb + j, h * 64:(h + 1) * 64],
                                            pT[:, j * 128:(j + 1) * 128],
                                            start=(j == 0), stop=(j == 2))
                        return last
                    @p.step("vector", reads=[r_ops], writes=[r_aTs])
                    def _(e, hp0=hp0, hc=hc):
                        return e.tensor_copy(aTs[hp0:hp0 + 64, hc, :], ops_[:])
                @p.step("gpsimd", reads=[r_aTs], writes=[r_aTh[qb]], dma=True)
                def _(e, dd, qb=qb):
                    dd(out=aT_h[:, :, qb * 128:(qb + 1) * 128].rearrange("c p q -> p c q"),
                       in_=aTs[:])
            for t in range(NQB):
                bi = t % 2
                @p.step("sync", reads=[r_aTh[t]], writes=[R['xt'][bi], r_aTin[bi]], dma=True)
                def _(e, dd, t=t, bi=bi):
                    dd(out=xt[:, bi, :1024], in_=xh_d[x0 + 128 + t * 128: x0 + 128 + (t + 1) * 128, :])
                    dd(out=aTin[:, bi, :, :],
                       in_=aT_h[:, :, t * 128:(t + 1) * 128].rearrange("c p q -> p c q"))
                for oi in range(2):
                    mi = oi % 2
                    @p.step("tensor", reads=[r_aTin[bi], r_wo], writes=[R['mm'][mi]])
                    def _(e, oi=oi, mi=mi, bi=bi):
                        for d in range(8):
                            e.matmul(mm[mi][:], aTin[:, bi, d, :],
                                     wo_sb[:, d, oi * 512:(oi + 1) * 512],
                                     start=(d == 0), stop=False)
                        return e.matmul(mm[mi][:], ones_sb[:, 0:128],
                                        bout_sb[:, oi * 512:(oi + 1) * 512],
                                        start=False, stop=True)
                    @p.step("vector", reads=[R['mm'][mi], R['xt'][bi]], writes=[R['xo'][bi]])
                    def _(e, oi=oi, mi=mi, bi=bi):
                        return e.tensor_tensor(xo[:, bi, oi * 512:(oi + 1) * 512], mm[mi][:],
                                               xt[:, bi, oi * 512:(oi + 1) * 512], ALU.add)
                @p.step("gpsimd", reads=[R['xo'][bi]], writes=[R['x1'][sub * 8 + t]], dma=True)
                def _(e, dd, t=t, bi=bi):
                    dd(out=x1_h[sub * 1024 + t * 128: sub * 1024 + (t + 1) * 128, :],
                       in_=xo[:, bi, :])
            p.fence()
            es3.close()
            es1.close()

        R['x1'] = [Res() for _ in range(32)]
        for _sub in range(4):
            emit_l0attn(_sub)

        def emit_l0ffn():
            NT32 = HTOK // 128
            NI = 22
            r_hqh = [Res() for _ in range(NT32)]
            R['x2'] = [Res() for _ in range(NT32)]
            es1 = ExitStack()
            sb1 = lambda *a: es1.enter_context(nc.sbuf_tensor(*a))
            wgT = sb1("wgT", [128, 8, 2816], BF16)
            wuT = sb1("wuT", [128, 8, 2816], BF16)
            xqTt = sb1("xqTt", [128, 2, 1024], BF16)
            hqTt = sb1("hqTt", [128, 2816], BF16)
            hbuf = sb1("hbuf", [128, 2816], F32)
            gs = sb1("gs", [128, 512], BF16)
            gps = es1.enter_context(nc.psum_tensor("gps", [128, 512], F32))
            ups = es1.enter_context(nc.psum_tensor("ups", [128, 512], F32))
            r_wgT, r_wuT = Res(), Res()
            r_xqTt = [Res(), Res()]
            r_hqt, r_hbuf, r_gs = Res(), Res(), Res()
            r_gps, r_ups = Res(), Res()

            for wdram, wT, wres in ((wg_d, wgT, r_wgT), (wu_d, wuT, r_wuT)):
                for i in range(NI):
                    bi = i % 2
                    @p.step("sync", writes=[R['xt'][bi]], dma=True)
                    def _(e, dd, i=i, bi=bi, wdram=wdram):
                        dd(out=xt[:, bi, :1024], in_=wdram[i * 128:(i + 1) * 128, :])
                    quant_chain(R['xt'][bi], xt[:, bi, :1024], R['qout'][bi],
                                qout[:, bi, :1024], 64)
                    transpose_blocks(R['qout'][bi],
                                     (lambda d, bi=bi: qout[:, bi, d * 128:(d + 1) * 128]),
                                     8, wres,
                                     (lambda d, i=i, wT=wT: wT[:, d, i * 128:(i + 1) * 128]))

            otiles = [(0, 512), (512, 512), (1024, 512), (1536, 512), (2048, 512), (2560, 256)]
            for t in range(NT32):
                bi = t % 2
                @p.step("sync", reads=[R['x1'][t]], writes=[R['xt'][bi]], dma=True)
                def _(e, dd, t=t, bi=bi):
                    dd(out=xt[:, bi, :1024], in_=x1_h[t * 128:(t + 1) * 128, :])
                ln_tile(R['xt'][bi], xt[:, bi, :1024], R['nf32'][bi], nf32[:, bi, :])
                quant_chain(R['nf32'][bi], nf32[:, bi, :], R['qout'][bi], qout[:, bi, :1024], 64)
                transpose_blocks(R['qout'][bi],
                                 (lambda d, bi=bi: qout[:, bi, d * 128:(d + 1) * 128]),
                                 8, r_xqTt[bi],
                                 (lambda d, bi=bi: xqTt[:, bi, d * 128:(d + 1) * 128]))
                for (o0, ow) in otiles:
                    @p.step("tensor", reads=[r_xqTt[bi], r_wgT, r_wuT], writes=[r_gps, r_ups])
                    def _(e, bi=bi, o0=o0, ow=ow):
                        for d in range(8):
                            e.matmul(gps[:, :ow], xqTt[:, bi, d * 128:(d + 1) * 128],
                                     wgT[:, d, o0:o0 + ow], start=(d == 0), stop=(d == 7))
                        for d in range(8):
                            last = e.matmul(ups[:, :ow], xqTt[:, bi, d * 128:(d + 1) * 128],
                                            wuT[:, d, o0:o0 + ow], start=(d == 0), stop=(d == 7))
                        return last
                    @p.step("scalar", reads=[r_gps], writes=[r_gs])
                    def _(e, ow=ow):
                        return e.activation(out=gs[:, :ow], in_=gps[:, :ow], func=AF.Silu)
                    @p.step("vector", reads=[r_gs, r_ups], writes=[r_hbuf])
                    def _(e, o0=o0, ow=ow):
                        return e.tensor_tensor(hbuf[:, o0:o0 + ow], ups[:, :ow],
                                               gs[:, :ow], ALU.mult)
                for hf in range(2):
                    h0 = hf * 1408
                    quant_chain(r_hbuf, hbuf[:, h0:h0 + 1408], R['qout'][hf],
                                qout[:, hf, :1408], 88)
                    transpose_blocks(R['qout'][hf],
                                     (lambda d, hf=hf: qout[:, hf, d * 128:(d + 1) * 128]),
                                     11, r_hqt,
                                     (lambda d, h0=h0: hqTt[:, h0 + d * 128: h0 + (d + 1) * 128]))
                @p.step("gpsimd", reads=[r_hqt], writes=[r_hqh[t]], dma=True)
                def _(e, dd, t=t):
                    dd(out=hq_h[t], in_=hqTt[:, :])
            p.fence()
            es1.close()
            es2 = ExitStack()
            wdT = es2.enter_context(nc.sbuf_tensor("wdT", [128, NI, 1024], BF16))
            hqin = es2.enter_context(nc.sbuf_tensor("hqin", [128, 2, 2816], BF16))
            dps = es2.enter_context(nc.psum_tensor("dps", [128, 2, 512], F32))
            r_wdT = Res()
            r_hqin = [Res(), Res()]
            r_dps = [Res(), Res()]
            for i in range(8):
                for hf in range(2):
                    h0 = hf * 1408
                    bi = (2 * i + hf) % 2
                    @p.step("sync", writes=[R['xt'][bi]], dma=True)
                    def _(e, dd, i=i, h0=h0, bi=bi):
                        dd(out=xt[:, bi, :1408], in_=wd_d[i * 128:(i + 1) * 128, h0:h0 + 1408])
                    quant_chain(R['xt'][bi], xt[:, bi, :1408], R['qout'][bi],
                                qout[:, bi, :1408], 88)
                    transpose_blocks(R['qout'][bi],
                                     (lambda d, bi=bi: qout[:, bi, d * 128:(d + 1) * 128]),
                                     11, r_wdT,
                                     (lambda d, i=i, h0=h0: wdT[:, h0 // 128 + d, i * 128:(i + 1) * 128]))
            for t in range(NT32):
                bi = t % 2
                @p.step("sync", reads=[r_hqh[t], R['x1'][t]], writes=[r_hqin[bi], R['xt'][bi]], dma=True)
                def _(e, dd, t=t, bi=bi):
                    dd(out=hqin[:, bi, :], in_=hq_h[t])
                    dd(out=xt[:, bi, :1024], in_=x1_h[t * 128:(t + 1) * 128, :])
                for oi, (o0, ow) in enumerate([(0, 512), (512, 512)]):
                    @p.step("tensor", reads=[r_hqin[bi], r_wdT], writes=[r_dps[oi]])
                    def _(e, bi=bi, o0=o0, ow=ow, oi=oi):
                        for i in range(NI):
                            last = e.matmul(dps[:, oi, :ow], hqin[:, bi, i * 128:(i + 1) * 128],
                                            wdT[:, i, o0:o0 + ow], start=(i == 0), stop=(i == NI - 1))
                        return last
                    @p.step("vector", reads=[r_dps[oi], R['xt'][bi]], writes=[R['xo'][bi]])
                    def _(e, bi=bi, o0=o0, ow=ow, oi=oi):
                        return e.tensor_tensor(xo[:, bi, o0:o0 + ow], dps[:, oi, :ow],
                                               xt[:, bi, o0:o0 + ow], ALU.add)
                @p.step("gpsimd", reads=[R['xo'][bi]], writes=[R['x2'][t]], dma=True)
                def _(e, dd, t=t, bi=bi):
                    dd(out=x2_h[t * 128:(t + 1) * 128, :], in_=xo[:, bi, :])
            p.fence()
            es2.close()

        if _DBG_STAGE == 1:
            dbg_x1 = dram("dbg_x1", [HTOK, D], F32, kind="ExternalOutput")
            for t in range(HTOK // 128):
                bi = t % 2
                @p.step("sync", reads=[R['x1'][t]], writes=[R['xt'][bi]], dma=True)
                def _(e, dd, t=t, bi=bi):
                    dd(out=xt[:, bi, :1024], in_=x1_h[t * 128:(t + 1) * 128, :])
                @p.step("gpsimd", reads=[R['xt'][bi]], dma=True)
                def _(e, dd, t=t, bi=bi):
                    dd(out=dbg_x1[t * 128:(t + 1) * 128, :], in_=xt[:, bi, :1024])
            p.drain()
            es.close()
            return nc

        emit_l0ffn()
        p.fence()
        esQ.close()

        if _DBG_STAGE == 2:
            dbg_x2 = dram("dbg_x2", [HTOK, D], F32, kind="ExternalOutput")
            for t in range(HTOK // 128):
                bi = t % 2
                @p.step("sync", reads=[R['x2'][t]], writes=[R['xt'][bi]], dma=True)
                def _(e, dd, t=t, bi=bi):
                    dd(out=xt[:, bi, :1024], in_=x2_h[t * 128:(t + 1) * 128, :])
                @p.step("gpsimd", reads=[R['xt'][bi]], dma=True)
                def _(e, dd, t=t, bi=bi):
                    dd(out=dbg_x2[t * 128:(t + 1) * 128, :], in_=xt[:, bi, :1024])
            p.drain()
            es.close()
            return nc

        def emit_l1attn():
            NT32 = HTOK // 128
            es1 = ExitStack()
            sb1 = lambda *a: es1.enter_context(nc.sbuf_tensor(*a))
            wln = sb1("wln1", [128, 3, 8, 1024], BF16)
            wT = sb1("wT1", [128, 3, 8, 1024], BF16)
            gb = sb1("gb_sb", [128, 4 * 64], F32)
            qtm = sb1("qtm", [128, 2, D], BF16)
            ktm = sb1("ktm", [128, 2, D], BF16)
            vtm = sb1("vtm", [128, 2, D], BF16)
            sca = sb1("sca1", [128, 2048], F32)
            hst = sb1("hst", [128, 8, 16], F32)
            sT = sb1("sT", [128, 8, 128], BF16)
            Ttot = sb1("Ttot", [64, 16, 64], F32)
            r_wln, r_wT, r_gb = Res(), Res(), Res()
            r_qtm = [Res(), Res()]
            r_ktm = [Res(), Res()]
            r_vtm = [Res(), Res()]
            r_sca, r_hst, r_sT, r_Ttot = Res(), Res(), Res(), Res()
            r_qkv_h = [Res() for _ in range(NT32)]
            st3 = dict(st, sq=sca)
            hs = dict(s1=hst[:, 0, :], s2=hst[:, 1, :], mu=hst[:, 2, :], m2=hst[:, 3, :],
                      t=hst[:, 4, :], var=hst[:, 5, :], vpe=hst[:, 6, :], std=hst[:, 7, :])

            @p.step("sync", writes=[r_gb], dma=True)
            def _(e, dd):
                dd(out=gb[:], in_=gb_d[:, :])

            @p.step("gpsimd", writes=[r_wln], dma=True)
            def _(e, dd):
                dd(out=wln[:, 0], in_=wq1_d.rearrange("(c p) o -> p c o", p=128))
                dd(out=wln[:, 1], in_=wk1_d.rearrange("(c p) o -> p c o", p=128))
                dd(out=wln[:, 2], in_=wv1_d.rearrange("(c p) o -> p c o", p=128))

            for w in range(3):
                for oc in range(8):
                    transpose_blocks(r_wln, (lambda ic, w=w, oc=oc: wln[:, w, oc, ic * 128:(ic + 1) * 128]),
                                     8, r_wT,
                                     (lambda ic, w=w, oc=oc: wT[:, w, ic, oc * 128:(oc + 1) * 128]))

            @p.step("vector", writes=[r_Ttot])
            def _(e):
                return e.memset(Ttot[:], 0.0)

            def headln(e, src_ap, g0):
                ch = p.chain(e)
                x3 = src_ap.rearrange("p (h x) -> p h x", x=64)
                e.tensor_reduce(hs['s1'], x3, AX.X, ALU.add)
                e.tensor_tensor(sca[:, 0:1024], src_ap, src_ap, ALU.mult)
                ch(e.tensor_reduce(hs['s2'], sca[:, 0:1024].rearrange("p (h x) -> p h x", x=64),
                                   AX.X, ALU.add))
                e.tensor_scalar_mul(hs['mu'], hs['s1'], 1.0 / 64)
                ch(e.tensor_scalar_mul(hs['m2'], hs['s2'], 1.0 / 64))
                ch(e.tensor_tensor(hs['t'], hs['mu'], hs['mu'], ALU.mult))
                ch(e.tensor_tensor(hs['var'], hs['m2'], hs['t'], ALU.subtract))
                return e.tensor_scalar_add(hs['vpe'], hs['var'], 1e-5)

            def headln_apply(e, src_ap, g0):
                ch = p.chain(e)
                ch(e.reciprocal(hs['s1'], hs['std']))
                x3 = src_ap.rearrange("p (h x) -> p h x", x=64)
                a3 = sca[:, 0:1024].rearrange("p (h x) -> p h x", x=64)
                mu_b = hs['mu'].unsqueeze(2).broadcast_to([128, 16, 64])
                rs_b = hs['s1'].unsqueeze(2).broadcast_to([128, 16, 64])
                gam = gb[:, g0 * 64:(g0 + 1) * 64].unsqueeze(1).broadcast_to([128, 16, 64])
                bet = gb[:, (g0 + 1) * 64:(g0 + 2) * 64].unsqueeze(1).broadcast_to([128, 16, 64])
                e.tensor_tensor(a3, x3, mu_b, ALU.subtract)
                e.tensor_tensor(a3, a3, rs_b, ALU.mult)
                e.tensor_tensor(a3, a3, gam, ALU.mult)
                return e.tensor_tensor(x3, a3, bet, ALU.add)

            for t in range(NT32):
                bi = t % 2
                @p.step("sync", reads=[R['x2'][t]], writes=[R['xt'][bi]], dma=True)
                def _(e, dd, t=t, bi=bi):
                    dd(out=xt[:, bi, :1024], in_=x2_h[t * 128:(t + 1) * 128, :])
                @p.step("vector", reads=[R['xt'][bi]], writes=[R['nsm'], r_sca])
                def _(e, bi=bi):
                    return emit_ln_stats(e, xt[:, bi, :1024], st3, 1024, p.chain(e))
                @p.step("scalar", reads=[R['nsm']], writes=[R['nsm']])
                def _(e):
                    return e.activation(out=st['std'], in_=st['vpe'], func=AF.Sqrt)
                @p.step("vector", reads=[R['xt'][bi], R['nsm']], writes=[R['nbf'][bi], R['nsm']])
                def _(e, bi=bi):
                    p.chain(e)(e.reciprocal(st['rstd'], st['std']))
                    return e.tensor_scalar(nbf[:, bi, :], xt[:, bi, :1024],
                                           st['mu'], st['rstd'], ALU.subtract, ALU.mult)
                transpose_blocks(R['nbf'][bi], (lambda d, bi=bi: nbf[:, bi, d * 128:(d + 1) * 128]),
                                 8, R['ntS'][bi], (lambda d, bi=bi: ntS[:, bi, d * 128:(d + 1) * 128]))
                for (w, dstb, dres) in ((0, qtm, r_qtm[bi]), (1, ktm, r_ktm[bi]), (2, vtm, r_vtm[bi])):
                    for oi in range(2):
                        mi = oi % 2
                        @p.step("tensor", reads=[R['ntS'][bi], r_wT], writes=[R['mm'][mi]])
                        def _(e, bi=bi, w=w, oi=oi, mi=mi):
                            for d in range(8):
                                last = e.matmul(mm[mi][:], ntS[:, bi, d * 128:(d + 1) * 128],
                                                wT[:, w, d, oi * 512:(oi + 1) * 512],
                                                start=(d == 0), stop=(d == 7))
                            return last
                        @p.step("vector", reads=[R['mm'][mi]], writes=[dres])
                        def _(e, bi=bi, oi=oi, mi=mi, dstb=dstb):
                            return e.tensor_copy(dstb[:, bi, oi * 512:(oi + 1) * 512], mm[mi][:])
                for (buf, rres, g0) in ((qtm, r_qtm[bi], 0), (ktm, r_ktm[bi], 2)):
                    @p.step("vector", reads=[rres], writes=[r_hst, r_sca])
                    def _(e, bi=bi, buf=buf, g0=g0):
                        return headln(e, buf[:, bi, :], g0)
                    @p.step("scalar", reads=[r_hst], writes=[r_hst])
                    def _(e):
                        return e.activation(out=hs['std'], in_=hs['vpe'], func=AF.Sqrt)
                    @p.step("vector", reads=[rres, r_hst, r_gb], writes=[rres, r_hst, r_sca])
                    def _(e, bi=bi, buf=buf, g0=g0):
                        return headln_apply(e, buf[:, bi, :], g0)
                @p.step("gpsimd", reads=[r_ktm[bi], r_vtm[bi]], writes=[r_qkv_h[t]], dma=True)
                def _(e, dd, t=t, bi=bi):
                    dd(out=ktm_h[t * 128:(t + 1) * 128, :], in_=ktm[:, bi, :])
                    dd(out=vtm_h[t * 128:(t + 1) * 128, :], in_=vtm[:, bi, :])
                for (buf, rres, dram2) in ((qtm, r_qtm[bi], qT1_h), (ktm, r_ktm[bi], kT1_h)):
                    transpose_blocks(rres, (lambda d, bi=bi, buf=buf: buf[:, bi, d * 128:(d + 1) * 128]),
                                     8, r_sT, (lambda d: sT[:, d, :]))
                    @p.step("gpsimd", reads=[r_sT], writes=[r_qkv_h[t]], dma=True)
                    def _(e, dd, t=t, dram2=dram2):
                        dd(out=dram2[:, :, t * 128:(t + 1) * 128].rearrange("c p q -> p c q"),
                           in_=sT[:])
                for h4 in range(4):
                    mi = h4 % 2
                    @p.step("tensor", reads=[r_ktm[bi], r_vtm[bi]], writes=[R['mm'][mi]])
                    def _(e, bi=bi, h4=h4, mi=mi):
                        for hh in range(4):
                            h = h4 * 4 + hh
                            last = e.matmul(mm[mi][0:64, hh * 64:(hh + 1) * 64],
                                            ktm[:, bi, h * 64:(h + 1) * 64],
                                            vtm[:, bi, h * 64:(h + 1) * 64],
                                            start=True, stop=True)
                        return last
                    @p.step("vector", reads=[R['mm'][mi]], writes=[r_Ttot])
                    def _(e, h4=h4, mi=mi):
                        t4 = Ttot[:, h4 * 4:(h4 + 1) * 4, :].rearrange("p h x -> p (h x)")
                        return e.tensor_tensor(t4, t4, mm[mi][0:64, 0:256], ALU.add)
            @p.step("vector", reads=[r_Ttot, R['const']], writes=[r_Ttot])
            def _(e):
                t2d = Ttot[:].rearrange("p h x -> p (h x)")
                p.chain(e)(e.tensor_scalar(nsm[0:64, 14:15], sidec[0:64, 0:1], -1.0, 1.0,
                                           ALU.mult, ALU.add))
                return e.tensor_scalar_mul(t2d, t2d, nsm[0:64, 14:15])
            @p.step("gpsimd", reads=[r_Ttot], writes=[R['const']], dma=True, fixed_inc=16)
            def _(e, dd):
                for arm in nc.switch_core_id(engines=[e], n=128):
                    slot = int(arm) % 2
                    dd(out=kvst_s[slot], in_=Ttot[:].rearrange("p h x -> p (h x)"))
            @p.step("gpsimd", reads=[R['const']], writes=[R['const']])
            def _(e):
                emit_flag_store(e, 0)
                emit_flag_poll(e, 0, both=True)
                return e.nop()
            Sini = sb1("Sini", [64, 16, 64], F32)
            Sini2 = sb1("Sini2", [64, 16, 64], F32)
            r_Sini = Res()
            @p.step("sync", reads=[R['const']], writes=[r_Sini], dma=True)
            def _(e, dd):
                dd(out=Sini[:].rearrange("p h x -> p (h x)"), in_=kvst_s[0])
                dd(out=Sini2[:].rearrange("p h x -> p (h x)"), in_=kvst_s[1])
            @p.step("vector", reads=[r_Sini, R['const']], writes=[r_Sini])
            def _(e):
                s2d = Sini[:].rearrange("p h x -> p (h x)")
                s2b = Sini2[:].rearrange("p h x -> p (h x)")
                p.chain(e)(e.tensor_tensor(s2d, s2d, s2b, ALU.add))
                return e.tensor_scalar_mul(s2d, s2d, sidec[0:64, 0:1])
            p.fence()
            es1.close()

            es2 = ExitStack()
            sb2 = lambda *a: es2.enter_context(nc.sbuf_tensor(*a))
            qTs = sb2("qTs1", [64, 2, HTOK], BF16)
            kTs = sb2("kTs1", [64, 2, HTOK], BF16)
            ktms = sb2("ktms", [128, 2, NT32, 64], BF16)
            vtms = sb2("vtms", [128, 2, NT32, 64], BF16)
            Sf = sb2("Sf", [64, 64], F32)
            Sb = sb2("Sb", [64, 64], BF16)
            asb = sb2("asb", [128, 2, 128], BF16)
            ost = sb2("ost", [64, 2, 128], BF16)
            aps_ = es2.enter_context(nc.psum_tensor("apsS", [128, 2, 128], F32))
            ops2 = es2.enter_context(nc.psum_tensor("opsS", [64, 2, 128], F32))
            tps2 = es2.enter_context(nc.psum_tensor("tpsS", [64, 2, 64], F32))
            r_hin = [Res(), Res()]
            r_S = Res()
            r_asb = [Res(), Res()]
            r_ost = [Res(), Res()]
            r_aps = [Res(), Res()]
            r_ops = [Res(), Res()]
            r_tps = [Res(), Res()]
            r_oT_h = [Res() for _ in range(16)]
            for h in range(16):
                bj = h % 2
                @p.step("sync", reads=[r_Sini] + r_qkv_h, writes=[r_hin[bj]], dma=True)
                def _(e, dd, h=h, bj=bj):
                    dd(out=qTs[:, bj, :], in_=qT1_h[h // 2, (h % 2) * 64:(h % 2) * 64 + 64, :])
                    dd(out=kTs[:, bj, :], in_=kT1_h[h // 2, (h % 2) * 64:(h % 2) * 64 + 64, :])
                    dd(out=ktms[:, bj], in_=ktm_h[:, h * 64:(h + 1) * 64].rearrange("(c p) x -> p c x", p=128))
                    dd(out=vtms[:, bj], in_=vtm_h[:, h * 64:(h + 1) * 64].rearrange("(c p) x -> p c x", p=128))
                @p.step("vector", reads=[r_Sini], writes=[r_S])
                def _(e, h=h):
                    e.tensor_copy(Sf[:], Sini[:, h, :])
                    return e.tensor_copy(Sb[:], Sini[:, h, :])
                for cch in range(NT32):
                    ci = cch % 2
                    @p.step("tensor", reads=[r_hin[bj]], writes=[r_aps[ci]])
                    def _(e, bj=bj, cch=cch, ci=ci):
                        return e.matmul(aps_[:, ci, :], kTs[:, bj, cch * 128:(cch + 1) * 128],
                                        qTs[:, bj, cch * 128:(cch + 1) * 128], start=True, stop=True)
                    @p.step("vector", reads=[r_aps[ci], R['const']], writes=[r_asb[ci]])
                    def _(e, ci=ci):
                        return e.tensor_tensor(asb[:, ci, :], aps_[:, ci, :], tri[:], ALU.mult)
                    @p.step("tensor", reads=[r_asb[ci], r_hin[bj], r_S], writes=[r_ops[ci]])
                    def _(e, bj=bj, cch=cch, ci=ci):
                        e.matmul(ops2[:, ci, :], vtms[:, bj, cch, :], asb[:, ci, :],
                                 start=True, stop=False)
                        return e.matmul(ops2[:, ci, :], Sb[:, :], qTs[:, bj, cch * 128:(cch + 1) * 128],
                                        start=False, stop=True)
                    @p.step("tensor", reads=[r_hin[bj]], writes=[r_tps[ci]])
                    def _(e, bj=bj, cch=cch, ci=ci):
                        return e.matmul(tps2[:, ci, :], ktms[:, bj, cch, :], vtms[:, bj, cch, :],
                                        start=True, stop=True)
                    @p.step("vector", reads=[r_ops[ci]], writes=[r_ost[ci]])
                    def _(e, ci=ci):
                        return e.tensor_copy(ost[:, ci, :], ops2[:, ci, :])
                    @p.step("gpsimd", reads=[r_ost[ci]], writes=[r_oT_h[h]], dma=True)
                    def _(e, dd, h=h, cch=cch, ci=ci):
                        dd(out=oT_h[h // 2, (h % 2) * 64:(h % 2) * 64 + 64,
                                    cch * 128:(cch + 1) * 128], in_=ost[:, ci, :])
                    @p.step("vector", reads=[r_tps[ci], r_S], writes=[r_S])
                    def _(e, ci=ci):
                        ch = p.chain(e)
                        ch(e.tensor_tensor(Sf[:], Sf[:], tps2[:, ci, :], ALU.add))
                        return e.tensor_copy(Sb[:], Sf[:])
            p.fence()
            es2.close()

            es3 = ExitStack()
            wln2 = es3.enter_context(nc.sbuf_tensor("wlnP", [128, 8, 1024], BF16))
            wTo = es3.enter_context(nc.sbuf_tensor("wTP", [128, 8, 1024], BF16))
            oTs2 = es3.enter_context(nc.sbuf_tensor("oTin", [128, 2, 8, 128], BF16))
            r_wln2, r_wTo = Res(), Res()
            r_oTin = [Res(), Res()]
            R['x3'] = [Res() for _ in range(NT32)]

            @p.step("gpsimd", writes=[r_wln2], dma=True)
            def _(e, dd):
                dd(out=wln2[:], in_=wo1_d.rearrange("(c p) o -> p c o", p=128))
            for oc in range(8):
                transpose_blocks(r_wln2, (lambda ic, oc=oc: wln2[:, oc, ic * 128:(ic + 1) * 128]),
                                 8, r_wTo, (lambda ic, oc=oc: wTo[:, ic, oc * 128:(oc + 1) * 128]))
            for t in range(NT32):
                bi = t % 2
                @p.step("sync", reads=[R['x2'][t]] + r_oT_h, writes=[R['xt'][bi], r_oTin[bi]], dma=True)
                def _(e, dd, t=t, bi=bi):
                    dd(out=xt[:, bi, :1024], in_=x2_h[t * 128:(t + 1) * 128, :])
                    dd(out=oTs2[:, bi], in_=oT_h[:, :, t * 128:(t + 1) * 128].rearrange("c p q -> p c q"))
                for oi in range(2):
                    mi = oi % 2
                    @p.step("tensor", reads=[r_oTin[bi], r_wTo], writes=[R['mm'][mi]])
                    def _(e, t=t, oi=oi, mi=mi, bi=bi):
                        for d in range(8):
                            last = e.matmul(mm[mi][:], oTs2[:, bi, d, :],
                                            wTo[:, d, oi * 512:(oi + 1) * 512],
                                            start=(d == 0), stop=(d == 7))
                        return last
                    @p.step("vector", reads=[R['mm'][mi], R['xt'][bi]], writes=[R['xo'][bi]])
                    def _(e, oi=oi, mi=mi, bi=bi):
                        return e.tensor_tensor(xo[:, bi, oi * 512:(oi + 1) * 512], mm[mi][:],
                                               xt[:, bi, oi * 512:(oi + 1) * 512], ALU.add)
                @p.step("gpsimd", reads=[R['xo'][bi]], writes=[R['x3'][t]], dma=True)
                def _(e, dd, t=t, bi=bi):
                    dd(out=x3_h[t * 128:(t + 1) * 128, :], in_=xo[:, bi, :])
            p.fence()
            es3.close()

        emit_l1attn()


        if _DBG_STAGE == 3:
            dbg_x3 = dram("dbg_x3", [HTOK, D], F32, kind="ExternalOutput")
            for t in range(HTOK // 128):
                bi = t % 2
                @p.step("sync", reads=[R['x3'][t]], writes=[R['xt'][bi]], dma=True)
                def _(e, dd, t=t, bi=bi):
                    dd(out=xt[:, bi, :1024], in_=x3_h[t * 128:(t + 1) * 128, :])
                @p.step("gpsimd", reads=[R['xt'][bi]], dma=True)
                def _(e, dd, t=t, bi=bi):
                    dd(out=dbg_x3[t * 128:(t + 1) * 128, :], in_=xt[:, bi, :1024])
            p.drain()
            es.close()
            return nc

        kT2_h = dram("kT2_h", [8, 128, HTOK], BF16, kind="Internal")
        Vloc_h = dram("Vloc_h", [HTOK, 8, 132], BF16, kind="Internal")

        def emit_l1ffn_pre():
            NT32 = HTOK // 128
            es1 = ExitStack()
            sb1 = lambda *a: es1.enter_context(nc.sbuf_tensor(*a))
            nT = sb1("nT2", [128, 8, HTOK], BF16)
            sqP = sb1("sqP", [128, 1024], F32)
            selT = sb1("selT", [128, 8, 128], BF16)
            selw = sb1("selw_sb", [128, 1024], BF16)
            gates = sb1("gates", [128, NT32, 16, 8], F32)
            qst = sb1("qst", [128, 2, 512], BF16)
            knr = sb1("knr", [1, 16], F32)
            lgp = es1.enter_context(nc.psum_tensor("lgp", [128, 128], F32))
            nrp2 = es1.enter_context(nc.psum_tensor("nrp2", [1, 512], F32))
            esP = ExitStack()
            probs = esP.enter_context(nc.sbuf_tensor("probs", [128, NT32, 128], F32))
            stP = dict(st, sq=sqP)
            r_nT, r_selT, r_selw = Res(), Res(), Res()
            r_probs, r_gates, r_knr = Res(), Res(), Res()
            r_qst = [Res(), Res()]
            r_lgp, r_nrp = Res(), Res()
            R['kT2'] = Res()
            R['qT2'] = Res()
            R['Vloc'] = Res()
            R['goh'] = Res()
            R['gates'] = r_gates

            @p.step("gpsimd", writes=[r_selw], dma=True)
            def _(e, dd):
                dd(out=selw[:], in_=selw_d[:, :])
            transpose_blocks(r_selw, (lambda ic: selw[:, ic * 128:(ic + 1) * 128]), 8,
                             r_selT, (lambda ic: selT[:, ic, :]))

            for t in range(NT32):
                bi = t % 2
                @p.step("sync", reads=[R['x3'][t]], writes=[R['xt'][bi]], dma=True)
                def _(e, dd, t=t, bi=bi):
                    dd(out=xt[:, bi, :1024], in_=x3_h[t * 128:(t + 1) * 128, :])
                ln_tile(R['xt'][bi], xt[:, bi, :1024], R['nbf'][bi], nbf[:, bi, :], stP)
                transpose_blocks(R['nbf'][bi], (lambda d, bi=bi: nbf[:, bi, d * 128:(d + 1) * 128]),
                                 8, r_nT, (lambda d, t=t: nT[:, d, t * 128:(t + 1) * 128]))
                @p.step("tensor", reads=[r_nT, r_selT], writes=[r_lgp])
                def _(e, t=t):
                    for d in range(8):
                        last = e.matmul(lgp[:], nT[:, d, t * 128:(t + 1) * 128],
                                        selT[:, d, :], start=(d == 0), stop=(d == 7))
                    return last
                @p.step("scalar", reads=[r_lgp], writes=[r_probs])
                def _(e, t=t):
                    return e.activation(out=probs[:, t, :], in_=lgp[:], func=AF.Sigmoid)

            es3 = ExitStack()
            scaG = es3.enter_context(nc.sbuf_tensor("scaG", [128, 4608], F32))
            rank = es3.enter_context(nc.sbuf_tensor("rank", [128, NT32, 128], F32))
            r_scaG, r_rank = Res(), Res()
            NG = NT32 * 16
            @p.step("vector", reads=[r_probs, R['const']], writes=[r_rank, r_gates, r_scaG])
            def _(e):
                ch = p.chain(e)
                p3 = probs[:].rearrange("p t (g e) -> p (t g) e", e=8)
                r3 = rank[:].rearrange("p t (g e) -> p (t g) e", e=8)
                e.memset(rank[:], 0.0)
                t1 = scaG[:, 0:NG * 8].rearrange("p (g e) -> p g e", e=8)
                for dlt in range(1, 8):
                    wdt = 8 - dlt
                    aa = p3[:, :, 0:wdt]
                    bb = p3[:, :, dlt:8]
                    ch(e.tensor_tensor(t1[:, :, 0:wdt], bb, aa, ALU.is_gt))
                    ch(e.tensor_tensor(r3[:, :, 0:wdt], r3[:, :, 0:wdt], t1[:, :, 0:wdt], ALU.add))
                    ch(e.tensor_tensor(t1[:, :, 0:wdt], aa, bb, ALU.is_ge))
                    ch(e.tensor_tensor(r3[:, :, dlt:8], r3[:, :, dlt:8], t1[:, :, 0:wdt], ALU.add))
                ch(e.tensor_single_scalar(r3, r3, 2.0, ALU.is_lt))
                ch(e.tensor_tensor(t1, p3, r3, ALU.mult))
                s_ = scaG[:, NG * 8:NG * 9]
                ch(e.tensor_reduce(s_.rearrange("p (g x) -> p g x", x=1), t1, AX.X, ALU.add))
                ch(e.tensor_scalar_max(s_, s_, 1e-9))
                ch(e.reciprocal(s_, s_))
                ch(e.tensor_scalar_mul(s_, s_, rsc[:, 0:1]))
                sb_ = s_.rearrange("p (g x) -> p g x", x=1).broadcast_to([128, NG, 8])
                g4 = gates[:].rearrange("p t g e -> p (t g) e")
                return e.tensor_tensor(g4, t1, sb_, ALU.mult)
            @p.step("gpsimd", reads=[r_gates], writes=[R['goh']], dma=True)
            def _(e, dd):
                dd(out=go_h.rearrange("(t p) (h e) -> p t h e", p=128, e=8),
                   in_=gates[:, :, 8:16, :])
            p.fence()
            es3.close()
            esP.close()

            es2 = ExitStack()
            wqkT = es2.enter_context(nc.sbuf_tensor("wqkT", [128, 2, 8, 1024], BF16))
            wln3 = es2.enter_context(nc.sbuf_tensor("wln3", [128, 2, 1024], BF16))
            sqb2 = es2.enter_context(nc.sbuf_tensor("sqb2", [128, 512], BF16))
            onec2 = es2.enter_context(nc.sbuf_tensor("onec2", [128, 1], BF16))
            r_wqkT, r_sqb = Res(), Res()
            r_wln3 = [Res(), Res()]
            r_kTh = Res()
            r_qTh = Res()
            @p.step("vector", writes=[r_sqb])
            def _(e):
                return e.memset(onec2[:], 1.0)
            for w, wd2 in ((0, qw2_d), (1, kw2_d)):
                for oc in range(8):
                    bo = oc % 2
                    @p.step("gpsimd", writes=[r_wln3[bo]], dma=True)
                    def _(e, dd, wd2=wd2, oc=oc, bo=bo):
                        dd(out=wln3[:, bo, :], in_=wd2[oc * 128:(oc + 1) * 128, :])
                    transpose_blocks(r_wln3[bo], (lambda ic, bo=bo: wln3[:, bo, ic * 128:(ic + 1) * 128]),
                                     8, r_wqkT, (lambda ic, w=w, oc=oc: wqkT[:, w, ic, oc * 128:(oc + 1) * 128]))
            @p.step("vector", writes=[r_knr])
            def _(e):
                return e.memset(knr[:], 0.0)
            for w, dram2, rdr in ((0, qT2_h, R['qT2']), (1, kT2_h, r_kTh)):
                for hp in range(8):
                    for tq in range(8):
                        mi = (hp * 8 + tq) % 2
                        @p.step("tensor", reads=[r_nT, r_wqkT], writes=[R['mm'][mi]])
                        def _(e, w=w, hp=hp, tq=tq, mi=mi):
                            for d in range(8):
                                last = e.matmul(mm[mi][:], wqkT[:, w, d, hp * 128:(hp + 1) * 128],
                                                nT[:, d, tq * 512:(tq + 1) * 512],
                                                start=(d == 0), stop=(d == 7))
                            return last
                        @p.step("vector", reads=[R['mm'][mi]], writes=[r_qst[mi]])
                        def _(e, mi=mi):
                            return e.tensor_copy(qst[:, mi, :], mm[mi][:])
                        @p.step("gpsimd", reads=[r_qst[mi]], writes=[rdr], dma=True)
                        def _(e, dd, hp=hp, tq=tq, mi=mi, dram2=dram2):
                            dd(out=dram2[hp][:, tq * 512:(tq + 1) * 512], in_=qst[:, mi, :])
                        if w == 1:
                            @p.step("scalar", reads=[r_qst[mi]], writes=[r_sqb])
                            def _(e, mi=mi):
                                return e.activation(out=sqb2[:], in_=qst[:, mi, :], func=AF.Square)
                            @p.step("tensor", reads=[r_sqb], writes=[r_nrp])
                            def _(e):
                                return e.matmul(nrp2[:], onec2[:], sqb2[:], start=True, stop=True)
                            @p.step("vector", reads=[r_nrp, r_knr], writes=[r_knr, R['nsm']])
                            def _(e, hp=hp):
                                ch = p.chain(e)
                                ch(e.reduce_max(nsm[0:1, 13:14], nrp2[:], AX.X))
                                return e.tensor_tensor(knr[0:1, hp:hp + 1], knr[0:1, hp:hp + 1],
                                                       nsm[0:1, 13:14], ALU.max)
            es2.close()

            es4 = ExitStack()
            wvh = es4.enter_context(nc.sbuf_tensor("wvh", [128, 2, 8, 1024], BF16))
            Vh = es4.enter_context(nc.sbuf_tensor("Vh", [128, NT32, 132], BF16))
            r_wvh = [Res(), Res()]
            r_Vh = Res()
            for h in range(8):
                bh = h % 2
                @p.step("gpsimd", writes=[r_wvh[bh]], dma=True)
                def _(e, dd, h=h, bh=bh):
                    for l in range(8):
                        dd(out=wvh[:, bh, :, l * 128:(l + 1) * 128],
                           in_=vw_d[8 * h + l].rearrange("(c p) x -> p c x", p=128))
                @p.step("vector", writes=[r_Vh])
                def _(e):
                    return e.memset(Vh[:], 1.0)
                for t in range(NT32):
                    for sl in range(2):
                        mi = sl % 2
                        @p.step("tensor", reads=[r_nT, r_wvh[bh]], writes=[R['mm'][mi]])
                        def _(e, h=h, t=t, sl=sl, mi=mi, bh=bh):
                            for d in range(8):
                                last = e.matmul(mm[mi][:], nT[:, d, t * 128:(t + 1) * 128],
                                                wvh[:, bh, d, sl * 512:(sl + 1) * 512],
                                                start=(d == 0), stop=(d == 7))
                            return last
                        @p.step("vector", reads=[R['mm'][mi], r_gates], writes=[r_Vh])
                        def _(e, h=h, t=t, sl=sl, mi=mi):
                            last = None
                            for l in range(4):
                                ei = sl * 4 + l
                                gcol = gates[:, t, h, ei:ei + 1]
                                dst = Vh[:, t, 0:128]
                                if ei == 0:
                                    last = e.tensor_scalar_mul(dst, mm[mi][:, l * 128:(l + 1) * 128], gcol)
                                else:
                                    last = e.scalar_tensor_tensor(dst, mm[mi][:, l * 128:(l + 1) * 128],
                                                                  gcol, dst, ALU.mult, ALU.add)
                            return last
                @p.step("gpsimd", reads=[r_Vh], writes=[R['Vloc']], dma=True)
                def _(e, dd, h=h):
                    dd(out=Vloc_h[:, h, :].rearrange("(t p) x -> p t x", p=128), in_=Vh[:])
            p.fence()
            es4.close()

            @p.step("gpsimd", reads=[r_kTh, R['Vloc'], r_knr], writes=[R['const']], dma=True, fixed_inc=48)
            def _(e, dd):
                for arm in nc.switch_core_id(engines=[e], n=128):
                    slot = int(arm) % 2
                    dd(out=K_s[slot], in_=kT2_h[:, :, :])
                    dd(out=V_s[slot], in_=Vloc_h[:, :, :])
                    dd(out=kn_s[slot], in_=knr[0:1, :])
            @p.step("gpsimd", reads=[R['const']], writes=[R['const']])
            def _(e):
                emit_flag_store(e, 1)
                emit_flag_poll(e, 1, both=True)
                return e.nop()
            @p.step("gpsimd", reads=[R['goh'], R['qT2']] + R['x3'], writes=[R['const']], dma=True)
            def _(e, dd):
                o_x3 = e.value_load(qoffs[0:1, 0:1])
                o_go = e.value_load(qoffs[0:1, 1:2])
                o_qT = e.value_load(qoffs[0:1, 2:3])
                dd(out=x3q_h[:, :], in_=dataclasses.replace(x3_h[0:QTOK, :], offset=o_x3))
                dd(out=goq_h[:, :], in_=dataclasses.replace(go_h[0:QTOK, :], offset=o_go))
                dd(out=qTq_h[:, :, :], in_=dataclasses.replace(qT2_h[:, :, 0:QTOK], offset=o_qT))
            p.fence()
            es1.close()

        emit_l1ffn_pre()

        SCALE2 = float(1.0 / np.sqrt(128.0))
        esC = ExitStack()
        ctxF = esC.enter_context(nc.sbuf_tensor("ctxF", [128, 8, QTOK], BF16))
        rsF = esC.enter_context(nc.sbuf_tensor("rsF", [8, QTOK], F32))
        esF = ExitStack()
        sbF = lambda *a: esF.enter_context(nc.sbuf_tensor(*a))
        qsF = sbF("qsF", [128, 8, QTOK], BF16)
        ksF = sbF("ksF", [128, T], BF16)
        vsF = sbF("vsF", [128, T // 128, 132], BF16)
        pbt = sbF("pbt", [128, 2, 512], BF16)
        cstF = sbF("cstF", [128, 512], BF16)
        rs1 = sbF("rs1", [1, 512], F32)
        sqbF = sbF("sqbF", [128, 512], BF16)
        nrmF = sbF("nrmF", [1, QTOK], F32)
        smaxF = sbF("smaxF", [1, 16], F32)
        bcolF = sbF("bcolF", [128, 1], F32)
        onecF = sbF("onecF", [128, 1], BF16)
        onerF = sbF("onerF", [1, 128], BF16)
        knin = sbF("knin", [1, 2, 16], F32)
        scpF = esF.enter_context(nc.psum_tensor("scpF", [128, 2, 512], F32))
        ctxpF = esF.enter_context(nc.psum_tensor("ctxpF", [128, 512], F32))
        rspF = esF.enter_context(nc.psum_tensor("rspF", [1, 512], F32))

        r_qsF, r_ctxF, r_rsF, r_knin = Res(), Res(), Res(), Res()
        r_ksF = Res()
        r_pbtF = [Res(), Res()]
        r_cstF, r_rs1 = Res(), Res()
        r_scpF = [Res(), Res()]
        r_ctxpF, r_rspF = Res(), Res()
        nrpF, mbpF = mmA, mmB
        r_nrpF, r_mbpF = R['mm'][0], R['mm'][1]
        r_sqbF, r_nrmF, r_smaxF, r_bcolF = Res(), Res(), Res(), Res()

        @p.step("vector", writes=[r_sqbF])
        def _(e):
            e.memset(onecF[:], 1.0)
            return e.memset(onerF[:], 1.0)

        @p.step("sync", reads=[R['const']], writes=[r_qsF, r_knin], dma=True)
        def _(e, dd):
            dd(out=qsF[:], in_=qTq_h.rearrange("c p q -> p c q"))
            dd(out=knin[:], in_=kn_s[:, :, :].rearrange("s o x -> o s x"))

        for h in range(8):
            for sl in range(4):
                @p.step("scalar", reads=[r_qsF], writes=[r_sqbF])
                def _(e, h=h, sl=sl):
                    return e.activation(out=sqbF[:], in_=qsF[:, h, sl * 512:(sl + 1) * 512],
                                        func=AF.Square)
                @p.step("tensor", reads=[r_sqbF], writes=[r_nrpF])
                def _(e):
                    return e.matmul(nrpF[0:1, :], onecF[:], sqbF[:], start=True, stop=True)
                @p.step("vector", reads=[r_nrpF], writes=[r_nrmF])
                def _(e, sl=sl):
                    return e.tensor_copy(nrmF[0:1, sl * 512:(sl + 1) * 512], nrpF[0:1, :])
            @p.step("vector", reads=[r_nrmF, r_knin], writes=[r_smaxF, R['nsm']])
            def _(e, h=h):
                ch = p.chain(e)
                ch(e.reduce_max(smaxF[0:1, h:h + 1], nrmF[0:1, :], AX.X))
                ch(e.tensor_tensor(nsm[0:1, 13:14], knin[0:1, 0, h:h + 1],
                                   knin[0:1, 1, h:h + 1], ALU.max))
                ch(e.tensor_tensor(smaxF[0:1, h:h + 1], smaxF[0:1, h:h + 1],
                                   nsm[0:1, 13:14], ALU.mult))
                return e.tensor_scalar_mul(smaxF[0:1, h:h + 1], smaxF[0:1, h:h + 1],
                                           SCALE2 * SCALE2)
        @p.step("scalar", reads=[r_smaxF], writes=[r_smaxF])
        def _(e):
            return e.activation(out=smaxF[0:1, 8:16], in_=smaxF[0:1, 0:8], func=AF.Sqrt)

        for h in range(8):
            @p.step("sync", reads=[R['const']], writes=[r_ksF], dma=True)
            def _(e, dd, h=h):
                for slot in range(2):
                    dd(out=ksF[:, slot * HTOK:(slot + 1) * HTOK], in_=K_s[slot, h])
                    dd(out=vsF[:, slot * 32:(slot + 1) * 32, :],
                       in_=V_s[slot, :, h, :].rearrange("(c p) x -> p c x", p=128))
            @p.step("vector", reads=[r_smaxF], writes=[r_sqbF, R['nsm']])
            def _(e, h=h):
                p.chain(e)(e.tensor_scalar_mul(nsm[0:1, 13:14], smaxF[0:1, 8 + h:9 + h], -1.0))
                return e.tensor_copy(sqbF[0:1, 0:1], nsm[0:1, 13:14])
            @p.step("tensor", reads=[r_sqbF], writes=[r_mbpF])
            def _(e):
                return e.matmul(mbpF[:, 0:1], onerF[:, :], sqbF[0:1, 0:1], start=True, stop=True)
            @p.step("vector", reads=[r_mbpF], writes=[r_bcolF])
            def _(e):
                return e.tensor_copy(bcolF[:], mbpF[:, 0:1])
            for qt in range(4):
                for kp in range(32):
                    kb0 = 2 * kp
                    @p.step("tensor", reads=[r_ksF, r_qsF], writes=[r_scpF[0], r_scpF[1]])
                    def _(e, h=h, qt=qt, kb0=kb0):
                        for j in range(2):
                            last = e.matmul(scpF[:, j, :], ksF[:, (kb0 + j) * 128:(kb0 + j + 1) * 128],
                                            qsF[:, h, qt * 512:(qt + 1) * 512], start=True, stop=True)
                        return last
                    @p.step("scalar", reads=[r_scpF[0], r_scpF[1], r_bcolF],
                            writes=[r_pbtF[0], r_pbtF[1]])
                    def _(e):
                        return e.activation(out=pbt[:].rearrange("p a b -> p (a b)"),
                                            in_=scpF[:].rearrange("p a b -> p (a b)"),
                                            func=AF.Exp, bias=bcolF[:, 0:1], scale=SCALE2)
                    @p.step("tensor", reads=[r_pbtF[0], r_pbtF[1], r_ksF],
                            writes=[r_ctxpF, r_rspF])
                    def _(e, kb0=kb0):
                        for j in range(2):
                            kb = kb0 + j
                            e.matmul(ctxpF[:], vsF[:, kb, 0:128], pbt[:, j, :],
                                     start=(kb == 0), stop=(kb == 63))
                            last = e.matmul(rspF[:], vsF[:, kb, 128:129], pbt[:, j, :],
                                            start=(kb == 0), stop=(kb == 63))
                        return last
                @p.step("vector", reads=[r_ctxpF], writes=[r_ctxF])
                def _(e, h=h, qt=qt):
                    return e.tensor_copy(ctxF[:, h, qt * 512:(qt + 1) * 512], ctxpF[:])
                @p.step("vector", reads=[r_rspF], writes=[r_rs1])
                def _(e):
                    return e.tensor_copy(rs1[:], rspF[:])
                @p.step("gpsimd", reads=[r_rs1], writes=[r_rsF], dma=True)
                def _(e, dd, h=h, qt=qt):
                    dd(out=rsF[h:h + 1, qt * 512:(qt + 1) * 512], in_=rs1[0:1, :])

        p.fence()
        esF.close()
        if _DBG_STAGE == 4:
            dbg_ctx = dram("dbg_ctx", [128, 8 * QTOK], BF16, kind="ExternalOutput")
            dbg_rs = dram("dbg_rs", [8, QTOK], F32, kind="ExternalOutput")
            @p.step("gpsimd", reads=[r_ctxF, r_rsF], dma=True)
            def _(e, dd):
                dd(out=dbg_ctx.rearrange("p (h q) -> p h q", h=8), in_=ctxF[:])
                dd(out=dbg_rs[:, :], in_=rsF[:])
        esD = ExitStack()
        sbD = lambda *a: esD.enter_context(nc.sbuf_tensor(*a))
        repl = sbD("repl_sb", [64, 64 * 128], BF16)
        idf_sb = sbD("idf_sb", [128, 128], F32)
        go4 = sbD("go4", [128, 16, 8, 8], F32)
        rcp = sbD("rcp", [128, 16, 8], F32)
        gob = sbD("gob", [128, 16, 64], BF16)
        goT = sbD("goT", [64, QTOK], BF16)
        cT = sbD("cT", [128, 64, 512], BF16)
        gbs = sbD("gbs", [128, 512], BF16)
        owt = sbD("owt", [128, 2, 512], BF16)
        oTsb = sbD("oTsb", [128, 8, 512], BF16)
        odps = esD.enter_context(nc.psum_tensor("odps", [128, 4, 512], F32))
        gbp, rtp = mmA, mmB
        r_repl, r_go4, r_rcp, r_gob, r_goT = Res(), Res(), Res(), Res(), Res()
        r_cT, r_gbs, r_oTsb = Res(), Res(), Res()
        r_owt = [Res(), Res()]
        r_odps = Res()
        r_gbp, r_rtp = R['mm'][0], R['mm'][1]

        @p.step("sync", reads=[R['const']], writes=[r_repl, r_go4], dma=True)
        def _(e, dd):
            dd(out=repl[:], in_=repl_d[:, :])
            dd(out=idf_sb[:], in_=idf_d[:, :])
            dd(out=go4[:], in_=goq_h.rearrange("(t p) (h e) -> p t h e", p=128, e=8))
        for t16 in range(16):
            ti = t16 % 2
            @p.step("tensor", reads=[r_rsF, R['const']], writes=[r_rtp])
            def _(e, t16=t16):
                return e.transpose(rtp[:, 0:8], rsF[:, t16 * 128:(t16 + 1) * 128], idf_sb[0:8, 0:8])
            @p.step("vector", reads=[r_rtp], writes=[r_rcp])
            def _(e, t16=t16):
                return e.tensor_copy(rcp[:, t16, :], rtp[:, 0:8])
        @p.step("vector", reads=[r_rcp, r_go4], writes=[r_rcp, r_gob])
        def _(e):
            ch = p.chain(e)
            ch(e.reciprocal(rcp[:], rcp[:]))
            rb = rcp[:].unsqueeze(3).broadcast_to([128, 16, 8, 8])
            return e.tensor_tensor(gob[:].rearrange("p t (h e) -> p t h e", e=8),
                                   go4[:], rb, ALU.mult)
        for t16 in range(0, 16, 8):
            @p.step("tensor", reads=[r_gob, R['const']], writes=[R['tpW']])
            def _(e, t16=t16):
                for j in range(8):
                    last = e.transpose(tpW[0:64, j * 128:(j + 1) * 128], gob[:, t16 + j, :], idb[:])
                return last
            @p.step("vector", reads=[R['tpW']], writes=[r_goT])
            def _(e, t16=t16):
                for j in range(8):
                    last = e.tensor_copy(goT[:, (t16 + j) * 128:(t16 + j + 1) * 128],
                                         tpW[0:64, j * 128:(j + 1) * 128])
                return last
        if _DBG_STAGE == 5:
            dbg_cT = dram("dbg_cT", [128, 64 * 512], BF16, kind="ExternalOutput")
            dbg_oT = dram("dbg_oT", [128, 8 * 512], BF16, kind="ExternalOutput")
            dbg_goT = dram("dbg_goT", [64, QTOK], BF16, kind="ExternalOutput")
        for tsl in range(4):
            for he in range(64):
                h = he // 8
                @p.step("tensor", reads=[r_goT, r_repl], writes=[r_gbp])
                def _(e, he=he, tsl=tsl):
                    return e.matmul(gbp[:, :], repl[:, he * 128:(he + 1) * 128],
                                    goT[:, tsl * 512:(tsl + 1) * 512], start=True, stop=True)
                @p.step("vector", reads=[r_gbp], writes=[r_gbs])
                def _(e):
                    return e.tensor_copy(gbs[:], gbp[:, :])
                @p.step("vector", reads=[r_gbs, r_ctxF], writes=[r_cT])
                def _(e, he=he, h=h, tsl=tsl):
                    return e.tensor_tensor(cT[:, he, :], ctxF[:, h, tsl * 512:(tsl + 1) * 512],
                                           gbs[:], ALU.mult)
            for dh in range(2):
                for he in range(64):
                    bo = he % 2
                    @p.step("gpsimd", writes=[r_owt[bo]], dma=True)
                    def _(e, dd, he=he, dh=dh, bo=bo):
                        dd(out=owt[:, bo], in_=ow_d[he][:, dh * 512:(dh + 1) * 512])
                    @p.step("tensor", reads=[r_owt[bo], r_cT], writes=[r_odps])
                    def _(e, he=he, dh=dh, bo=bo):
                        for ds in range(4):
                            last = e.matmul(odps[:, ds, :], owt[:, bo, ds * 128:(ds + 1) * 128],
                                            cT[:, he, :], start=(he == 0), stop=(he == 63))
                        return last
                @p.step("vector", reads=[r_odps], writes=[r_oTsb])
                def _(e, dh=dh):
                    return e.tensor_copy(oTsb[:, 4 * dh:4 * dh + 4, :], odps[:])
            if _DBG_STAGE == 5 and tsl == 2:
                @p.step("gpsimd", reads=[r_cT, r_oTsb, r_goT], dma=True)
                def _(e, dd):
                    dd(out=dbg_cT.rearrange("p (a b) -> p a b", b=512), in_=cT[:])
                    dd(out=dbg_oT.rearrange("p (a b) -> p a b", b=512), in_=oTsb[:])
                    dd(out=dbg_goT[:, :], in_=goT[:])
            for tt in range(4):
                t = tsl * 4 + tt
                bi = t % 2
                @p.step("sync", reads=[R['const']], writes=[R['xt'][bi]], dma=True)
                def _(e, dd, t=t, bi=bi):
                    dd(out=xt[:, bi, :1024], in_=x3q_h[t * 128:(t + 1) * 128, :])
                @p.step("tensor", reads=[r_oTsb, R['const']], writes=[R['tpW']])
                def _(e, tt=tt):
                    for d in range(8):
                        last = e.transpose(tpW[:, d * 128:(d + 1) * 128],
                                           oTsb[:, d, tt * 128:(tt + 1) * 128], idb[:])
                    return last
                @p.step("vector", reads=[R['tpW'], R['xt'][bi]], writes=[R['xo'][bi]])
                def _(e, bi=bi):
                    return e.tensor_tensor(xo[:, bi, :], tpW[:].bitcast(BF16),
                                           xt[:, bi, :1024], ALU.add)
                @p.step("gpsimd", reads=[R['xo'][bi]], dma=True)
                def _(e, dd, t=t, bi=bi):
                    dd(out=xout[t * 128:(t + 1) * 128, :], in_=xo[:, bi, :])
            p.fence()
        p.drain()
        esD.close()
        esC.close()
    es.close()
    return nc



_BUILD_CACHE = {}


def _get(name, builder):
    if name not in _BUILD_CACHE:
        _BUILD_CACHE[name] = builder()
    return _BUILD_CACHE[name]


_EYE_BF = np.eye(128).astype(BF)
_EYE_F32 = np.eye(128).astype(np.float32)


def run_l0ffn(x_full, wg, wu, wd):
    nc = _get("l0ffn", build_l0ffn)
    xs = x_full.reshape(8, TC, D)
    in_maps = [{"xin": np.ascontiguousarray(xs[c]), "wg": wg, "wu": wu, "wd": wd,
                "idb": _EYE_BF} for c in range(NCORES)]
    res = run_bass_kernel_spmd(nc, in_maps, core_ids=list(range(NCORES)))
    out = np.stack([r["xout"] for r in res.results]).reshape(2, 8192, 1024)
    return out


def _fused_in_maps(inputs, runid):
    x = np.ascontiguousarray(inputs['L0_attn_x'], dtype=np.float32)
    gbrep = np.ascontiguousarray(np.concatenate(
        [np.broadcast_to(np.asarray(v, np.float32).reshape(1, 64), (128, 64)) for v in
         (inputs['L1_attn_q_gamma'], inputs['L1_attn_q_beta'],
          inputs['L1_attn_k_gamma'], inputs['L1_attn_k_beta'])], axis=1))
    selw = np.ascontiguousarray(np.concatenate(
        [inputs['L1_ffn_sel_v_w'], inputs['L1_ffn_sel_o_w']], 0), dtype=np.float32)
    repl = np.zeros((64, 64 * 128), np.float32)
    for he in range(64):
        repl[he, he * 128:(he + 1) * 128] = 1.0
    repl = repl.astype(BF)
    tri = np.triu(np.ones((128, 128), np.float32)).astype(BF)
    inv = (1.0 / 10000.0 ** (np.arange(0, 64, 2, dtype=np.float32) / 64)).astype(np.float32)
    common = {
        "wqkv": inputs['L0_attn_Wqkv'], "wout": inputs['L0_attn_Wout'],
        "bout": np.asarray(inputs['L0_attn_bout'], np.float32).reshape(1, D),
        "wg": inputs['L0_ffn_w_gate'], "wu": inputs['L0_ffn_w_up'], "wd": inputs['L0_ffn_w_down'],
        "wq1": inputs['L1_attn_Wq'], "wk1": inputs['L1_attn_Wk'], "wv1": inputs['L1_attn_Wv'],
        "wo1": inputs['L1_attn_Wo'], "gb": gbrep,
        "qw2": inputs['L1_ffn_q_w'], "kw2": inputs['L1_ffn_k_w'],
        "selw": selw, "vw": inputs['L1_ffn_v_w'], "ow": inputs['L1_ffn_o_w'],
        "rsc": np.ascontiguousarray(np.broadcast_to(
            np.asarray(inputs['L1_ffn_route_scale'], np.float32).reshape(1, 1), (128, 1))),
        "idb": _EYE_BF, "idf": _EYE_F32, "trid": tri, "repl": repl,
        "runid": np.full((1, 16), runid, np.int32),
    }
    ims = []
    for c in range(NCORES):
        b, side, par = c // 4, c % 2, (c % 4) // 2
        t0 = side * HTOK
        xhl = np.zeros((HTOK + 256, D), np.float32)
        lo, hi = max(0, t0 - 128), min(T, t0 + HTOK + 128)
        xhl[lo - (t0 - 128):hi - (t0 - 128)] = x[b, lo:hi]
        pos = np.arange(t0 - 128, t0 + HTOK + 128)
        pv = np.where((pos >= 0) & (pos < T), pos, 0).astype(np.float32)
        fr = np.outer(pv, inv).astype(np.float32)
        msk = np.zeros((4, 8 * 128, 384), np.float32)
        for sub in range(4):
            qb = np.arange(8)[:, None, None]
            qi = np.arange(128)[None, :, None]
            ki = np.arange(384)[None, None, :]
            gq = t0 + sub * 1024 + qb * 128 + qi
            gk = t0 + sub * 1024 - 128 + qb * 128 + ki
            keep = (np.abs(gk - gq) <= 128) & (gk >= 0) & (gk < T)
            msk[sub] = keep.reshape(8 * 128, 384)
        ims.append(dict(common, xh=xhl,
                        cosd=np.cos(fr).astype(np.float32),
                        sind=np.sin(fr).astype(np.float32),
                        mskd=msk.astype(BF),
                        sidec=np.full((128, 1), float(side), np.float32),
                        qoff=np.array([[par * QTOK * D, par * QTOK * 64, par * QTOK]
                                       + [0] * 13], np.int32)))
    return ims


def kernel(**inputs):
    nc = _get("fused", build_fused)
    runid = int(np.random.randint(1 << 20, 1 << 29))
    ims = _fused_in_maps(inputs, runid)
    res = run_bass_kernel_spmd(nc, ims, core_ids=list(range(NCORES))).results
    out = np.zeros((2, T, D), np.float32)
    for c in range(NCORES):
        b, side, par = c // 4, c % 2, (c % 4) // 2
        q0 = side * HTOK + par * QTOK
        out[b, q0:q0 + QTOK] = res[c]["xout"]
    return out
